# version 25
# baseline (speedup 1.0000x reference)
import math
from contextlib import ExitStack

import numpy as np
import ml_dtypes
import concourse.bass as bass
import concourse.mybir as mybir
from concourse.bass_utils import run_bass_kernel_spmd

F32 = mybir.dt.float32
BF16 = mybir.dt.bfloat16
AF = mybir.ActivationFunctionType
ALU = mybir.AluOpType
AX = mybir.AxisListType
BF = ml_dtypes.bfloat16

B, S, D, NH, DH, II = 4, 2048, 2048, 16, 128, 8192
ND = D // 128
NT = S // 128
NTO = 8
NI = II // 128
SCALE = 1.0 / math.sqrt(DH)
EPS = 1e-5
INV_D = 1.0 / D

STAGE = 5
DEBUG = False
P2_SUB = "all"

ENGS = ("sync", "tensor", "vector", "scalar", "gpsimd")


class Prog:

    def __init__(self, nc, ctx):
        self.nc = nc
        self.ctx = ctx
        self.ops = {e: [] for e in ENGS}
        self.seq = {e: 0 for e in ENGS}
        self.dcnt = {}
        self.marks = {e: set() for e in ENGS}
        self.sems = {}

    def _add(self, eng, fn, waits, dsem):
        ws = []
        for name, val in waits:
            if not val or val <= 0:
                continue
            ws.append((name, val))
            if name in ENGS:
                self.marks[name].add(val)
        self.seq[eng] += 1
        self.ops[eng].append((fn, tuple(ws), self.seq[eng], dsem))

    def op(self, eng, fn, waits=()):
        self._add(eng, fn, waits, None)
        return self.seq[eng]

    def dma(self, eng, fn, sem, waits=()):
        self.dcnt[sem] = self.dcnt.get(sem, 0) + 16
        self._add(eng, fn, waits, sem)
        return self.dcnt[sem]

    def v(self, eng):
        return self.seq[eng]

    def barrier(self):
        self.dcnt["bar"] = self.dcnt.get("bar", 0) + len(ENGS)
        target = self.dcnt["bar"]
        for e in ENGS:
            self.seq[e] += 1
            self.ops[e].append(("BARRIER", target, self.seq[e], None))

    def emit(self):
        nc = self.nc
        rank = {}
        for e in ENGS:
            rank[e] = {s: i + 1 for i, s in enumerate(sorted(self.marks[e]))}
        for e in ENGS:
            self.sems["q" + e] = self.ctx.enter_context(nc.semaphore("q" + e))
        for name in self.dcnt:
            self.sems[name] = self.ctx.enter_context(nc.semaphore(name))

        def mk(e):
            ops = self.ops[e]
            sems = self.sems
            marks = self.marks[e]

            def body(eng):
                for fn, waits, eseq, dsem in ops:
                    if fn == "BARRIER":
                        if e == "sync":
                            eng.sem_inc(sems["bar"], 1)
                        else:
                            eng.drain().then_inc(sems["bar"], 1)
                        eng.wait_ge(sems["bar"], waits)
                        continue
                    for wname, wval in waits:
                        if wname in ENGS:
                            eng.wait_ge(sems["q" + wname], rank[wname][wval])
                        else:
                            eng.wait_ge(sems[wname], wval)
                    inst = fn(eng)
                    if dsem is not None:
                        inst.then_inc(sems[dsem], 16)
                    if eseq in marks:
                        inst.then_inc(sems["q" + e], 1)
            return body

        with nc.Block() as blk:
            for e in ENGS:
                getattr(blk, e)(mk(e))



def build_program(stage=STAGE, debug=DEBUG):
    nc = bass.Bass("TRN2", target_bir_lowering=False)

    dram = {}

    def din(name, shape, dtype):
        dram[name] = nc.dram_tensor(name, shape, dtype, kind="ExternalInput").ap()
        return dram[name]

    def dout(name, shape, dtype):
        dram[name] = nc.dram_tensor(name, shape, dtype, kind="ExternalOutput").ap()
        return dram[name]

    xb = din("xb", [S, D], F32)
    din("wip", [24, 128, 4096], F32)
    din("wop", [4, 128, 8192], F32)
    din("gbp", [32, 128, 4096], BF16)
    din("ubp", [32, 128, 4096], BF16)
    din("dwp", [16, 128, 8192], BF16)
    din("cs", [128, S], F32)
    din("lnwc", [128, ND], F32)
    din("lnbc", [128, ND], F32)
    din("gscp", [128, NI], F32)
    din("uscp", [128, NI], F32)
    din("dscp", [4, 512], F32)
    din("pmat", [64, 64], BF16)
    din("idf", [128, 128], F32)

    dout("out", [NTO * 128, D], F32)

    if debug:
        dout("d_xn2", [128, ND * S], BF16)
        dout("d_xnfm", [128, ND * 1024], BF16)
        dout("d_rstd", [128, NT], F32)
        dout("d_nmr", [128, NT], F32)
        if stage >= 2:
            dout("d_q", [128, 2 * 1024], BF16)
            dout("d_k", [128, 2 * S], BF16)
            dout("d_v", [128, NT * 256], BF16)
            dout("d_bias", [128, 2], F32)
            dout("d_ctx", [128, NH * 1024], BF16)
        if stage >= 3:
            dout("d_acc", [128, NTO * D], F32)
        if stage >= 4:
            dout("d_acc2", [128, NTO * D], F32)

    with ExitStack() as top:
        prog = Prog(nc, top)
        L = lambda *a: top.enter_context(nc.sbuf_tensor(*a, side="left"))

        env = {"nc": nc, "prog": prog, "dram": dram, "debug": debug, "top": top,
               "stage": stage}

        env["lnw"] = L("lnw_sb", [128, ND], F32)
        env["lnb"] = L("lnb_sb", [128, ND], F32)
        env["gsc"] = L("gsc_sb", [128, NI], F32)
        env["usc"] = L("usc_sb", [128, NI], F32)
        env["pm"] = L("pm_sb", [64, 64], BF16)
        env["idf"] = L("idf_sb", [128, 128], F32)
        env["ones_cb"] = L("ones_cb", [128, 1], BF16)
        env["ones_bb"] = L("ones_bb", [128, 128], BF16)
        env["ones_rf"] = L("ones_rf", [1, 128], F32)
        env["n128"] = L("n128_r", [1, 512], BF16)
        env["eps_c"] = L("eps_c", [128, 1], F32)
        env["rstd"] = L("rstd_sb", [128, NT], F32)
        env["nmr"] = L("nmr_sb", [128, NT], F32)
        env["scr"] = L("scr_c", [128, 32], F32)
        env["scr1"] = L("scr_1", [1, 8], F32)
        env["xnfm"] = L("xnfm", [128, ND * 1024], BF16)

        with ExitStack() as sCtx:
          env["sCtx"] = sCtx
          with ExitStack() as sA:
            env["xn2"] = sA.enter_context(nc.sbuf_tensor("xn2f", [128, ND * S], BF16, side="left"))
            env["cs"] = sA.enter_context(nc.sbuf_tensor("cs_sb", [128, S], F32, side="left"))
            ldc = []
            for i, (name, dst) in enumerate([
                    ("lnwc", env["lnw"]), ("lnbc", env["lnb"]),
                    ("gscp", env["gsc"]), ("uscp", env["usc"]),
                    ("pmat", env["pm"]), ("idf", env["idf"]),
                    ("cs", env["cs"])]):
                prog.dma("sync", lambda e, d=dst, s=dram[name]: e.dma_start(out=d[:, :], in_=s[:, :]),
                         sem=f"ldc{i}")
                ldc.append((f"ldc{i}", 16))
            env["ldc"] = ldc
            for dst, valc in [(env["ones_cb"], 1.0), (env["ones_bb"], 1.0), (env["ones_rf"], 1.0),
                              (env["n128"], -128.0), (env["eps_c"], EPS)]:
                prog.op("vector", lambda e, d=dst, vv=valc: e.memset(d[:, :], vv))
            env["dvc_done"] = prog.v("vector")

            _emit_ln(env, xb)
            if debug:
                prog.barrier()
                prog.dma("sync", lambda e: e.dma_start(out=dram["d_xn2"][:, :], in_=env["xn2"][:, :]), sem="st")
                prog.dma("sync", lambda e: e.dma_start(out=dram["d_xnfm"][:, :], in_=env["xnfm"][:, :]), sem="st")
                prog.dma("sync", lambda e: e.dma_start(out=dram["d_rstd"][:, :], in_=env["rstd"][:, :]), sem="st")
                prog.dma("sync", lambda e: e.dma_start(out=dram["d_nmr"][:, :], in_=env["nmr"][:, :]), sem="st")

            if stage >= 2:
                _emit_attention(env)
                if stage == 2:
                    prog.barrier()
          if stage >= 3:
            env["acc"] = top.enter_context(nc.sbuf_tensor("out_acc", [128, NTO * D], F32, side="left"))
            _emit_outproj(env)
            prog.barrier()
        if stage >= 4:
            _emit_ffn(env, xb if stage >= 5 else None)
            prog.barrier()

        prog.barrier()
        st_total = prog.dcnt.get("st", 0)
        if st_total:
            prog.op("sync", lambda e: e.nop(), waits=[("st", st_total)])
        prog.emit()
    return nc



def _emit_ln(env, xb):
    nc, prog = env["nc"], env["prog"]
    scr, eps_c = env["scr"], env["eps_c"]
    rstd, nmr = env["rstd"], env["nmr"]
    xn2, xnfm = env["xn2"], env["xnfm"]
    lnw, lnb, idf = env["lnw"], env["lnb"], env["idf"]
    V, Sc, T = "vector", "scalar", "tensor"

    with ExitStack() as ctx:
        Ls = lambda *a: ctx.enter_context(nc.sbuf_tensor(*a, side="left"))
        NC = 4
        xt = [Ls(f"xt{i}", [128, D], F32) for i in range(NC)]
        xnt = [Ls(f"xnt{i}", [128, D], F32) for i in range(NC)]
        xn2t = [Ls(f"xn2t{i}", [128, D], F32) for i in range(NC)]
        pst = [ctx.enter_context(nc.psum_tensor(f"pst{i}", [128, 512], F32, side="right"))
               for i in range(4)]

        xt_free = [0] * NC
        src_free = [0] * NC
        slot_ev = [None] * 4
        col1_free = [0] * NC
        prev_d_last = [0] * NC
        prev_a_last = [0] * NC
        sq_prev = [0] * NC
        gp_last = [0] * NC
        env["x2ev"] = [0] * NT
        res = {}

        def chain(tt):
            sl = tt % NC
            cb = 8 * sl
            cc = [scr[:, cb + i:cb + i + 1] for i in range(8)]
            ldx = prog.dma("sync",
                           lambda e, t=tt, s=sl: e.dma_start(
                               out=xt[s][:, :], in_=xb[t * 128:(t + 1) * 128, :]),
                           sem=f"ldx{sl}", waits=[(Sc, xt_free[sl]), (V, prev_d_last[sl]),
                                                  ("gpsimd", gp_last[sl])])
            d1 = prog.op(V, lambda e, s=sl, c=cc: e.reduce_sum(c[0], xt[s][:, :], axis=AX.X),
                         waits=[(f"ldx{sl}", ldx), (V, prev_d_last[sl]), (Sc, prev_a_last[sl]),
                                ("gpsimd", gp_last[sl])]
                               + (env["ldc"] if tt == 0 else []))
            a1 = prog.op(Sc, lambda e, s=sl, c=cc: e.activation(xnt[s][:, :], xt[s][:, :],
                                                                AF.Square, accum_out=c[1]),
                         waits=[(f"ldx{sl}", ldx), (V, col1_free[sl]), (Sc, max(sq_prev)),
                                (T, src_free[sl])])
            sq_prev[sl] = a1
            yield
            d2 = prog.op(V, lambda e, c=cc: e.tensor_scalar(c[2], c[0], INV_D, None, ALU.mult),
                         waits=[(V, d1)])
            d3 = prog.op(V, lambda e, c=cc: e.tensor_scalar(c[3], c[1], INV_D, None, ALU.mult),
                         waits=[(Sc, a1), (V, d2)])
            col1_free[sl] = d3
            d4 = prog.op(V, lambda e, c=cc: e.tensor_scalar(c[4], c[2], c[2], None, ALU.mult),
                         waits=[(V, d3)])
            d5 = prog.op(V, lambda e, c=cc: e.tensor_sub(c[3], c[3], c[4]), waits=[(V, d4)])
            yield
            a2 = prog.op(Sc, lambda e, c=cc: e.activation(c[5], c[3], AF.Ln, bias=eps_c[:, :]),
                         waits=[(V, d5)])
            a3 = prog.op(Sc, lambda e, t=tt, c=cc: e.activation(rstd[:, t:t + 1], c[5],
                                                                AF.Exp, scale=-0.5),
                         waits=[(Sc, a2)])
            yield
            d6 = prog.op(V, lambda e, t=tt, c=cc: e.tensor_scalar(nmr[:, t:t + 1], c[2],
                                                                  rstd[:, t:t + 1], -1.0,
                                                                  ALU.mult, ALU.mult),
                         waits=[(Sc, a3)])
            d6b = prog.op("gpsimd", lambda e, t=tt, s=sl: e.tensor_scalar(
                              xnt[s][:, :], xt[s][:, :], rstd[:, t:t + 1],
                              nmr[:, t:t + 1], ALU.mult, ALU.add),
                          waits=[(V, d6), (T, src_free[sl]), (Sc, a1)])
            yield
            a5 = prog.op(Sc, lambda e, s=sl, c=cc: e.activation(xt[s][:, :], xnt[s][:, :],
                                                                AF.Square, accum_out=c[1]),
                         waits=[(V, col1_free[sl]), ("gpsimd", d6b), (Sc, max(sq_prev))])
            sq_prev[sl] = a5
            yield
            d9 = prog.op(V, lambda e, c=cc: e.tensor_scalar(c[3], c[1], INV_D, None, ALU.mult),
                         waits=[(Sc, a5)])
            col1_free[sl] = d9
            yield
            a6 = prog.op(Sc, lambda e, c=cc: e.activation(c[5], c[3], AF.Ln, bias=eps_c[:, :]),
                         waits=[(V, d9)])
            a7 = prog.op(Sc, lambda e, c=cc: e.activation(c[6], c[5], AF.Exp, scale=-0.5),
                         waits=[(Sc, a6)])
            yield
            a8 = prog.op("gpsimd", lambda e, s=sl, c=cc: e.tensor_scalar(
                             xn2t[s][:, :], xnt[s][:, :], c[6], 0.0,
                             ALU.mult, ALU.add),
                         waits=[(Sc, a7), (T, src_free[sl])])
            prev_d_last[sl], prev_a_last[sl] = d9, a7
            xt_free[sl] = a5
            gp_last[sl] = a8
            res[tt] = (sl, a8, d6b)

        def transposes(tt):
            sl, a8, d6b = res[tt]
            x2ev_pair = [0, 0]
            work = [("x2", dc) for dc in range(ND)]
            if tt < NTO:
                work += [("xn", dc) for dc in range(ND)]
            last_t = 0
            for j, (kind, dc) in enumerate(work):
                pslot = j % 4
                p = pst[pslot]
                srcbuf = xn2t[sl] if kind == "x2" else xnt[sl]
                wts = [("gpsimd", a8)] if kind == "x2" else [("gpsimd", d6b)]
                if slot_ev[pslot] is not None:
                    wts.append(slot_ev[pslot])
                mm_t = prog.op(T, lambda e, sb=srcbuf, d=dc, pp=p: e.transpose(
                                   pp[:, 0:128], sb[:, d * 128:(d + 1) * 128], idf[:, :]),
                               waits=wts)
                last_t = mm_t
                if kind == "x2":
                    if dc % 2 == 0:
                        ev = prog.op(V, lambda e, t=tt, d=dc, pp=p: e.tensor_scalar(
                                         xn2[:, d * S + t * 128: d * S + (t + 1) * 128],
                                         pp[:, 0:128], lnw[:, d:d + 1], lnb[:, d:d + 1],
                                         ALU.mult, ALU.add),
                                     waits=[(T, mm_t)])
                        slot_ev[pslot] = (V, ev)
                    else:
                        ev = prog.op(Sc, lambda e, t=tt, d=dc, pp=p: e.activation(
                                         xn2[:, d * S + t * 128: d * S + (t + 1) * 128],
                                         pp[:, 0:128], AF.Identity,
                                         bias=lnb[:, d:d + 1], scale=lnw[:, d:d + 1]),
                                     waits=[(T, mm_t)])
                        slot_ev[pslot] = (Sc, ev)
                    x2ev_pair[dc % 2] = ev
                else:
                    ev = prog.op(V, lambda e, t=tt, d=dc, pp=p: e.tensor_copy(
                                     xnfm[:, d * 1024 + t * 128: d * 1024 + (t + 1) * 128],
                                     pp[:, 0:128]),
                                 waits=[(T, mm_t)])
                    slot_ev[pslot] = (V, ev)
            env["x2ev"][tt] = [(V, x2ev_pair[0]), (Sc, x2ev_pair[1])]
            src_free[sl] = last_t

        for k in range(NT // 4):
            alive = [chain(4 * k + i) for i in range(4)]
            while alive:
                for gch in list(alive):
                    try:
                        next(gch)
                    except StopIteration:
                        alive.remove(gch)
            for i in range(4):
                transposes(4 * k + i)
        env["p1T"] = prog.v("tensor")


def _emit_attention(env):
    nc, prog, dram = env["nc"], env["prog"], env["dram"]
    xn2f = env["xn2"]
    cs, pm = env["cs"], env["pm"]
    ones_cb, ones_rf, scr1 = env["ones_cb"], env["ones_rf"], env["scr1"]
    ones_bb = env["ones_bb"]
    debug = env["debug"]
    V, Sc, T = "vector", "scalar", "tensor"
    wip = dram["wip"]

    env["ctx"] = env["sCtx"].enter_context(
        nc.sbuf_tensor("ctxf", [128, NH * 1024], BF16, side="right"))
    ctxf = env["ctx"]
    with ExitStack() as ctx:
        R = lambda *a: ctx.enter_context(nc.sbuf_tensor(*a, side="right"))
        wf32 = [R(f"wf32_{i}", [128, 512], F32) for i in range(2)]
        wbf = [R(f"wbf{i}", [128, 4096], BF16) for i in range(3)]
        k2 = R("k2", [128, 2 * S], BF16)
        q2 = R("q2", [128, 2 * 1024], BF16)
        v2 = R("v2", [128, NT * 256], BF16)
        ebig = ctx.enter_context(nc.sbuf_tensor("ebig", [128, 8 * 512], BF16, side="left"))
        sqb = R("sqb", [128, 1024], BF16)
        rb = R("rb", [128, 512], F32)
        biasc = R("biasc", [128, 2], F32)
        pp = [ctx.enter_context(nc.psum_tensor(f"pp{i}", [128, 512], F32))
              for i in range(2)]
        ppR = ctx.enter_context(nc.psum_tensor("ppR", [128, 512], F32))
        ppS = ctx.enter_context(nc.psum_tensor("ppS", [128, 512], F32))
        ppD = [ctx.enter_context(nc.psum_tensor(f"ppD{i}", [128, 512], F32))
               for i in range(2)]
        ppC = [ctx.enter_context(nc.psum_tensor(f"ppC{i}", [128, 512], F32))
               for i in range(2)]

        wf32_free = [0, 0]
        wbf_mm_last = [0, 0, 0]
        slab_ready = {}
        pp_free = [[], [], [], []]
        ppR_free = []
        bankF = {"S": [], "D0": [], "D1": []}
        ppC_free = [[], []]
        e_free = 0
        cv_prev = 0
        tcos_free = 0
        sq_free = []
        k2_rd = []
        v2_rd = []
        tile_i = 0

        def load_slab(j):
            for qt in range(8):
                qp = (j * 8 + qt) % 2
                w_extra = [("tensor", env["p1T"])] if (j < 3) else []
                ld = prog.dma("sync",
                              lambda e, jj=j, h=qt, q=qp: e.dma_start(
                                  out=wf32[q][:, :], in_=wip[jj, :, h * 512:(h + 1) * 512]),
                              sem=f"ldw{qp}", waits=[("gpsimd", wf32_free[qp])] + w_extra)
                wts = [(f"ldw{qp}", ld)]
                if qt == 0:
                    wts.append((T, wbf_mm_last[j % 3]))
                cv = prog.op("gpsimd", lambda e, jj=j, h=qt, q=qp: e.tensor_copy(
                                 wbf[jj % 3][:, h * 512:(h + 1) * 512], wf32[q][:, :]),
                             waits=wts)
                wf32_free[qp] = cv
            slab_ready[j] = cv

        load_slab(0)
        load_slab(1)
        load_slab(2)

        pp4 = [pp[0], pp[1], ppC[0], ppC[1]]
        for g in range(8):

            rope_last = 0
            vproj_last = 0
            rope_done = {}
            qk_jobs = [("q", hh, tg) for hh in range(2) for tg in range(2)] + \
                      [("k", hh, tg) for hh in range(2) for tg in range(4)]
            pending_rope = None
            for kind, hh, tg in qk_jobs:
                slabj = 3 * g + (0 if kind == "q" else 1)
                dst = q2 if kind == "q" else k2
                dstoff = hh * (1024 if kind == "q" else 2048) + tg * 512
                sl = tile_i % 4
                w0 = [("gpsimd", slab_ready[slabj])] + pp_free[sl] + k2_rd
                if g == 0:
                    w0 += env["x2ev"][4 * tg + 3]
                if g == 0 and tile_i in (2, 3):
                    w0 += env["x2ev"][14] + env["x2ev"][15]
                k2_rd = []
                m_last = None
                for dc in range(ND):
                    m_last = prog.op(T, lambda e, d=dc, h=hh, t=tg, s=sl, sj=slabj: e.matmul(
                                         pp4[s][:, :],
                                         wbf[sj % 3][:, d * 256 + h * 128: d * 256 + (h + 1) * 128],
                                         xn2f[:, d * S + t * 512: d * S + t * 512 + 512],
                                         start=(d == 0), stop=(d == ND - 1)),
                                     waits=(w0 if dc == 0 else ()))
                wbf_mm_last[slabj % 3] = m_last
                aev = prog.op(Sc, lambda e, s=sl, do=dstoff, dd=dst: e.activation(
                                  dd[:, do:do + 512], pp4[s][:, :], AF.Copy),
                              waits=[(T, m_last)])
                if pending_rope is not None:
                    pending_rope()
                if P2_SUB == "mm":
                    pending_rope = None
                    pp_free[sl] = [(Sc, aev)]
                    tile_i += 1
                    continue

                def rope(m_last=m_last, aev=aev, sl=sl, tg=tg, dst=dst, dstoff=dstoff,
                         key=(kind, hh, tg)):
                    nonlocal tcos_free, ppR_free, rope_last
                    mB = prog.op(T, lambda e, do=dstoff, dd=dst: e.matmul(
                                     ppR[0:64, :], pm[:, :], dd[0:64, do:do + 512],
                                     start=True, stop=True),
                                 waits=[(Sc, aev)] + ppR_free)
                    dco = prog.op(V, lambda e, t=tg, do=dstoff, dd=dst: e.tensor_mul(
                                      dd[0:64, do:do + 512], dd[0:64, do:do + 512],
                                      cs[0:64, t * 512:t * 512 + 512]),
                                  waits=[(T, mB)])
                    dt2 = prog.op(V, lambda e, t=tg: e.tensor_mul(
                                      rb[0:64, :], ppR[0:64, :], cs[64:128, t * 512:t * 512 + 512]),
                                  waits=[(T, mB), (V, tcos_free)])
                    dadd = prog.op(V, lambda e, do=dstoff, dd=dst: e.tensor_add(
                                       dd[0:64, do:do + 512], dd[0:64, do:do + 512], rb[0:64, :]),
                                   waits=[(V, dt2)])
                    rope_last = dadd
                    rope_done[key] = dadd
                    tcos_free = dadd
                    ppR_free = [(V, dt2)]
                    pp_free[sl] = [(Sc, aev)]

                pp_free[sl] = [(Sc, aev)]
                pending_rope = rope
                tile_i += 1
            if pending_rope is not None:
                pending_rope()
                pending_rope = None

            slabv = 3 * g + 2
            bias_ready = [0, 0]
            pb = [ppS, ppD[0], ppD[1]]
            pbk = ["S", "D0", "D1"]
            bsteps = []
            if P2_SUB != "qkv":
                for hh in range(2):
                    def mk_head(hh=hh):
                        state = {}

                        def s_sq(kind, half):
                            def f():
                                if kind == "q":
                                    state["gate"] = prog.op(Sc, lambda e, h=hh: e.activation(
                                        sqb[:, 0:1024], q2[:, h * 1024:(h + 1) * 1024], AF.Square),
                                        waits=[(V, rope_done[("q", hh, 1)])] + sq_free)
                                else:
                                    state["gate"] = prog.op(Sc, lambda e, h=hh, q=half: e.activation(
                                        sqb[:, :],
                                        k2[:, h * 2048 + q * 1024: h * 2048 + q * 1024 + 1024],
                                        AF.Square),
                                        waits=[(V, rope_done[("k", hh, 2 * half + 1)]),
                                               (T, state["mb"])])
                            return f

                        def s_mm(j):
                            def f():
                                bank = j % 3
                                mb = prog.op(T, lambda e, b=j % 2, k=bank: e.matmul(
                                                 pb[k][:, :], ones_bb[:, :],
                                                 sqb[:, b * 512:(b + 1) * 512],
                                                 start=True, stop=True),
                                             waits=[(Sc, state["gate"])] + bankF[pbk[bank]])
                                state["mb"] = mb
                                rd = prog.op(V, lambda e, j_=j, k=bank: e.reduce_max(
                                                 scr1[0:1, j_:j_ + 1], pb[k][0:1, :], axis=AX.X),
                                             waits=[(T, mb)])
                                bankF[pbk[bank]] = [(V, rd)]
                                state["rd"] = rd
                            return f

                        def s_fin():
                            nonlocal_state = state

                            def f():
                                rd_last = nonlocal_state["rd"]
                                c1 = prog.op(V, lambda e: e.tensor_max(scr1[0:1, 6:7], scr1[0:1, 0:1],
                                                                       scr1[0:1, 1:2]), waits=[(V, rd_last)])
                                c2 = prog.op(V, lambda e: e.tensor_max(scr1[0:1, 7:8], scr1[0:1, 2:3],
                                                                       scr1[0:1, 3:4]), waits=[(V, c1)])
                                c3 = prog.op(V, lambda e: e.tensor_max(scr1[0:1, 7:8], scr1[0:1, 7:8],
                                                                       scr1[0:1, 4:5]), waits=[(V, c2)])
                                c4 = prog.op(V, lambda e: e.tensor_max(scr1[0:1, 7:8], scr1[0:1, 7:8],
                                                                       scr1[0:1, 5:6]), waits=[(V, c3)])
                                c5 = prog.op(V, lambda e: e.tensor_add(scr1[0:1, 6:7], scr1[0:1, 6:7],
                                                                       scr1[0:1, 7:8]), waits=[(V, c4)])
                                c6 = prog.op(V, lambda e: e.tensor_scalar(scr1[0:1, 7:8], scr1[0:1, 6:7],
                                                                          -0.5 * SCALE, None, ALU.mult),
                                             waits=[(V, c5)])
                                mbb = prog.op(T, lambda e: e.matmul(ppS[:, 0:1], ones_rf[:, :],
                                                                    scr1[0:1, 7:8], start=True, stop=True),
                                              waits=[(V, c6)] + bankF["S"])
                                bias_ready[hh] = prog.op(V, lambda e, h=hh: e.tensor_copy(
                                                             biasc[:, h:h + 1], ppS[:, 0:1]),
                                                         waits=[(T, mbb)])
                                bankF["S"] = [(V, bias_ready[hh])]
                            return f

                        steps = [s_sq("q", 0), s_mm(0), s_mm(1), s_sq("k", 0), s_mm(2),
                                 s_mm(3), s_sq("k", 1), s_mm(4), s_mm(5), s_fin()]

                        def close_sqfree():
                            nonlocal sq_free
                            sq_free = [(T, state["mb"])]
                        steps.append(close_sqfree)
                        return steps
                    bsteps += mk_head()

            bi = 0
            for tb in range(NT):
                sl = tile_i % 4
                w0 = [("gpsimd", slab_ready[slabv])] + pp_free[sl] + v2_rd
                if g == 0:
                    w0 += env["x2ev"][tb]
                v2_rd = []
                m_last = None
                for dc in range(ND):
                    m_last = prog.op(T, lambda e, d=dc, t=tb, s=sl, sj=slabv: e.matmul(
                                         pp4[s][:, 0:256],
                                         xn2f[:, d * S + t * 128: d * S + t * 128 + 128],
                                         wbf[sj % 3][:, d * 256: d * 256 + 256],
                                         start=(d == 0), stop=(d == ND - 1)),
                                     waits=(w0 if dc == 0 else ()))
                wbf_mm_last[slabv % 3] = m_last
                aev = prog.op(Sc, lambda e, s=sl, t=tb: e.activation(
                                  v2[:, t * 256:(t + 1) * 256], pp4[s][:, 0:256], AF.Copy),
                              waits=[(T, m_last)])
                vproj_last = aev
                pp_free[sl] = [(Sc, aev)]
                tile_i += 1
                if tb >= 2:
                    for _ in range(2):
                        if bi < len(bsteps):
                            bsteps[bi]()
                            bi += 1
            while bi < len(bsteps):
                bsteps[bi]()
                bi += 1
            for j in range(3 * g + 3, min(3 * g + 6, 24)):
                load_slab(j)

            pps = [pp[0], pp[1], ppR]
            pp3_free = [pp_free[0], pp_free[1], ppR_free]
            sc3 = 0
            for hh in (range(2) if P2_SUB == "all" else ()):
                for qg in range(2):
                    un = (g * 4 + hh * 2 + qg) % 2
                    dkey = "D0" if un == 0 else "D1"
                    ms_seq = [0] * 16
                    ex_seq = [0] * 16
                    md_seq = [0] * 16
                    for kb in range(16):
                        sl = sc3 % 3
                        sc3 += 1
                        wts = list(pp3_free[sl])
                        if kb == 0:
                            wts += [(V, rope_last), (Sc, vproj_last)]
                        ms_seq[kb] = prog.op(T, lambda e, h=hh, q=qg, k=kb, s=sl: e.matmul(
                                                 pps[s][:, :],
                                                 k2[:, h * 2048 + k * 128: h * 2048 + (k + 1) * 128],
                                                 q2[:, h * 1024 + q * 512: h * 1024 + q * 512 + 512],
                                                 start=True, stop=True),
                                             waits=wts)
                        wts = [(T, ms_seq[kb]), (V, bias_ready[hh])]
                        if kb == 0:
                            wts.append((T, e_free))
                        if kb >= 8:
                            wts.append((T, md_seq[kb - 8]))
                        ex_seq[kb] = prog.op(Sc, lambda e, k=kb, s=sl, h=hh: e.activation(
                                                 ebig[:, (k % 8) * 512:(k % 8) * 512 + 512],
                                                 pps[s][:, :],
                                                 AF.Exp, bias=biasc[:, h:h + 1], scale=SCALE),
                                             waits=wts)
                        pp3_free[sl] = [(Sc, ex_seq[kb])]
                        if kb >= 2:
                            prog.op(T, lambda e, k=kb - 2, h=hh, u=un: e.matmul(
                                        ppC[u][:, :],
                                        v2[:, k * 256 + h * 128: k * 256 + (h + 1) * 128],
                                        ebig[:, (k % 8) * 512:(k % 8) * 512 + 512],
                                        start=(k == 0), stop=False),
                                    waits=[(Sc, ex_seq[kb - 2])] +
                                          (ppC_free[un] + pp_free[2 + un]
                                           if kb == 2 else []))
                            md_seq[kb - 2] = prog.op(T, lambda e, k=kb - 2, u=un: e.matmul(
                                        ppD[u][:, :], ones_bb[:, :],
                                        ebig[:, (k % 8) * 512:(k % 8) * 512 + 512],
                                        start=(k == 0), stop=False),
                                    waits=(bankF[dkey] if kb == 2 else ()))
                    prog.op(T, lambda e, h=hh, u=un: e.matmul(
                                ppC[u][:, :],
                                v2[:, 14 * 256 + h * 128: 14 * 256 + (h + 1) * 128],
                                ebig[:, 6 * 512:7 * 512],
                                start=False, stop=False),
                            waits=[(Sc, ex_seq[14])])
                    md_seq[14] = prog.op(T, lambda e, u=un: e.matmul(
                                ppD[u][:, :], ones_bb[:, :],
                                ebig[:, 6 * 512:7 * 512], start=False, stop=False))
                    mp_last = prog.op(T, lambda e, h=hh, u=un: e.matmul(
                                          ppC[u][:, :],
                                          v2[:, 15 * 256 + h * 128: 15 * 256 + (h + 1) * 128],
                                          ebig[:, 7 * 512:8 * 512],
                                          start=False, stop=True),
                                      waits=[(Sc, ex_seq[15])])
                    md_last = prog.op(T, lambda e, u=un: e.matmul(
                                ppD[u][:, :], ones_bb[:, :],
                                ebig[:, 7 * 512:8 * 512], start=False, stop=True))
                    md_seq[15] = md_last
                    e_free = md_last
                    rcb = prog.op(V, lambda e, u=un: e.reciprocal(rb[:, :], ppD[u][:, :]),
                                  waits=[(T, md_last), (V, cv_prev), (V, rope_last)])
                    bankF[dkey] = [(V, rcb)]
                    cv = prog.op(V, lambda e, h=hh, q=qg, gg=g, u=un: e.tensor_mul(
                                     ctxf[:, (2 * gg + h) * 1024 + q * 512:
                                          (2 * gg + h) * 1024 + q * 512 + 512],
                                     ppC[u][:, :], rb[:, :]),
                                 waits=[(V, rcb), (T, mp_last)])
                    cv_prev = cv
                    ppC_free[un] = [(V, cv)]
                    k2_rd = [(T, mp_last), (Sc, ex_seq[15])]
                    v2_rd = [(T, mp_last)]
            pp_free[0] = pp3_free[0]
            pp_free[1] = pp3_free[1]
            pp_free[2] = ppC_free[0]
            pp_free[3] = ppC_free[1]
            ppR_free = pp3_free[2]
            k2_rd = k2_rd[-2:]
            v2_rd = v2_rd[-1:]
            if P2_SUB == "all":
                tcos_free = cv_prev

            if debug and g == 0:
                stv = prog.dma("sync", lambda e: e.dma_start(out=dram["d_q"][:, :], in_=q2[:, :]),
                               sem="st", waits=[(V, rope_last)])
                stv = prog.dma("sync", lambda e: e.dma_start(out=dram["d_k"][:, :], in_=k2[:, :]),
                               sem="st", waits=[(V, rope_last)])
                stv = prog.dma("sync", lambda e: e.dma_start(out=dram["d_v"][:, :], in_=v2[:, :]),
                               sem="st", waits=[(Sc, vproj_last)])
                if P2_SUB != "qkv":
                    stv = prog.dma("sync", lambda e: e.dma_start(out=dram["d_bias"][:, :], in_=biasc[:, :]),
                                   sem="st", waits=[(V, bias_ready[1])])
                sn = prog.op("sync", lambda e: e.nop(), waits=[("st", stv)])
                k2_rd.append(("sync", sn))
                v2_rd.append(("sync", sn))
                bankF["S"].append(("sync", sn))

        if debug and P2_SUB == "all":
            stv = 0
            for blk in range(32):
                stv = prog.dma("sync", lambda e, bb=blk: e.dma_start(
                                   out=dram["d_ctx"][:, bb * 512:(bb + 1) * 512],
                                   in_=ctxf[:, bb * 512:(bb + 1) * 512]),
                               sem="st", waits=ppC_free[0] + ppC_free[1])
            prog.op("sync", lambda e: e.nop(), waits=[("st", stv)])
        env["p2T"] = prog.v("tensor")
        env["p2V"] = prog.v("vector")
        env["p2Sc"] = prog.v("scalar")
        env["p2G"] = prog.v("gpsimd")


def _emit_outproj(env):
    nc, prog, dram = env["nc"], env["prog"], env["dram"]
    ctxf, acc = env["ctx"], env["acc"]
    debug = env["debug"]
    V, Sc, T = "vector", "scalar", "tensor"
    wop = dram["wop"]

    with ExitStack() as ctx:
        Ls = lambda *a: ctx.enter_context(nc.sbuf_tensor(*a, side="left"))
        wof = [Ls(f"wof{i}", [128, 4096], F32) for i in range(2)]
        wob = [Ls(f"wob{i}", [128, 8192], BF16) for i in range(2)]
        po = [ctx.enter_context(nc.psum_tensor(f"po{i}", [128, 512], F32))
              for i in range(2)]
        wof_free = [0, 0]
        wob_mm_last = [0, 0]
        po_free = [[(Sc, env["p2Sc"])], [(Sc, env["p2Sc"])]]
        ev_last = 0
        slab_casts = {}

        def load_wo(fg):
            wp = fg % 2
            halves = [0, 0]
            for half in range(2):
                wx = [(V, wof_free[wp])]
                if fg < 2 and half == 0:
                    wx += [(T, env["p2T"]), (V, env["p2V"]),
                           (Sc, env["p2Sc"]), ("gpsimd", env["p2G"])]
                ld = prog.dma("sync",
                              lambda e, f=fg, h=half, q=wp: e.dma_start(
                                  out=wof[q][:, :], in_=wop[f, :, h * 4096:(h + 1) * 4096]),
                              sem=f"ldo{wp}", waits=wx)
                wts = [(f"ldo{wp}", ld)]
                if half == 0:
                    wts.append((T, wob_mm_last[wp]))
                cv = prog.op(V, lambda e, h=half, q=wp: e.tensor_copy(
                                 wob[q][:, h * 4096:(h + 1) * 4096], wof[q][:, :]),
                             waits=wts)
                wof_free[wp] = cv
                halves[half] = cv
            slab_casts[fg] = tuple(halves)

        load_wo(0)
        load_wo(1)
        for fg in range(4):
            wp = fg % 2
            c0, c1 = slab_casts[fg]
            for tb in range(NTO):
                sl = tb % 2
                w0 = [(V, c0)] + po_free[sl]
                m_last = None
                for h in range(NH):
                    m_last = prog.op(T, lambda e, hh=h, t=tb, s=sl, q=wp: e.matmul(
                                         po[s][:, :],
                                         ctxf[:, hh * 1024 + t * 128: hh * 1024 + (t + 1) * 128],
                                         wob[q][:, hh * 512:(hh + 1) * 512],
                                         start=(hh == 0), stop=(hh == NH - 1)),
                                     waits=(w0 if h == 0 else
                                            ([(V, c1)] if (h == 8 and tb == 0) else ())))
                ev = prog.op(Sc, lambda e, t=tb, f=fg, s=sl: e.activation(
                                 acc[:, t * D + f * 512: t * D + f * 512 + 512],
                                 po[s][:, :], AF.Copy),
                             waits=[(T, m_last)] +
                                   ([(V, env["p2V"]), ("gpsimd", env["p2G"])]
                                    if (fg == 0 and tb == 0) else []))
                po_free[sl] = [(Sc, ev)]
                ev_last = ev
            wob_mm_last[wp] = m_last
            if fg + 2 < 4:
                load_wo(fg + 2)
        if debug:
            stv = prog.dma("sync", lambda e: e.dma_start(out=dram["d_acc"][:, :], in_=acc[:, :]),
                           sem="st", waits=[(Sc, ev_last)])
            prog.op("sync", lambda e: e.nop(), waits=[("st", stv)])


def _emit_ffn(env, fxb=None):
    nc, prog, dram = env["nc"], env["prog"], env["dram"]
    xnfm, acc = env["xnfm"], env["acc"]
    do_final = fxb is not None
    rstd, nmr = env["rstd"], env["nmr"]
    gsc, usc = env["gsc"], env["usc"]
    ones_cb, ones_rf, n128 = env["ones_cb"], env["ones_rf"], env["n128"]
    ones_bb = env["ones_bb"]
    debug = env["debug"]
    V, Sc, T, G = "vector", "scalar", "tensor", "gpsimd"
    gbp, ubp, dwp, dscp = dram["gbp"], dram["ubp"], dram["dwp"], dram["dscp"]

    with ExitStack() as ctx:
        Ls = lambda *a: ctx.enter_context(nc.sbuf_tensor(*a, side="left"))
        sgp = Ls("sgp", [128, 512], F32)
        sup = Ls("sup", [128, 512], F32)
        dscb = Ls("dscb", [128, 2048], BF16)
        hsr = Ls("hsr", [1, 512], BF16)
        if do_final:
            xst = Ls("xst0", [128, 512], F32)
            sst = ctx.enter_context(nc.psum_tensor("pfs", [128, 512], F32))
            r1p = Ls("r1p", [128, 16], F32)
        hfm = Ls("hfm", [128, NI * 512], BF16)
        pg = [ctx.enter_context(nc.psum_tensor(f"pg{i}", [128, 512], F32))
              for i in range(2)]
        pu = [ctx.enter_context(nc.psum_tensor(f"pu{i}", [128, 512], F32))
              for i in range(2)]
        phs = ctx.enter_context(nc.psum_tensor("phs", [128, 512], F32))
        ppB = ctx.enter_context(nc.psum_tensor("ppB", [128, 512], F32))
        pd = [pg[0], pg[1], pu[0], pu[1]]

        mprev = 0
        cprev = 0
        with ExitStack() as c0:
            dsrow = c0.enter_context(nc.sbuf_tensor("dsrow", [1, 512], F32, side="left"))
            for piece in range(4):
                ld = prog.dma("sync", lambda e, p=piece: e.dma_start(
                                  out=dsrow[:, :], in_=dscp[p:p + 1, :]),
                              sem="ldd", waits=[(T, mprev)])
                mprev = prog.op(T, lambda e: e.matmul(ppB[:, :], ones_rf[:, :],
                                                      dsrow[0:1, :], start=True, stop=True),
                                waits=[("ldd", ld), (V, cprev)])
                cprev = prog.op(V, lambda e, p=piece: e.tensor_copy(
                                    dscb[:, p * 512:(p + 1) * 512], ppB[:, :]),
                                waits=[(T, mprev)])
        dscb_ready = cprev
        ax_prev = 0
        ad_prev = 0
        ga_tab = [0] * 4
        if do_final:
            r1p_seq = prog.op(V, lambda e: e.tensor_scalar(r1p[:, :], rstd[:, :],
                                                           1.0, None, ALU.add))
            outp = dram["out"]
        wstage = Ls("wstage", [128, 16384], BF16)
        gb2 = [wstage[:, p * 4096:(p + 1) * 4096] for p in range(2)]
        ub2 = [wstage[:, 8192 + p * 4096: 8192 + (p + 1) * 4096] for p in range(2)]
        dw2 = [wstage[:, q * 8192:(q + 1) * 8192] for q in range(2)]

        sg_free = 0
        h_free = 0
        acc_g = 0
        he_prev = 0
        hs_prev = 0
        gb_mm_last = [0, 0]
        dw_mm_last = [mprev, mprev]
        pg_free = [[], []]
        pu_free = [[], []]
        pd_free = [[], [], [], []]
        phs_free = []
        tmp_free = 0
        for tg in range(2):
            pending_hs = []
            gb_ready = [0, 0]
            ub_ready = [0, 0]
            gh = 0
            for isc in range(32):
                par = isc % 2
                gb_ready[par] = prog.dma("sync",
                                         lambda e, i=isc, p=par: e.dma_start(
                                             out=gb2[p], in_=gbp[i, :, :]),
                                         sem=f"ldg{par}",
                                         waits=[(T, gb_mm_last[par]),
                                                (T, dw_mm_last[0])])
                ub_ready[par] = prog.dma("sync",
                                         lambda e, i=isc, p=par: e.dma_start(
                                             out=ub2[p], in_=ubp[i, :, :]),
                                         sem=f"ldu{par}",
                                         waits=[(T, gb_mm_last[par]),
                                                (T, dw_mm_last[1])])
                for ic2 in range(2):
                    ic = isc * 2 + ic2
                    psl = ic % 2
                    m = None
                    for hc in range(ND):
                        wts = []
                        if hc == 0:
                            wts = [(f"ldg{par}", gb_ready[par])] + pg_free[psl]
                        m = prog.op(T, lambda e, h=hc, i2=ic2, p=par, t=tg, s=psl: e.matmul(
                                        pg[s][:, :],
                                        gb2[p][:, h * 256 + i2 * 128: h * 256 + i2 * 128 + 128],
                                        xnfm[:, h * 1024 + t * 512: h * 1024 + t * 512 + 512],
                                        start=(h == 0), stop=(h == ND - 1)),
                                    waits=wts)
                    mg = m
                    if len(pending_hs) > 1:
                        pending_hs.pop(0)()
                    a1 = prog.op(Sc, lambda e, i=ic, s=psl: e.activation(
                                     sgp[:, :], pg[s][:, :], AF.Silu,
                                     scale=gsc[:, i:i + 1]),
                                 waits=[(T, mg), (G, sg_free)])
                    m = None
                    for hc in range(ND):
                        wts = []
                        if hc == 0:
                            wts = [(f"ldu{par}", ub_ready[par])] + pu_free[psl]
                        m = prog.op(T, lambda e, h=hc, i2=ic2, p=par, t=tg, s=psl: e.matmul(
                                        pu[s][:, :],
                                        ub2[p][:, h * 256 + i2 * 128: h * 256 + i2 * 128 + 128],
                                        xnfm[:, h * 1024 + t * 512: h * 1024 + t * 512 + 512],
                                        start=(h == 0), stop=(h == ND - 1)),
                                    waits=wts)
                    mu_ = m
                    gb_mm_last[par] = mu_
                    a2 = prog.op(Sc, lambda e, i=ic, s=psl: e.activation(
                                     sup[:, :], pu[s][:, :], AF.Identity,
                                     scale=usc[:, i:i + 1]),
                                 waits=[(T, mu_), (G, sg_free)])
                    gh = prog.op(G, lambda e, i=ic: e.tensor_mul(
                                     hfm[:, i * 512:(i + 1) * 512], sgp[:, :], sup[:, :]),
                                 waits=[(Sc, a2), (Sc, a1), (T, h_free)])
                    sg_free = gh
                    pg_free[psl] = [(Sc, a1)]
                    pu_free[psl] = [(Sc, a2)]

                    def hsum(i=ic, gh=gh):
                        nonlocal hs_prev
                        hs_prev = prog.op(T, lambda e, i2=i: e.matmul(
                                              phs[:, :], ones_bb[:, :],
                                              hfm[:, i2 * 512:(i2 + 1) * 512],
                                              start=(i2 == 0), stop=(i2 == NI - 1)),
                                          waits=([(V, he_prev)] + phs_free
                                                 if i == 0 else []) + [(G, gh)])
                    pending_hs.append(hsum)
            for f in pending_hs:
                f()
            pending_hs.clear()
            phs_free = []
            he = prog.op(V, lambda e: e.tensor_copy(hsr[:, :], phs[0:1, :]),
                         waits=[(T, hs_prev)])
            he_prev = he
            vbar = prog.v(V)
            scbar = prog.v(Sc)

            for fg in range(4):
                stops = [None] * 4
                for pc in range(4):
                    par = pc % 2
                    lw = prog.dma("sync", lambda e, f=fg, p=pc, q=par: e.dma_start(
                                      out=dw2[q], in_=dwp[f * 4 + p, :, :]),
                                  sem=f"lddw{par}",
                                  waits=[(T, dw_mm_last[par]),
                                         (T, gb_mm_last[0]), (T, gb_mm_last[1])])
                    for tb in range(4):
                        for icp in range(16):
                            wts = []
                            if icp == 0 and tb == 0:
                                wts = [(f"lddw{par}", lw), (G, gh), (V, he)]
                            if icp == 0 and pc == 0:
                                wts = wts + pd_free[tb] + \
                                      ([(V, vbar), (Sc, scbar)] if fg == 0 else [])
                            m = prog.op(T, lambda e, f=fg, p=pc, q=par, t=tb, i=icp: e.matmul(
                                            pd[t][:, :],
                                            hfm[:, (p * 16 + i) * 512 + t * 128:
                                                (p * 16 + i) * 512 + t * 128 + 128],
                                            dw2[q][:, i * 512:(i + 1) * 512],
                                            start=(p == 0 and i == 0), stop=False),
                                        waits=wts)
                        if pc == 3:
                            stops[tb] = prog.op(T, lambda e, t=tb: e.matmul(
                                                    pd[t][:, :],
                                                    hsr[0:1, t * 128:(t + 1) * 128],
                                                    n128[0:1, :],
                                                    start=False, stop=True))
                    dw_mm_last[par] = prog.v(T)
                for tb in range(4):
                    tv = prog.op(V, lambda e, t=tb, f=fg: e.tensor_mul(
                                     sgp[:, :], pd[t][:, :], dscb[:, f * 512:(f + 1) * 512]),
                                 waits=[(T, stops[tb]), (G, tmp_free), (G, gh),
                                        (V, dscb_ready)])
                    ga = prog.op(G, lambda e, t=tb, f=fg, tt=tg: e.tensor_add(
                                     acc[:, (tt * 4 + t) * D + f * 512:
                                         (tt * 4 + t) * D + f * 512 + 512],
                                     acc[:, (tt * 4 + t) * D + f * 512:
                                         (tt * 4 + t) * D + f * 512 + 512],
                                     sgp[:, :]),
                                 waits=[(V, tv)])
                    tmp_free = ga
                    acc_g = ga
                    ga_tab[tb] = ga
                    pd_free[tb] = [(V, tv)]
                if do_final:
                    for t4 in range(4):
                        tbg = tg * 4 + t4
                        ld = prog.dma("scalar", lambda e, tt=tbg, f=fg: e.dma_start(
                                          out=xst[:, :],
                                          in_=fxb[tt * 128:(tt + 1) * 128,
                                                  f * 512:(f + 1) * 512]),
                                      sem="ldp0", waits=[(V, ax_prev)])
                        ax = prog.op(V, lambda e, tt=tbg: e.tensor_scalar(
                                         sst[:, :], xst[:, :], r1p[:, tt:tt + 1],
                                         nmr[:, tt:tt + 1], ALU.mult, ALU.add),
                                     waits=[("ldp0", ld), (V, ad_prev),
                                            (V, r1p_seq)])
                        ad = prog.op(V, lambda e, tt=tbg, f=fg: e.tensor_add(
                                         acc[:, tt * D + f * 512: tt * D + f * 512 + 512],
                                         acc[:, tt * D + f * 512: tt * D + f * 512 + 512],
                                         sst[:, :]),
                                     waits=[(V, ax), (G, ga_tab[t4])])
                        prog.dma("scalar", lambda e, tt=tbg, f=fg: e.dma_start(
                                     out=outp[tt * 128:(tt + 1) * 128,
                                              f * 512:(f + 1) * 512],
                                     in_=acc[:, tt * D + f * 512:
                                             tt * D + f * 512 + 512]),
                                 sem="stp", waits=[(V, ad)])
                        ax_prev = ax
                        ad_prev = ad
            sg_free = tmp_free
            h_free = prog.v(T)
            phs_free = [(V, he)]
            pg_free[0] = pd_free[0]
            pg_free[1] = pd_free[1]
            pu_free[0] = pd_free[2]
            pu_free[1] = pd_free[3]
        if do_final:
            prog.op("sync", lambda e: e.nop(), waits=[("stp", prog.dcnt["stp"])])
        if debug and not do_final:
            stv = prog.dma("sync", lambda e: e.dma_start(out=dram["d_acc2"][:, :], in_=acc[:, :]),
                           sem="st", waits=[(G, acc_g)])
            prog.op("sync", lambda e: e.nop(), waits=[("st", stv)])


def _emit_final(env, xb):
    nc, prog, dram = env["nc"], env["prog"], env["dram"]
    acc = env["acc"]
    rstd, nmr = env["rstd"], env["nmr"]
    V, Sc, T, G = "vector", "scalar", "tensor", "gpsimd"
    out = dram["out"]

    with ExitStack() as ctx:
        Ls = lambda *a: ctx.enter_context(nc.sbuf_tensor(*a, side="left"))
        xt = [Ls(f"fxt{i}", [128, D], F32) for i in range(2)]
        xnt = [Ls(f"fxn{i}", [128, D], F32) for i in range(2)]
        fin = [Ls(f"fin{i}", [128, D], F32) for i in range(2)]
        xt_free = [0, 0]
        xn_free = [0, 0]
        fin_free = [0, 0]
        for tb in range(NTO):
            sl = tb % 2
            ld = prog.dma("sync", lambda e, t=tb, s=sl: e.dma_start(
                              out=xt[s][:, :], in_=xb[t * 128:(t + 1) * 128, :]),
                          sem=f"ldf{sl}", waits=[(V, xt_free[sl])])
            ax = prog.op(Sc, lambda e, t=tb, s=sl: e.activation(
                             xnt[s][:, :], xt[s][:, :], AF.Identity,
                             bias=nmr[:, t:t + 1], scale=rstd[:, t:t + 1]),
                         waits=[(f"ldf{sl}", ld), (V, xn_free[sl])])
            d1 = prog.op(V, lambda e, s=sl: e.tensor_add(fin[s][:, :], xt[s][:, :],
                                                         xnt[s][:, :]),
                         waits=[(Sc, ax)] + ([(f"stf{sl}", fin_free[sl])]
                                            if fin_free[sl] else []))
            d2 = prog.op(V, lambda e, t=tb, s=sl: e.tensor_add(
                             fin[s][:, :], fin[s][:, :], acc[:, t * D:(t + 1) * D]),
                         waits=[(V, d1)])
            fin_free[sl] = prog.dma("sync", lambda e, t=tb, s=sl: e.dma_start(
                                        out=out[t * 128:(t + 1) * 128, :], in_=fin[s][:, :]),
                                    sem=f"stf{sl}", waits=[(V, d2)])
            xt_free[sl] = d1
            xn_free[sl] = d1
        prog.op("sync", lambda e: e.nop(),
                waits=[("stf0", prog.dcnt.get("stf0", 0)), ("stf1", prog.dcnt.get("stf1", 0))])



def host_prep(inputs):
    X = np.asarray(inputs["L0_attn_X"], dtype=np.float32)
    ln_w = np.asarray(inputs["L0_attn_ln_w"], dtype=np.float32)
    ln_b = np.asarray(inputs["L0_attn_ln_b"], dtype=np.float32)
    W_in = np.ascontiguousarray(np.asarray(inputs["L0_attn_W_in"], dtype=np.float32))
    W_out = np.asarray(inputs["L0_attn_W_out"], dtype=np.float32)
    gw = np.asarray(inputs["L0_ffn_gate_w"])
    uw = np.asarray(inputs["L0_ffn_up_w"])
    dw = np.asarray(inputs["L0_ffn_down_w"])
    gs = np.asarray(inputs["L0_ffn_gate_scale"], dtype=np.float32)
    us = np.asarray(inputs["L0_ffn_up_scale"], dtype=np.float32)
    ds = np.asarray(inputs["L0_ffn_down_scale"], dtype=np.float32)

    wip = np.empty((24, 128, 4096), np.float32)
    for g in range(8):
        for w, base in enumerate((0, 2048, 4096)):
            blk = W_in[:, base + 2 * g * 128: base + 2 * g * 128 + 256]
            wip[g * 3 + w] = blk.reshape(16, 128, 256).transpose(1, 0, 2).reshape(128, 4096)
    wop = np.empty((4, 128, 8192), np.float32)
    for fg in range(4):
        blk = W_out[:, fg * 512:(fg + 1) * 512]
        wop[fg] = blk.reshape(16, 128, 512).transpose(1, 0, 2).reshape(128, 8192)

    def pack_gu(wq):
        a = wq.astype(np.float32).astype(BF)
        a = a.reshape(32, 256, 16, 128)
        return np.ascontiguousarray(a.transpose(0, 3, 2, 1)).reshape(32, 128, 4096)

    gbp = pack_gu(gw)
    ubp = pack_gu(uw)
    a = dw.astype(np.float32).astype(BF)
    a = a.reshape(4, 512, 4, 16, 128)
    dwp = np.ascontiguousarray(a.transpose(0, 2, 4, 3, 1)).reshape(16, 128, 8192)

    shared = dict(
        wip=wip, wop=wop, gbp=gbp, ubp=ubp, dwp=dwp,
        lnwc=np.ascontiguousarray(ln_w.reshape(ND, 128).T),
        lnbc=np.ascontiguousarray(ln_b.reshape(ND, 128).T),
        gscp=np.ascontiguousarray(gs.reshape(NI, 128).T),
        uscp=np.ascontiguousarray(us.reshape(NI, 128).T),
        dscp=np.ascontiguousarray(ds.reshape(4, 512)),
        idf=np.eye(128, dtype=np.float32),
    )
    pm = np.zeros((64, 64), np.float32)
    for t in range(32):
        pm[2 * t + 1, 2 * t] = -1.0
        pm[2 * t, 2 * t + 1] = 1.0
    shared["pmat"] = pm.astype(BF)

    inv_freq = (1.0 / (10000.0 ** (np.arange(0, 64, 2, dtype=np.float32)
                                   / np.float32(64.0)))).astype(np.float32)
    in_maps = []
    for c in range(8):
        b, qh = c // 2, c % 2
        off = qh * 1024
        pos = (np.arange(S) + off) % S
        ang = pos[:, None].astype(np.float32) * inv_freq[None, :]
        ang2 = np.repeat(ang, 2, axis=1)
        m = dict(shared)
        m["cs"] = np.ascontiguousarray(np.concatenate(
            [np.cos(ang2).T, np.sin(ang2).T], axis=0).astype(np.float32))
        m["xb"] = np.ascontiguousarray(np.concatenate([X[b, off:], X[b, :off]], axis=0))
        in_maps.append(m)
    return in_maps


def kernel(**inputs):
    in_maps = host_prep(inputs)
    nc = build_program(stage=STAGE, debug=DEBUG)
    res = run_bass_kernel_spmd(nc, in_maps, core_ids=list(range(8)))
    Y = np.empty((B, S, D), np.float32)
    for c in range(8):
        b, qh = c // 2, c % 2
        off = qh * 1024
        Y[b, off:off + 1024, :] = res.results[c]["out"]
    return Y


# version 26
# speedup vs baseline: 1.1993x; 1.1993x over previous
import math
from contextlib import ExitStack

import numpy as np
import ml_dtypes
import concourse.bass as bass
import concourse.mybir as mybir
from concourse.bass_utils import run_bass_kernel_spmd

F32 = mybir.dt.float32
BF16 = mybir.dt.bfloat16
AF = mybir.ActivationFunctionType
ALU = mybir.AluOpType
AX = mybir.AxisListType
BF = ml_dtypes.bfloat16

B, S, D, NH, DH, II = 4, 2048, 2048, 16, 128, 8192
ND = D // 128
NT = S // 128
NTO = 8
NI = II // 128
SCALE = 1.0 / math.sqrt(DH)
EPS = 1e-5
INV_D = 1.0 / D

STAGE = 5
DEBUG = False
P2_SUB = "all"

ENGS = ("sync", "tensor", "vector", "scalar", "gpsimd")


class Prog:

    def __init__(self, nc, ctx):
        self.nc = nc
        self.ctx = ctx
        self.ops = {e: [] for e in ENGS}
        self.seq = {e: 0 for e in ENGS}
        self.dcnt = {}
        self.marks = {e: set() for e in ENGS}
        self.sems = {}

    def _add(self, eng, fn, waits, dsem):
        ws = []
        for name, val in waits:
            if not val or val <= 0:
                continue
            ws.append((name, val))
            if name in ENGS:
                self.marks[name].add(val)
        self.seq[eng] += 1
        self.ops[eng].append((fn, tuple(ws), self.seq[eng], dsem))

    def op(self, eng, fn, waits=()):
        self._add(eng, fn, waits, None)
        return self.seq[eng]

    def dma(self, eng, fn, sem, waits=()):
        self.dcnt[sem] = self.dcnt.get(sem, 0) + 16
        self._add(eng, fn, waits, sem)
        return self.dcnt[sem]

    def v(self, eng):
        return self.seq[eng]

    def barrier(self):
        self.dcnt["bar"] = self.dcnt.get("bar", 0) + len(ENGS)
        target = self.dcnt["bar"]
        for e in ENGS:
            self.seq[e] += 1
            self.ops[e].append(("BARRIER", target, self.seq[e], None))

    def emit(self):
        nc = self.nc
        rank = {}
        for e in ENGS:
            rank[e] = {s: i + 1 for i, s in enumerate(sorted(self.marks[e]))}
        for e in ENGS:
            self.sems["q" + e] = self.ctx.enter_context(nc.semaphore("q" + e))
        for name in self.dcnt:
            self.sems[name] = self.ctx.enter_context(nc.semaphore(name))

        def mk(e):
            ops = self.ops[e]
            sems = self.sems
            marks = self.marks[e]

            def body(eng):
                for fn, waits, eseq, dsem in ops:
                    if fn == "BARRIER":
                        if e == "sync":
                            eng.sem_inc(sems["bar"], 1)
                        else:
                            eng.drain().then_inc(sems["bar"], 1)
                        eng.wait_ge(sems["bar"], waits)
                        continue
                    for wname, wval in waits:
                        if wname in ENGS:
                            eng.wait_ge(sems["q" + wname], rank[wname][wval])
                        else:
                            eng.wait_ge(sems[wname], wval)
                    inst = fn(eng)
                    if dsem is not None:
                        inst.then_inc(sems[dsem], 16)
                    if eseq in marks:
                        inst.then_inc(sems["q" + e], 1)
            return body

        with nc.Block() as blk:
            for e in ENGS:
                getattr(blk, e)(mk(e))



def build_program(stage=STAGE, debug=DEBUG):
    nc = bass.Bass("TRN2", target_bir_lowering=False)

    dram = {}

    def din(name, shape, dtype):
        dram[name] = nc.dram_tensor(name, shape, dtype, kind="ExternalInput").ap()
        return dram[name]

    def dout(name, shape, dtype):
        dram[name] = nc.dram_tensor(name, shape, dtype, kind="ExternalOutput").ap()
        return dram[name]

    xb = din("xb", [S, D], F32)
    din("wip", [24, 128, 4096], F32)
    din("wop", [4, 128, 8192], F32)
    din("gbp", [32, 128, 4096], BF16)
    din("ubp", [32, 128, 4096], BF16)
    din("dwp", [16, 128, 8192], BF16)
    din("cs", [128, S], F32)
    din("lnwc", [128, ND], F32)
    din("lnbc", [128, ND], F32)
    din("gscp", [128, NI], F32)
    din("uscp", [128, NI], F32)
    din("dscp", [4, 512], F32)
    din("pmat", [64, 64], BF16)
    din("idf", [128, 128], F32)

    dout("out", [NTO * 128, D], F32)

    if debug:
        dout("d_xn2", [128, ND * S], BF16)
        dout("d_xnfm", [128, ND * 1024], BF16)
        dout("d_rstd", [128, NT], F32)
        dout("d_nmr", [128, NT], F32)
        if stage >= 2:
            dout("d_q", [128, 2 * 1024], BF16)
            dout("d_k", [128, 2 * S], BF16)
            dout("d_v", [128, NT * 256], BF16)
            dout("d_bias", [128, 2], F32)
            dout("d_ctx", [128, NH * 1024], BF16)
        if stage >= 3:
            dout("d_acc", [128, NTO * D], F32)
        if stage >= 4:
            dout("d_acc2", [128, NTO * D], F32)

    with ExitStack() as top:
        prog = Prog(nc, top)
        L = lambda *a: top.enter_context(nc.sbuf_tensor(*a, side="left"))

        env = {"nc": nc, "prog": prog, "dram": dram, "debug": debug, "top": top,
               "stage": stage}

        env["lnw"] = L("lnw_sb", [128, ND], F32)
        env["lnb"] = L("lnb_sb", [128, ND], F32)
        env["gsc"] = L("gsc_sb", [128, NI], F32)
        env["usc"] = L("usc_sb", [128, NI], F32)
        env["pm"] = L("pm_sb", [64, 64], BF16)
        env["idf"] = L("idf_sb", [128, 128], F32)
        env["ones_cb"] = L("ones_cb", [128, 1], BF16)
        env["ones_bb"] = L("ones_bb", [128, 128], BF16)
        env["ones_rf"] = L("ones_rf", [1, 128], F32)
        env["n128"] = L("n128_r", [1, 512], BF16)
        env["eps_c"] = L("eps_c", [128, 1], F32)
        env["rstd"] = L("rstd_sb", [128, NT], F32)
        env["nmr"] = L("nmr_sb", [128, NT], F32)
        env["scr"] = L("scr_c", [128, 32], F32)
        env["scr1"] = L("scr_1", [1, 8], F32)
        env["xnfm"] = L("xnfm", [128, ND * 1024], BF16)

        with ExitStack() as sCtx:
          env["sCtx"] = sCtx
          with ExitStack() as sA:
            env["xn2"] = sA.enter_context(nc.sbuf_tensor("xn2f", [128, ND * S], BF16, side="left"))
            env["cs"] = sA.enter_context(nc.sbuf_tensor("cs_sb", [128, S], F32, side="left"))
            ldc = []
            for i, (name, dst) in enumerate([
                    ("lnwc", env["lnw"]), ("lnbc", env["lnb"]),
                    ("gscp", env["gsc"]), ("uscp", env["usc"]),
                    ("pmat", env["pm"]), ("idf", env["idf"]),
                    ("cs", env["cs"])]):
                prog.dma("sync", lambda e, d=dst, s=dram[name]: e.dma_start(out=d[:, :], in_=s[:, :]),
                         sem=f"ldc{i}")
                ldc.append((f"ldc{i}", 16))
            env["ldc"] = ldc
            for dst, valc in [(env["ones_cb"], 1.0), (env["ones_bb"], 1.0), (env["ones_rf"], 1.0),
                              (env["n128"], -128.0), (env["eps_c"], EPS)]:
                prog.op("vector", lambda e, d=dst, vv=valc: e.memset(d[:, :], vv))
            env["dvc_done"] = prog.v("vector")

            _emit_ln(env, xb)
            if debug:
                prog.barrier()
                prog.dma("sync", lambda e: e.dma_start(out=dram["d_xn2"][:, :], in_=env["xn2"][:, :]), sem="st")
                prog.dma("sync", lambda e: e.dma_start(out=dram["d_xnfm"][:, :], in_=env["xnfm"][:, :]), sem="st")
                prog.dma("sync", lambda e: e.dma_start(out=dram["d_rstd"][:, :], in_=env["rstd"][:, :]), sem="st")
                prog.dma("sync", lambda e: e.dma_start(out=dram["d_nmr"][:, :], in_=env["nmr"][:, :]), sem="st")

            if stage >= 2:
                _emit_attention(env)
                if stage == 2:
                    prog.barrier()
          if stage >= 3:
            env["acc"] = top.enter_context(nc.sbuf_tensor("out_acc", [128, NTO * D], F32, side="left"))
            _emit_outproj(env)
            prog.barrier()
        if stage >= 4:
            _emit_ffn(env, xb if stage >= 5 else None)
            prog.barrier()

        prog.barrier()
        st_total = prog.dcnt.get("st", 0)
        if st_total:
            prog.op("sync", lambda e: e.nop(), waits=[("st", st_total)])
        prog.emit()
    return nc



def _emit_ln(env, xb):
    nc, prog = env["nc"], env["prog"]
    scr, eps_c = env["scr"], env["eps_c"]
    rstd, nmr = env["rstd"], env["nmr"]
    xn2, xnfm = env["xn2"], env["xnfm"]
    lnw, lnb, idf = env["lnw"], env["lnb"], env["idf"]
    V, Sc, T = "vector", "scalar", "tensor"

    with ExitStack() as ctx:
        Ls = lambda *a: ctx.enter_context(nc.sbuf_tensor(*a, side="left"))
        NC = 4
        xt = [Ls(f"xt{i}", [128, D], F32) for i in range(NC)]
        xnt = [Ls(f"xnt{i}", [128, D], F32) for i in range(NC)]
        xn2t = [Ls(f"xn2t{i}", [128, D], F32) for i in range(NC)]
        pst = [ctx.enter_context(nc.psum_tensor(f"pst{i}", [128, 512], F32, side="right"))
               for i in range(4)]

        xt_free = [0] * NC
        src_free = [0] * NC
        slot_ev = [None] * 4
        col1_free = [0] * NC
        prev_d_last = [0] * NC
        prev_a_last = [0] * NC
        sq_prev = [0] * NC
        gp_last = [0] * NC
        env["x2ev"] = [0] * NT
        res = {}

        def chain(tt):
            sl = tt % NC
            cb = 8 * sl
            cc = [scr[:, cb + i:cb + i + 1] for i in range(8)]
            ldx = prog.dma("sync",
                           lambda e, t=tt, s=sl: e.dma_start(
                               out=xt[s][:, :], in_=xb[t * 128:(t + 1) * 128, :]),
                           sem=f"ldx{sl}", waits=[(Sc, xt_free[sl]), (V, prev_d_last[sl]),
                                                  ("gpsimd", gp_last[sl])])
            d1 = prog.op(V, lambda e, s=sl, c=cc: e.reduce_sum(c[0], xt[s][:, :], axis=AX.X),
                         waits=[(f"ldx{sl}", ldx), (V, prev_d_last[sl]), (Sc, prev_a_last[sl]),
                                ("gpsimd", gp_last[sl])]
                               + (env["ldc"] if tt == 0 else []))
            a1 = prog.op(Sc, lambda e, s=sl, c=cc: e.activation(xnt[s][:, :], xt[s][:, :],
                                                                AF.Square, accum_out=c[1]),
                         waits=[(f"ldx{sl}", ldx), (V, col1_free[sl]), (Sc, max(sq_prev)),
                                (T, src_free[sl])])
            sq_prev[sl] = a1
            yield
            d2 = prog.op(V, lambda e, c=cc: e.tensor_scalar(c[2], c[0], INV_D, None, ALU.mult),
                         waits=[(V, d1)])
            d3 = prog.op(V, lambda e, c=cc: e.tensor_scalar(c[3], c[1], INV_D, None, ALU.mult),
                         waits=[(Sc, a1), (V, d2)])
            col1_free[sl] = d3
            d4 = prog.op(V, lambda e, c=cc: e.tensor_scalar(c[4], c[2], c[2], None, ALU.mult),
                         waits=[(V, d3)])
            d5 = prog.op(V, lambda e, c=cc: e.tensor_sub(c[3], c[3], c[4]), waits=[(V, d4)])
            yield
            a2 = prog.op(Sc, lambda e, c=cc: e.activation(c[5], c[3], AF.Ln, bias=eps_c[:, :]),
                         waits=[(V, d5)])
            a3 = prog.op(Sc, lambda e, t=tt, c=cc: e.activation(rstd[:, t:t + 1], c[5],
                                                                AF.Exp, scale=-0.5),
                         waits=[(Sc, a2)])
            yield
            d6 = prog.op(V, lambda e, t=tt, c=cc: e.tensor_scalar(nmr[:, t:t + 1], c[2],
                                                                  rstd[:, t:t + 1], -1.0,
                                                                  ALU.mult, ALU.mult),
                         waits=[(Sc, a3)])
            d6b = prog.op("gpsimd", lambda e, t=tt, s=sl: e.tensor_scalar(
                              xnt[s][:, :], xt[s][:, :], rstd[:, t:t + 1],
                              nmr[:, t:t + 1], ALU.mult, ALU.add),
                          waits=[(V, d6), (T, src_free[sl]), (Sc, a1)])
            yield
            a5 = prog.op(Sc, lambda e, s=sl, c=cc: e.activation(xt[s][:, :], xnt[s][:, :],
                                                                AF.Square, accum_out=c[1]),
                         waits=[(V, col1_free[sl]), ("gpsimd", d6b), (Sc, max(sq_prev))])
            sq_prev[sl] = a5
            yield
            d9 = prog.op(V, lambda e, c=cc: e.tensor_scalar(c[3], c[1], INV_D, None, ALU.mult),
                         waits=[(Sc, a5)])
            col1_free[sl] = d9
            yield
            a6 = prog.op(Sc, lambda e, c=cc: e.activation(c[5], c[3], AF.Ln, bias=eps_c[:, :]),
                         waits=[(V, d9)])
            a7 = prog.op(Sc, lambda e, c=cc: e.activation(c[6], c[5], AF.Exp, scale=-0.5),
                         waits=[(Sc, a6)])
            yield
            a8 = prog.op("gpsimd", lambda e, s=sl, c=cc: e.tensor_scalar(
                             xn2t[s][:, :], xnt[s][:, :], c[6], 0.0,
                             ALU.mult, ALU.add),
                         waits=[(Sc, a7), (T, src_free[sl])])
            prev_d_last[sl], prev_a_last[sl] = d9, a7
            xt_free[sl] = a5
            gp_last[sl] = a8
            res[tt] = (sl, a8, d6b)

        def transposes(tt):
            sl, a8, d6b = res[tt]
            x2ev_pair = [0, 0]
            work = [("x2", dc) for dc in range(ND)]
            if tt < NTO:
                work += [("xn", dc) for dc in range(ND)]
            last_t = 0
            for j, (kind, dc) in enumerate(work):
                pslot = j % 4
                p = pst[pslot]
                srcbuf = xn2t[sl] if kind == "x2" else xnt[sl]
                wts = [("gpsimd", a8)] if kind == "x2" else [("gpsimd", d6b)]
                if slot_ev[pslot] is not None:
                    wts.append(slot_ev[pslot])
                mm_t = prog.op(T, lambda e, sb=srcbuf, d=dc, pp=p: e.transpose(
                                   pp[:, 0:128], sb[:, d * 128:(d + 1) * 128], idf[:, :]),
                               waits=wts)
                last_t = mm_t
                if kind == "x2":
                    if dc % 2 == 0:
                        ev = prog.op(V, lambda e, t=tt, d=dc, pp=p: e.tensor_scalar(
                                         xn2[:, d * S + t * 128: d * S + (t + 1) * 128],
                                         pp[:, 0:128], lnw[:, d:d + 1], lnb[:, d:d + 1],
                                         ALU.mult, ALU.add),
                                     waits=[(T, mm_t)])
                        slot_ev[pslot] = (V, ev)
                    else:
                        ev = prog.op(Sc, lambda e, t=tt, d=dc, pp=p: e.activation(
                                         xn2[:, d * S + t * 128: d * S + (t + 1) * 128],
                                         pp[:, 0:128], AF.Identity,
                                         bias=lnb[:, d:d + 1], scale=lnw[:, d:d + 1]),
                                     waits=[(T, mm_t)])
                        slot_ev[pslot] = (Sc, ev)
                    x2ev_pair[dc % 2] = ev
                else:
                    ev = prog.op(V, lambda e, t=tt, d=dc, pp=p: e.tensor_copy(
                                     xnfm[:, d * 1024 + t * 128: d * 1024 + (t + 1) * 128],
                                     pp[:, 0:128]),
                                 waits=[(T, mm_t)])
                    slot_ev[pslot] = (V, ev)
            env["x2ev"][tt] = [(V, x2ev_pair[0]), (Sc, x2ev_pair[1])]
            src_free[sl] = last_t

        for k in range(NT // 4):
            alive = [chain(4 * k + i) for i in range(4)]
            while alive:
                for gch in list(alive):
                    try:
                        next(gch)
                    except StopIteration:
                        alive.remove(gch)
            for i in range(4):
                transposes(4 * k + i)
        env["p1T"] = prog.v("tensor")


def _emit_attention(env):
    nc, prog, dram = env["nc"], env["prog"], env["dram"]
    xn2f = env["xn2"]
    cs, pm = env["cs"], env["pm"]
    ones_cb, ones_rf, scr1 = env["ones_cb"], env["ones_rf"], env["scr1"]
    ones_bb = env["ones_bb"]
    debug = env["debug"]
    V, Sc, T = "vector", "scalar", "tensor"
    wip = dram["wip"]

    env["ctx"] = env["sCtx"].enter_context(
        nc.sbuf_tensor("ctxf", [128, NH * 1024], BF16, side="right"))
    ctxf = env["ctx"]
    with ExitStack() as ctx:
        R = lambda *a: ctx.enter_context(nc.sbuf_tensor(*a, side="right"))
        wf32 = [R(f"wf32_{i}", [128, 512], F32) for i in range(2)]
        wbf = [R(f"wbf{i}", [128, 4096], BF16) for i in range(3)]
        k2 = R("k2", [128, 2 * S], BF16)
        q2 = R("q2", [128, 2 * 1024], BF16)
        v2 = R("v2", [128, NT * 256], BF16)
        ebig = ctx.enter_context(nc.sbuf_tensor("ebig", [128, 8 * 512], BF16, side="left"))
        sqb = R("sqb", [128, 1024], BF16)
        rb = R("rb", [128, 512], F32)
        biasc = R("biasc", [128, 2], F32)
        pp = [ctx.enter_context(nc.psum_tensor(f"pp{i}", [128, 512], F32))
              for i in range(2)]
        ppR = ctx.enter_context(nc.psum_tensor("ppR", [128, 512], F32))
        ppS = ctx.enter_context(nc.psum_tensor("ppS", [128, 512], F32))
        ppD = [ctx.enter_context(nc.psum_tensor(f"ppD{i}", [128, 512], F32))
               for i in range(2)]
        ppC = [ctx.enter_context(nc.psum_tensor(f"ppC{i}", [128, 512], F32))
               for i in range(2)]

        wf32_free = [0, 0]
        wbf_mm_last = [0, 0, 0]
        slab_ready = {}
        pp_free = [[], [], [], []]
        ppR_free = []
        bankF = {"S": [], "D0": [], "D1": []}
        ppC_free = [[], []]
        e_free = 0
        cv_prev = 0
        tcos_free = 0
        sq_free = []
        k2_rd = []
        v2_rd = []
        tile_i = 0

        def load_slab(j):
            for qt in range(8):
                qp = (j * 8 + qt) % 2
                w_extra = [("tensor", env["p1T"])] if (j < 3) else []
                ld = prog.dma("sync",
                              lambda e, jj=j, h=qt, q=qp: e.dma_start(
                                  out=wf32[q][:, :], in_=wip[jj, :, h * 512:(h + 1) * 512]),
                              sem=f"ldw{qp}", waits=[("gpsimd", wf32_free[qp])] + w_extra)
                wts = [(f"ldw{qp}", ld)]
                if qt == 0:
                    wts.append((T, wbf_mm_last[j % 3]))
                cv = prog.op("gpsimd", lambda e, jj=j, h=qt, q=qp: e.tensor_copy(
                                 wbf[jj % 3][:, h * 512:(h + 1) * 512], wf32[q][:, :]),
                             waits=wts)
                wf32_free[qp] = cv
            slab_ready[j] = cv

        load_slab(0)
        load_slab(1)
        load_slab(2)

        pp4 = [pp[0], pp[1], ppC[0], ppC[1]]
        for g in range(8):

            rope_last = 0
            vproj_last = 0
            rope_done = {}
            qk_jobs = [("q", hh, tg) for hh in range(2) for tg in range(2)] + \
                      [("k", hh, tg) for hh in range(2) for tg in range(4)]
            pending_rope = None
            for kind, hh, tg in qk_jobs:
                slabj = 3 * g + (0 if kind == "q" else 1)
                dst = q2 if kind == "q" else k2
                dstoff = hh * (1024 if kind == "q" else 2048) + tg * 512
                sl = tile_i % 4
                w0 = [("gpsimd", slab_ready[slabj])] + pp_free[sl] + k2_rd
                if g == 0:
                    w0 += env["x2ev"][4 * tg + 3]
                if g == 0 and tile_i in (2, 3):
                    w0 += env["x2ev"][14] + env["x2ev"][15]
                k2_rd = []
                m_last = None
                for dc in range(ND):
                    m_last = prog.op(T, lambda e, d=dc, h=hh, t=tg, s=sl, sj=slabj: e.matmul(
                                         pp4[s][:, :],
                                         wbf[sj % 3][:, d * 256 + h * 128: d * 256 + (h + 1) * 128],
                                         xn2f[:, d * S + t * 512: d * S + t * 512 + 512],
                                         start=(d == 0), stop=(d == ND - 1)),
                                     waits=(w0 if dc == 0 else ()))
                wbf_mm_last[slabj % 3] = m_last
                aev = prog.op(Sc, lambda e, s=sl, do=dstoff, dd=dst: e.activation(
                                  dd[:, do:do + 512], pp4[s][:, :], AF.Copy),
                              waits=[(T, m_last)])
                if pending_rope is not None:
                    pending_rope()
                if P2_SUB == "mm":
                    pending_rope = None
                    pp_free[sl] = [(Sc, aev)]
                    tile_i += 1
                    continue

                def rope(m_last=m_last, aev=aev, sl=sl, tg=tg, dst=dst, dstoff=dstoff,
                         key=(kind, hh, tg)):
                    nonlocal tcos_free, ppR_free, rope_last
                    mB = prog.op(T, lambda e, do=dstoff, dd=dst: e.matmul(
                                     ppR[0:64, :], pm[:, :], dd[0:64, do:do + 512],
                                     start=True, stop=True),
                                 waits=[(Sc, aev)] + ppR_free)
                    dco = prog.op(V, lambda e, t=tg, do=dstoff, dd=dst: e.tensor_mul(
                                      dd[0:64, do:do + 512], dd[0:64, do:do + 512],
                                      cs[0:64, t * 512:t * 512 + 512]),
                                  waits=[(T, mB)])
                    dt2 = prog.op(V, lambda e, t=tg: e.tensor_mul(
                                      rb[0:64, :], ppR[0:64, :], cs[64:128, t * 512:t * 512 + 512]),
                                  waits=[(T, mB), (V, tcos_free)])
                    dadd = prog.op(V, lambda e, do=dstoff, dd=dst: e.tensor_add(
                                       dd[0:64, do:do + 512], dd[0:64, do:do + 512], rb[0:64, :]),
                                   waits=[(V, dt2)])
                    rope_last = dadd
                    rope_done[key] = dadd
                    tcos_free = dadd
                    ppR_free = [(V, dt2)]
                    pp_free[sl] = [(Sc, aev)]

                pp_free[sl] = [(Sc, aev)]
                pending_rope = rope
                tile_i += 1
            slabv = 3 * g + 2
            bias_ready = [0, 0]
            pb = [ppS, ppD[0], ppD[1]]
            pbk = ["S", "D0", "D1"]
            bsteps = []
            if P2_SUB != "qkv":
                for hh in range(2):
                    def mk_head(hh=hh):
                        state = {}

                        def s_sq(kind, half):
                            def f():
                                if kind == "q":
                                    state["gate"] = prog.op("gpsimd", lambda e, h=hh: e.tensor_mul(
                                        sqb[:, 0:1024], q2[:, h * 1024:(h + 1) * 1024],
                                        q2[:, h * 1024:(h + 1) * 1024]),
                                        waits=[(V, rope_done[("q", hh, 1)])] + sq_free)
                                else:
                                    state["gate"] = prog.op("gpsimd", lambda e, h=hh, q=half: e.tensor_mul(
                                        sqb[:, :],
                                        k2[:, h * 2048 + q * 1024: h * 2048 + q * 1024 + 1024],
                                        k2[:, h * 2048 + q * 1024: h * 2048 + q * 1024 + 1024]),
                                        waits=[(V, rope_done[("k", hh, 2 * half + 1)]),
                                               (T, state["mb"])])
                            return f

                        def s_mm(j):
                            def f():
                                bank = j % 3
                                mb = prog.op(T, lambda e, b=j % 2, k=bank: e.matmul(
                                                 pb[k][:, :], ones_bb[:, :],
                                                 sqb[:, b * 512:(b + 1) * 512],
                                                 start=True, stop=True),
                                             waits=[("gpsimd", state["gate"])] + bankF[pbk[bank]])
                                state["mb"] = mb
                                rd = prog.op(V, lambda e, j_=j, k=bank: e.reduce_max(
                                                 scr1[0:1, j_:j_ + 1], pb[k][0:1, :], axis=AX.X),
                                             waits=[(T, mb)])
                                bankF[pbk[bank]] = [(V, rd)]
                                state["rd"] = rd
                            return f

                        def s_fin():
                            nonlocal_state = state

                            def f():
                                rd_last = nonlocal_state["rd"]
                                c1 = prog.op(V, lambda e: e.tensor_max(scr1[0:1, 6:7], scr1[0:1, 0:1],
                                                                       scr1[0:1, 1:2]), waits=[(V, rd_last)])
                                c2 = prog.op(V, lambda e: e.tensor_max(scr1[0:1, 7:8], scr1[0:1, 2:3],
                                                                       scr1[0:1, 3:4]), waits=[(V, c1)])
                                c3 = prog.op(V, lambda e: e.tensor_max(scr1[0:1, 7:8], scr1[0:1, 7:8],
                                                                       scr1[0:1, 4:5]), waits=[(V, c2)])
                                c4 = prog.op(V, lambda e: e.tensor_max(scr1[0:1, 7:8], scr1[0:1, 7:8],
                                                                       scr1[0:1, 5:6]), waits=[(V, c3)])
                                c5 = prog.op(V, lambda e: e.tensor_add(scr1[0:1, 6:7], scr1[0:1, 6:7],
                                                                       scr1[0:1, 7:8]), waits=[(V, c4)])
                                c6 = prog.op(V, lambda e: e.tensor_scalar(scr1[0:1, 7:8], scr1[0:1, 6:7],
                                                                          -0.5 * SCALE, None, ALU.mult),
                                             waits=[(V, c5)])
                                mbb = prog.op(T, lambda e: e.matmul(ppS[:, 0:1], ones_rf[:, :],
                                                                    scr1[0:1, 7:8], start=True, stop=True),
                                              waits=[(V, c6)] + bankF["S"])
                                bias_ready[hh] = prog.op(V, lambda e, h=hh: e.tensor_copy(
                                                             biasc[:, h:h + 1], ppS[:, 0:1]),
                                                         waits=[(T, mbb)])
                                bankF["S"] = [(V, bias_ready[hh])]
                            return f

                        steps = [s_sq("q", 0), s_mm(0), s_mm(1), s_sq("k", 0), s_mm(2),
                                 s_mm(3), s_sq("k", 1), s_mm(4), s_mm(5), s_fin()]

                        def close_sqfree():
                            nonlocal sq_free
                            sq_free = [(T, state["mb"])]
                        steps.append(close_sqfree)
                        return steps
                    bsteps += mk_head()

            bi = 0
            for tb in range(NT):
                sl = tile_i % 4
                w0 = [("gpsimd", slab_ready[slabv])] + pp_free[sl] + v2_rd
                if g == 0:
                    w0 += env["x2ev"][tb]
                v2_rd = []
                m_last = None
                for dc in range(ND):
                    m_last = prog.op(T, lambda e, d=dc, t=tb, s=sl, sj=slabv: e.matmul(
                                         pp4[s][:, 0:256],
                                         xn2f[:, d * S + t * 128: d * S + t * 128 + 128],
                                         wbf[sj % 3][:, d * 256: d * 256 + 256],
                                         start=(d == 0), stop=(d == ND - 1)),
                                     waits=(w0 if dc == 0 else ()))
                wbf_mm_last[slabv % 3] = m_last
                aev = prog.op(Sc, lambda e, s=sl, t=tb: e.activation(
                                  v2[:, t * 256:(t + 1) * 256], pp4[s][:, 0:256], AF.Copy),
                              waits=[(T, m_last)])
                vproj_last = aev
                if pending_rope is not None:
                    pending_rope()
                    pending_rope = None
                pp_free[sl] = [(Sc, aev)]
                tile_i += 1
                if tb >= 2:
                    for _ in range(2):
                        if bi < len(bsteps):
                            bsteps[bi]()
                            bi += 1
            while bi < len(bsteps):
                bsteps[bi]()
                bi += 1
            for j in range(3 * g + 3, min(3 * g + 6, 24)):
                load_slab(j)

            pps = [pp[0], pp[1], ppR]
            pp3_free = [pp_free[0], pp_free[1], ppR_free]
            sc3 = 0
            for hh in (range(2) if P2_SUB == "all" else ()):
                for qg in range(2):
                    un = (g * 4 + hh * 2 + qg) % 2
                    dkey = "D0" if un == 0 else "D1"
                    ms_seq = [0] * 16
                    ex_seq = [0] * 16
                    md_seq = [0] * 16
                    for kb in range(16):
                        sl = sc3 % 3
                        sc3 += 1
                        wts = list(pp3_free[sl])
                        if kb == 0:
                            wts += [(V, rope_last), (Sc, vproj_last)]
                        ms_seq[kb] = prog.op(T, lambda e, h=hh, q=qg, k=kb, s=sl: e.matmul(
                                                 pps[s][:, :],
                                                 k2[:, h * 2048 + k * 128: h * 2048 + (k + 1) * 128],
                                                 q2[:, h * 1024 + q * 512: h * 1024 + q * 512 + 512],
                                                 start=True, stop=True),
                                             waits=wts)
                        wts = [(T, ms_seq[kb]), (V, bias_ready[hh])]
                        if kb == 0:
                            wts.append((T, e_free))
                        if kb >= 8:
                            wts.append((T, md_seq[kb - 8]))
                        ex_seq[kb] = prog.op(Sc, lambda e, k=kb, s=sl, h=hh: e.activation(
                                                 ebig[:, (k % 8) * 512:(k % 8) * 512 + 512],
                                                 pps[s][:, :],
                                                 AF.Exp, bias=biasc[:, h:h + 1], scale=SCALE),
                                             waits=wts)
                        pp3_free[sl] = [(Sc, ex_seq[kb])]
                        if kb >= 2:
                            prog.op(T, lambda e, k=kb - 2, h=hh, u=un: e.matmul(
                                        ppC[u][:, :],
                                        v2[:, k * 256 + h * 128: k * 256 + (h + 1) * 128],
                                        ebig[:, (k % 8) * 512:(k % 8) * 512 + 512],
                                        start=(k == 0), stop=False),
                                    waits=[(Sc, ex_seq[kb - 2])] +
                                          (ppC_free[un] + pp_free[2 + un]
                                           if kb == 2 else []))
                            md_seq[kb - 2] = prog.op(T, lambda e, k=kb - 2, u=un: e.matmul(
                                        ppD[u][:, :], ones_bb[:, :],
                                        ebig[:, (k % 8) * 512:(k % 8) * 512 + 512],
                                        start=(k == 0), stop=False),
                                    waits=(bankF[dkey] if kb == 2 else ()))
                    prog.op(T, lambda e, h=hh, u=un: e.matmul(
                                ppC[u][:, :],
                                v2[:, 14 * 256 + h * 128: 14 * 256 + (h + 1) * 128],
                                ebig[:, 6 * 512:7 * 512],
                                start=False, stop=False),
                            waits=[(Sc, ex_seq[14])])
                    md_seq[14] = prog.op(T, lambda e, u=un: e.matmul(
                                ppD[u][:, :], ones_bb[:, :],
                                ebig[:, 6 * 512:7 * 512], start=False, stop=False))
                    mp_last = prog.op(T, lambda e, h=hh, u=un: e.matmul(
                                          ppC[u][:, :],
                                          v2[:, 15 * 256 + h * 128: 15 * 256 + (h + 1) * 128],
                                          ebig[:, 7 * 512:8 * 512],
                                          start=False, stop=True),
                                      waits=[(Sc, ex_seq[15])])
                    md_last = prog.op(T, lambda e, u=un: e.matmul(
                                ppD[u][:, :], ones_bb[:, :],
                                ebig[:, 7 * 512:8 * 512], start=False, stop=True))
                    md_seq[15] = md_last
                    e_free = md_last
                    rcb = prog.op(V, lambda e, u=un: e.reciprocal(rb[:, :], ppD[u][:, :]),
                                  waits=[(T, md_last), (V, cv_prev), (V, rope_last)])
                    bankF[dkey] = [(V, rcb)]
                    cv = prog.op(V, lambda e, h=hh, q=qg, gg=g, u=un: e.tensor_mul(
                                     ctxf[:, (2 * gg + h) * 1024 + q * 512:
                                          (2 * gg + h) * 1024 + q * 512 + 512],
                                     ppC[u][:, :], rb[:, :]),
                                 waits=[(V, rcb), (T, mp_last)])
                    cv_prev = cv
                    ppC_free[un] = [(V, cv)]
                    k2_rd = [(T, mp_last), (Sc, ex_seq[15])]
                    v2_rd = [(T, mp_last)]
            pp_free[0] = pp3_free[0]
            pp_free[1] = pp3_free[1]
            pp_free[2] = ppC_free[0]
            pp_free[3] = ppC_free[1]
            ppR_free = pp3_free[2]
            k2_rd = k2_rd[-2:]
            v2_rd = v2_rd[-1:]
            if P2_SUB == "all":
                tcos_free = cv_prev

            if debug and g == 0:
                stv = prog.dma("sync", lambda e: e.dma_start(out=dram["d_q"][:, :], in_=q2[:, :]),
                               sem="st", waits=[(V, rope_last)])
                stv = prog.dma("sync", lambda e: e.dma_start(out=dram["d_k"][:, :], in_=k2[:, :]),
                               sem="st", waits=[(V, rope_last)])
                stv = prog.dma("sync", lambda e: e.dma_start(out=dram["d_v"][:, :], in_=v2[:, :]),
                               sem="st", waits=[(Sc, vproj_last)])
                if P2_SUB != "qkv":
                    stv = prog.dma("sync", lambda e: e.dma_start(out=dram["d_bias"][:, :], in_=biasc[:, :]),
                                   sem="st", waits=[(V, bias_ready[1])])
                sn = prog.op("sync", lambda e: e.nop(), waits=[("st", stv)])
                k2_rd.append(("sync", sn))
                v2_rd.append(("sync", sn))
                bankF["S"].append(("sync", sn))

        if debug and P2_SUB == "all":
            stv = 0
            for blk in range(32):
                stv = prog.dma("sync", lambda e, bb=blk: e.dma_start(
                                   out=dram["d_ctx"][:, bb * 512:(bb + 1) * 512],
                                   in_=ctxf[:, bb * 512:(bb + 1) * 512]),
                               sem="st", waits=ppC_free[0] + ppC_free[1])
            prog.op("sync", lambda e: e.nop(), waits=[("st", stv)])
        env["p2T"] = prog.v("tensor")
        env["p2V"] = prog.v("vector")
        env["p2Sc"] = prog.v("scalar")
        env["p2G"] = prog.v("gpsimd")


def _emit_outproj(env):
    nc, prog, dram = env["nc"], env["prog"], env["dram"]
    ctxf, acc = env["ctx"], env["acc"]
    debug = env["debug"]
    V, Sc, T = "vector", "scalar", "tensor"
    wop = dram["wop"]

    with ExitStack() as ctx:
        Ls = lambda *a: ctx.enter_context(nc.sbuf_tensor(*a, side="left"))
        wof = [Ls(f"wof{i}", [128, 4096], F32) for i in range(2)]
        wob = [Ls(f"wob{i}", [128, 8192], BF16) for i in range(2)]
        po = [ctx.enter_context(nc.psum_tensor(f"po{i}", [128, 512], F32))
              for i in range(2)]
        wof_free = [0, 0]
        wob_mm_last = [0, 0]
        po_free = [[(Sc, env["p2Sc"])], [(Sc, env["p2Sc"])]]
        ev_last = 0
        slab_casts = {}

        def load_wo(fg):
            wp = fg % 2
            for half in range(2):
                wx = [(V, wof_free[wp])]
                if fg < 2 and half == 0:
                    wx += [(T, env["p2T"]), (V, env["p2V"]),
                           (Sc, env["p2Sc"]), ("gpsimd", env["p2G"])]
                ld = prog.dma("sync",
                              lambda e, f=fg, h=half, q=wp: e.dma_start(
                                  out=wof[q][:, :], in_=wop[f, :, h * 4096:(h + 1) * 4096]),
                              sem=f"ldo{wp}", waits=wx)
                wts = [(f"ldo{wp}", ld)]
                if half == 0:
                    wts.append((T, wob_mm_last[wp]))
                cv = prog.op(V, lambda e, h=half, q=wp: e.tensor_copy(
                                 wob[q][:, h * 4096:(h + 1) * 4096], wof[q][:, :]),
                             waits=wts)
                wof_free[wp] = cv
            slab_casts[fg] = wof_free[wp]

        load_wo(0)
        load_wo(1)
        for fg in range(4):
            wp = fg % 2
            casts = slab_casts[fg]
            for tb in range(NTO):
                sl = tb % 2
                w0 = [(V, casts)] + po_free[sl]
                m_last = None
                for h in range(NH):
                    m_last = prog.op(T, lambda e, hh=h, t=tb, s=sl, q=wp: e.matmul(
                                         po[s][:, :],
                                         ctxf[:, hh * 1024 + t * 128: hh * 1024 + (t + 1) * 128],
                                         wob[q][:, hh * 512:(hh + 1) * 512],
                                         start=(hh == 0), stop=(hh == NH - 1)),
                                     waits=(w0 if h == 0 else ()))
                ev = prog.op(Sc, lambda e, t=tb, f=fg, s=sl: e.activation(
                                 acc[:, t * D + f * 512: t * D + f * 512 + 512],
                                 po[s][:, :], AF.Copy),
                             waits=[(T, m_last)] +
                                   ([(V, env["p2V"]), ("gpsimd", env["p2G"])]
                                    if (fg == 0 and tb == 0) else []))
                po_free[sl] = [(Sc, ev)]
                ev_last = ev
            wob_mm_last[wp] = m_last
            if fg + 2 < 4:
                load_wo(fg + 2)
        if debug:
            stv = prog.dma("sync", lambda e: e.dma_start(out=dram["d_acc"][:, :], in_=acc[:, :]),
                           sem="st", waits=[(Sc, ev_last)])
            prog.op("sync", lambda e: e.nop(), waits=[("st", stv)])


def _emit_ffn(env, fxb=None):
    nc, prog, dram = env["nc"], env["prog"], env["dram"]
    xnfm, acc = env["xnfm"], env["acc"]
    do_final = fxb is not None
    rstd, nmr = env["rstd"], env["nmr"]
    gsc, usc = env["gsc"], env["usc"]
    ones_cb, ones_rf, n128 = env["ones_cb"], env["ones_rf"], env["n128"]
    ones_bb = env["ones_bb"]
    debug = env["debug"]
    V, Sc, T, G = "vector", "scalar", "tensor", "gpsimd"
    gbp, ubp, dwp, dscp = dram["gbp"], dram["ubp"], dram["dwp"], dram["dscp"]

    with ExitStack() as ctx:
        Ls = lambda *a: ctx.enter_context(nc.sbuf_tensor(*a, side="left"))
        sgp = Ls("sgp", [128, 512], F32)
        sup = Ls("sup", [128, 512], F32)
        dscb = Ls("dscb", [128, 2048], BF16)
        hsr = Ls("hsr", [1, 512], BF16)
        if do_final:
            xst = Ls("xst0", [128, 512], F32)
            sst = ctx.enter_context(nc.psum_tensor("pfs", [128, 512], F32))
            r1p = Ls("r1p", [128, 16], F32)
        hfm = Ls("hfm", [128, NI * 512], BF16)
        pg = [ctx.enter_context(nc.psum_tensor(f"pg{i}", [128, 512], F32))
              for i in range(2)]
        pu = [ctx.enter_context(nc.psum_tensor(f"pu{i}", [128, 512], F32))
              for i in range(2)]
        phs = ctx.enter_context(nc.psum_tensor("phs", [128, 512], F32))
        ppB = ctx.enter_context(nc.psum_tensor("ppB", [128, 512], F32))
        pd = [pg[0], pg[1], pu[0], pu[1]]

        mprev = 0
        cprev = 0
        with ExitStack() as c0:
            dsrow = c0.enter_context(nc.sbuf_tensor("dsrow", [1, 512], F32, side="left"))
            for piece in range(4):
                ld = prog.dma("sync", lambda e, p=piece: e.dma_start(
                                  out=dsrow[:, :], in_=dscp[p:p + 1, :]),
                              sem="ldd", waits=[(T, mprev)])
                mprev = prog.op(T, lambda e: e.matmul(ppB[:, :], ones_rf[:, :],
                                                      dsrow[0:1, :], start=True, stop=True),
                                waits=[("ldd", ld), (V, cprev)])
                cprev = prog.op(V, lambda e, p=piece: e.tensor_copy(
                                    dscb[:, p * 512:(p + 1) * 512], ppB[:, :]),
                                waits=[(T, mprev)])
        dscb_ready = cprev
        ax_prev = 0
        ad_prev = 0
        ga_tab = [0] * 4
        if do_final:
            r1p_seq = prog.op(V, lambda e: e.tensor_scalar(r1p[:, :], rstd[:, :],
                                                           1.0, None, ALU.add))
            outp = dram["out"]
        wstage = Ls("wstage", [128, 16384], BF16)
        gb2 = [wstage[:, p * 4096:(p + 1) * 4096] for p in range(2)]
        ub2 = [wstage[:, 8192 + p * 4096: 8192 + (p + 1) * 4096] for p in range(2)]
        dw2 = [wstage[:, q * 8192:(q + 1) * 8192] for q in range(2)]

        sg_free = 0
        h_free = 0
        acc_g = 0
        he_prev = 0
        hs_prev = 0
        gb_mm_last = [0, 0]
        dw_mm_last = [mprev, mprev]
        pg_free = [[], []]
        pu_free = [[], []]
        pd_free = [[], [], [], []]
        phs_free = []
        tmp_free = 0
        for tg in range(2):
            pending_hs = []
            gb_ready = [0, 0]
            ub_ready = [0, 0]
            gh = 0
            for isc in range(32):
                par = isc % 2
                gb_ready[par] = prog.dma("sync",
                                         lambda e, i=isc, p=par: e.dma_start(
                                             out=gb2[p], in_=gbp[i, :, :]),
                                         sem=f"ldg{par}",
                                         waits=[(T, gb_mm_last[par]),
                                                (T, dw_mm_last[0])])
                ub_ready[par] = prog.dma("sync",
                                         lambda e, i=isc, p=par: e.dma_start(
                                             out=ub2[p], in_=ubp[i, :, :]),
                                         sem=f"ldu{par}",
                                         waits=[(T, gb_mm_last[par]),
                                                (T, dw_mm_last[1])])
                for ic2 in range(2):
                    ic = isc * 2 + ic2
                    psl = ic % 2
                    m = None
                    for hc in range(ND):
                        wts = []
                        if hc == 0:
                            wts = [(f"ldg{par}", gb_ready[par])] + pg_free[psl]
                        m = prog.op(T, lambda e, h=hc, i2=ic2, p=par, t=tg, s=psl: e.matmul(
                                        pg[s][:, :],
                                        gb2[p][:, h * 256 + i2 * 128: h * 256 + i2 * 128 + 128],
                                        xnfm[:, h * 1024 + t * 512: h * 1024 + t * 512 + 512],
                                        start=(h == 0), stop=(h == ND - 1)),
                                    waits=wts)
                    mg = m
                    if len(pending_hs) > 1:
                        pending_hs.pop(0)()
                    a1 = prog.op(Sc, lambda e, i=ic, s=psl: e.activation(
                                     sgp[:, :], pg[s][:, :], AF.Silu,
                                     scale=gsc[:, i:i + 1]),
                                 waits=[(T, mg), (G, sg_free)])
                    m = None
                    for hc in range(ND):
                        wts = []
                        if hc == 0:
                            wts = [(f"ldu{par}", ub_ready[par])] + pu_free[psl]
                        m = prog.op(T, lambda e, h=hc, i2=ic2, p=par, t=tg, s=psl: e.matmul(
                                        pu[s][:, :],
                                        ub2[p][:, h * 256 + i2 * 128: h * 256 + i2 * 128 + 128],
                                        xnfm[:, h * 1024 + t * 512: h * 1024 + t * 512 + 512],
                                        start=(h == 0), stop=(h == ND - 1)),
                                    waits=wts)
                    mu_ = m
                    gb_mm_last[par] = mu_
                    a2 = prog.op(Sc, lambda e, i=ic, s=psl: e.activation(
                                     sup[:, :], pu[s][:, :], AF.Identity,
                                     scale=usc[:, i:i + 1]),
                                 waits=[(T, mu_), (G, sg_free)])
                    gh = prog.op(G, lambda e, i=ic: e.tensor_mul(
                                     hfm[:, i * 512:(i + 1) * 512], sgp[:, :], sup[:, :]),
                                 waits=[(Sc, a2), (Sc, a1), (T, h_free)])
                    sg_free = gh
                    pg_free[psl] = [(Sc, a1)]
                    pu_free[psl] = [(Sc, a2)]

                    def hsum(i=ic, gh=gh):
                        nonlocal hs_prev
                        hs_prev = prog.op(T, lambda e, i2=i: e.matmul(
                                              phs[:, :], ones_bb[:, :],
                                              hfm[:, i2 * 512:(i2 + 1) * 512],
                                              start=(i2 == 0), stop=(i2 == NI - 1)),
                                          waits=([(V, he_prev)] + phs_free
                                                 if i == 0 else []) + [(G, gh)])
                    pending_hs.append(hsum)
            for f in pending_hs:
                f()
            pending_hs.clear()
            phs_free = []
            he = prog.op(V, lambda e: e.tensor_copy(hsr[:, :], phs[0:1, :]),
                         waits=[(T, hs_prev)])
            he_prev = he
            vbar = prog.v(V)
            scbar = prog.v(Sc)

            for fg in range(4):
                stops = [None] * 4
                for pc in range(4):
                    par = pc % 2
                    lw = prog.dma("sync", lambda e, f=fg, p=pc, q=par: e.dma_start(
                                      out=dw2[q], in_=dwp[f * 4 + p, :, :]),
                                  sem=f"lddw{par}",
                                  waits=[(T, dw_mm_last[par]),
                                         (T, gb_mm_last[0]), (T, gb_mm_last[1])])
                    for tb in range(4):
                        for icp in range(16):
                            wts = []
                            if icp == 0 and tb == 0:
                                wts = [(f"lddw{par}", lw), (G, gh), (V, he)]
                            if icp == 0 and pc == 0:
                                wts = wts + pd_free[tb] + \
                                      ([(V, vbar), (Sc, scbar)] if fg == 0 else [])
                            m = prog.op(T, lambda e, f=fg, p=pc, q=par, t=tb, i=icp: e.matmul(
                                            pd[t][:, :],
                                            hfm[:, (p * 16 + i) * 512 + t * 128:
                                                (p * 16 + i) * 512 + t * 128 + 128],
                                            dw2[q][:, i * 512:(i + 1) * 512],
                                            start=(p == 0 and i == 0), stop=False),
                                        waits=wts)
                        if pc == 3:
                            stops[tb] = prog.op(T, lambda e, t=tb: e.matmul(
                                                    pd[t][:, :],
                                                    hsr[0:1, t * 128:(t + 1) * 128],
                                                    n128[0:1, :],
                                                    start=False, stop=True))
                    dw_mm_last[par] = prog.v(T)
                for tb in range(4):
                    tv = prog.op(V, lambda e, t=tb, f=fg: e.tensor_mul(
                                     sgp[:, :], pd[t][:, :], dscb[:, f * 512:(f + 1) * 512]),
                                 waits=[(T, stops[tb]), (G, tmp_free), (G, gh),
                                        (V, dscb_ready)])
                    ga = prog.op(G, lambda e, t=tb, f=fg, tt=tg: e.tensor_add(
                                     acc[:, (tt * 4 + t) * D + f * 512:
                                         (tt * 4 + t) * D + f * 512 + 512],
                                     acc[:, (tt * 4 + t) * D + f * 512:
                                         (tt * 4 + t) * D + f * 512 + 512],
                                     sgp[:, :]),
                                 waits=[(V, tv)])
                    tmp_free = ga
                    acc_g = ga
                    ga_tab[tb] = ga
                    pd_free[tb] = [(V, tv)]
                if do_final:
                    for t4 in range(4):
                        tbg = tg * 4 + t4
                        ld = prog.dma("scalar", lambda e, tt=tbg, f=fg: e.dma_start(
                                          out=xst[:, :],
                                          in_=fxb[tt * 128:(tt + 1) * 128,
                                                  f * 512:(f + 1) * 512]),
                                      sem="ldp0", waits=[(V, ax_prev)])
                        ax = prog.op(V, lambda e, tt=tbg: e.tensor_scalar(
                                         sst[:, :], xst[:, :], r1p[:, tt:tt + 1],
                                         nmr[:, tt:tt + 1], ALU.mult, ALU.add),
                                     waits=[("ldp0", ld), (V, ad_prev),
                                            (V, r1p_seq)])
                        ad = prog.op(V, lambda e, tt=tbg, f=fg: e.tensor_add(
                                         acc[:, tt * D + f * 512: tt * D + f * 512 + 512],
                                         acc[:, tt * D + f * 512: tt * D + f * 512 + 512],
                                         sst[:, :]),
                                     waits=[(V, ax), (G, ga_tab[t4])])
                        prog.dma("scalar", lambda e, tt=tbg, f=fg: e.dma_start(
                                     out=outp[tt * 128:(tt + 1) * 128,
                                              f * 512:(f + 1) * 512],
                                     in_=acc[:, tt * D + f * 512:
                                             tt * D + f * 512 + 512]),
                                 sem="stp", waits=[(V, ad)])
                        ax_prev = ax
                        ad_prev = ad
            sg_free = tmp_free
            h_free = prog.v(T)
            phs_free = [(V, he)]
            pg_free[0] = pd_free[0]
            pg_free[1] = pd_free[1]
            pu_free[0] = pd_free[2]
            pu_free[1] = pd_free[3]
        if do_final:
            prog.op("sync", lambda e: e.nop(), waits=[("stp", prog.dcnt["stp"])])
        if debug and not do_final:
            stv = prog.dma("sync", lambda e: e.dma_start(out=dram["d_acc2"][:, :], in_=acc[:, :]),
                           sem="st", waits=[(G, acc_g)])
            prog.op("sync", lambda e: e.nop(), waits=[("st", stv)])


def _emit_final(env, xb):
    nc, prog, dram = env["nc"], env["prog"], env["dram"]
    acc = env["acc"]
    rstd, nmr = env["rstd"], env["nmr"]
    V, Sc, T, G = "vector", "scalar", "tensor", "gpsimd"
    out = dram["out"]

    with ExitStack() as ctx:
        Ls = lambda *a: ctx.enter_context(nc.sbuf_tensor(*a, side="left"))
        xt = [Ls(f"fxt{i}", [128, D], F32) for i in range(2)]
        xnt = [Ls(f"fxn{i}", [128, D], F32) for i in range(2)]
        fin = [Ls(f"fin{i}", [128, D], F32) for i in range(2)]
        xt_free = [0, 0]
        xn_free = [0, 0]
        fin_free = [0, 0]
        for tb in range(NTO):
            sl = tb % 2
            ld = prog.dma("sync", lambda e, t=tb, s=sl: e.dma_start(
                              out=xt[s][:, :], in_=xb[t * 128:(t + 1) * 128, :]),
                          sem=f"ldf{sl}", waits=[(V, xt_free[sl])])
            ax = prog.op(Sc, lambda e, t=tb, s=sl: e.activation(
                             xnt[s][:, :], xt[s][:, :], AF.Identity,
                             bias=nmr[:, t:t + 1], scale=rstd[:, t:t + 1]),
                         waits=[(f"ldf{sl}", ld), (V, xn_free[sl])])
            d1 = prog.op(V, lambda e, s=sl: e.tensor_add(fin[s][:, :], xt[s][:, :],
                                                         xnt[s][:, :]),
                         waits=[(Sc, ax)] + ([(f"stf{sl}", fin_free[sl])]
                                            if fin_free[sl] else []))
            d2 = prog.op(V, lambda e, t=tb, s=sl: e.tensor_add(
                             fin[s][:, :], fin[s][:, :], acc[:, t * D:(t + 1) * D]),
                         waits=[(V, d1)])
            fin_free[sl] = prog.dma("sync", lambda e, t=tb, s=sl: e.dma_start(
                                        out=out[t * 128:(t + 1) * 128, :], in_=fin[s][:, :]),
                                    sem=f"stf{sl}", waits=[(V, d2)])
            xt_free[sl] = d1
            xn_free[sl] = d1
        prog.op("sync", lambda e: e.nop(),
                waits=[("stf0", prog.dcnt.get("stf0", 0)), ("stf1", prog.dcnt.get("stf1", 0))])



def host_prep(inputs):
    X = np.asarray(inputs["L0_attn_X"], dtype=np.float32)
    ln_w = np.asarray(inputs["L0_attn_ln_w"], dtype=np.float32)
    ln_b = np.asarray(inputs["L0_attn_ln_b"], dtype=np.float32)
    W_in = np.ascontiguousarray(np.asarray(inputs["L0_attn_W_in"], dtype=np.float32))
    W_out = np.asarray(inputs["L0_attn_W_out"], dtype=np.float32)
    gw = np.asarray(inputs["L0_ffn_gate_w"])
    uw = np.asarray(inputs["L0_ffn_up_w"])
    dw = np.asarray(inputs["L0_ffn_down_w"])
    gs = np.asarray(inputs["L0_ffn_gate_scale"], dtype=np.float32)
    us = np.asarray(inputs["L0_ffn_up_scale"], dtype=np.float32)
    ds = np.asarray(inputs["L0_ffn_down_scale"], dtype=np.float32)

    wip = np.empty((24, 128, 4096), np.float32)
    for g in range(8):
        for w, base in enumerate((0, 2048, 4096)):
            blk = W_in[:, base + 2 * g * 128: base + 2 * g * 128 + 256]
            wip[g * 3 + w] = blk.reshape(16, 128, 256).transpose(1, 0, 2).reshape(128, 4096)
    wop = np.empty((4, 128, 8192), np.float32)
    for fg in range(4):
        blk = W_out[:, fg * 512:(fg + 1) * 512]
        wop[fg] = blk.reshape(16, 128, 512).transpose(1, 0, 2).reshape(128, 8192)

    def pack_gu(wq):
        a = wq.astype(np.float32).astype(BF)
        a = a.reshape(32, 256, 16, 128)
        return np.ascontiguousarray(a.transpose(0, 3, 2, 1)).reshape(32, 128, 4096)

    gbp = pack_gu(gw)
    ubp = pack_gu(uw)
    a = dw.astype(np.float32).astype(BF)
    a = a.reshape(4, 512, 4, 16, 128)
    dwp = np.ascontiguousarray(a.transpose(0, 2, 4, 3, 1)).reshape(16, 128, 8192)

    shared = dict(
        wip=wip, wop=wop, gbp=gbp, ubp=ubp, dwp=dwp,
        lnwc=np.ascontiguousarray(ln_w.reshape(ND, 128).T),
        lnbc=np.ascontiguousarray(ln_b.reshape(ND, 128).T),
        gscp=np.ascontiguousarray(gs.reshape(NI, 128).T),
        uscp=np.ascontiguousarray(us.reshape(NI, 128).T),
        dscp=np.ascontiguousarray(ds.reshape(4, 512)),
        idf=np.eye(128, dtype=np.float32),
    )
    pm = np.zeros((64, 64), np.float32)
    for t in range(32):
        pm[2 * t + 1, 2 * t] = -1.0
        pm[2 * t, 2 * t + 1] = 1.0
    shared["pmat"] = pm.astype(BF)

    inv_freq = (1.0 / (10000.0 ** (np.arange(0, 64, 2, dtype=np.float32)
                                   / np.float32(64.0)))).astype(np.float32)
    in_maps = []
    for c in range(8):
        b, qh = c // 2, c % 2
        off = qh * 1024
        pos = (np.arange(S) + off) % S
        ang = pos[:, None].astype(np.float32) * inv_freq[None, :]
        ang2 = np.repeat(ang, 2, axis=1)
        m = dict(shared)
        m["cs"] = np.ascontiguousarray(np.concatenate(
            [np.cos(ang2).T, np.sin(ang2).T], axis=0).astype(np.float32))
        m["xb"] = np.ascontiguousarray(np.concatenate([X[b, off:], X[b, :off]], axis=0))
        in_maps.append(m)
    return in_maps


def kernel(**inputs):
    in_maps = host_prep(inputs)
    nc = build_program(stage=STAGE, debug=DEBUG)
    res = run_bass_kernel_spmd(nc, in_maps, core_ids=list(range(8)))
    Y = np.empty((B, S, D), np.float32)
    for c in range(8):
        b, qh = c // 2, c % 2
        off = qh * 1024
        Y[b, off:off + 1024, :] = res.results[c]["out"]
    return Y


# version 27
# speedup vs baseline: 1.2026x; 1.0028x over previous
import math
from contextlib import ExitStack

import numpy as np
import ml_dtypes
import concourse.bass as bass
import concourse.mybir as mybir
from concourse.bass_utils import run_bass_kernel_spmd

F32 = mybir.dt.float32
BF16 = mybir.dt.bfloat16
AF = mybir.ActivationFunctionType
ALU = mybir.AluOpType
AX = mybir.AxisListType
BF = ml_dtypes.bfloat16

B, S, D, NH, DH, II = 4, 2048, 2048, 16, 128, 8192
ND = D // 128
NT = S // 128
NTO = 8
NI = II // 128
SCALE = 1.0 / math.sqrt(DH)
EPS = 1e-5
INV_D = 1.0 / D

STAGE = 5
DEBUG = False
P2_SUB = "all"

ENGS = ("sync", "tensor", "vector", "scalar", "gpsimd")


class Prog:

    def __init__(self, nc, ctx):
        self.nc = nc
        self.ctx = ctx
        self.ops = {e: [] for e in ENGS}
        self.seq = {e: 0 for e in ENGS}
        self.dcnt = {}
        self.marks = {e: set() for e in ENGS}
        self.sems = {}

    def _add(self, eng, fn, waits, dsem):
        ws = []
        for name, val in waits:
            if not val or val <= 0:
                continue
            ws.append((name, val))
            if name in ENGS:
                self.marks[name].add(val)
        self.seq[eng] += 1
        self.ops[eng].append((fn, tuple(ws), self.seq[eng], dsem))

    def op(self, eng, fn, waits=()):
        self._add(eng, fn, waits, None)
        return self.seq[eng]

    def dma(self, eng, fn, sem, waits=()):
        self.dcnt[sem] = self.dcnt.get(sem, 0) + 16
        self._add(eng, fn, waits, sem)
        return self.dcnt[sem]

    def v(self, eng):
        return self.seq[eng]

    def barrier(self):
        self.dcnt["bar"] = self.dcnt.get("bar", 0) + len(ENGS)
        target = self.dcnt["bar"]
        for e in ENGS:
            self.seq[e] += 1
            self.ops[e].append(("BARRIER", target, self.seq[e], None))

    def emit(self):
        nc = self.nc
        rank = {}
        for e in ENGS:
            rank[e] = {s: i + 1 for i, s in enumerate(sorted(self.marks[e]))}
        for e in ENGS:
            self.sems["q" + e] = self.ctx.enter_context(nc.semaphore("q" + e))
        for name in self.dcnt:
            self.sems[name] = self.ctx.enter_context(nc.semaphore(name))

        def mk(e):
            ops = self.ops[e]
            sems = self.sems
            marks = self.marks[e]

            def body(eng):
                for fn, waits, eseq, dsem in ops:
                    if fn == "BARRIER":
                        if e == "sync":
                            eng.sem_inc(sems["bar"], 1)
                        else:
                            eng.drain().then_inc(sems["bar"], 1)
                        eng.wait_ge(sems["bar"], waits)
                        continue
                    for wname, wval in waits:
                        if wname in ENGS:
                            eng.wait_ge(sems["q" + wname], rank[wname][wval])
                        else:
                            eng.wait_ge(sems[wname], wval)
                    inst = fn(eng)
                    if dsem is not None:
                        inst.then_inc(sems[dsem], 16)
                    if eseq in marks:
                        inst.then_inc(sems["q" + e], 1)
            return body

        with nc.Block() as blk:
            for e in ENGS:
                getattr(blk, e)(mk(e))



def build_program(stage=STAGE, debug=DEBUG):
    nc = bass.Bass("TRN2", target_bir_lowering=False)

    dram = {}

    def din(name, shape, dtype):
        dram[name] = nc.dram_tensor(name, shape, dtype, kind="ExternalInput").ap()
        return dram[name]

    def dout(name, shape, dtype):
        dram[name] = nc.dram_tensor(name, shape, dtype, kind="ExternalOutput").ap()
        return dram[name]

    xb = din("xb", [S, D], F32)
    din("wip", [24, 128, 4096], F32)
    din("wop", [4, 128, 8192], F32)
    din("gbp", [32, 128, 4096], BF16)
    din("ubp", [32, 128, 4096], BF16)
    din("dwp", [16, 128, 8192], BF16)
    din("cs", [128, S], F32)
    din("lnwc", [128, ND], F32)
    din("lnbc", [128, ND], F32)
    din("gscp", [128, NI], F32)
    din("uscp", [128, NI], F32)
    din("dscp", [4, 512], F32)
    din("pmat", [64, 64], BF16)
    din("idf", [128, 128], F32)

    dout("out", [NTO * 128, D], F32)

    if debug:
        dout("d_xn2", [128, ND * S], BF16)
        dout("d_xnfm", [128, ND * 1024], BF16)
        dout("d_rstd", [128, NT], F32)
        dout("d_nmr", [128, NT], F32)
        if stage >= 2:
            dout("d_q", [128, 2 * 1024], BF16)
            dout("d_k", [128, 2 * S], BF16)
            dout("d_v", [128, NT * 256], BF16)
            dout("d_bias", [128, 2], F32)
            dout("d_ctx", [128, NH * 1024], BF16)
        if stage >= 3:
            dout("d_acc", [128, NTO * D], F32)
        if stage >= 4:
            dout("d_acc2", [128, NTO * D], F32)

    with ExitStack() as top:
        prog = Prog(nc, top)
        L = lambda *a: top.enter_context(nc.sbuf_tensor(*a, side="left"))

        env = {"nc": nc, "prog": prog, "dram": dram, "debug": debug, "top": top,
               "stage": stage}

        env["lnw"] = L("lnw_sb", [128, ND], F32)
        env["lnb"] = L("lnb_sb", [128, ND], F32)
        env["gsc"] = L("gsc_sb", [128, NI], F32)
        env["usc"] = L("usc_sb", [128, NI], F32)
        env["pm"] = L("pm_sb", [64, 64], BF16)
        env["idf"] = L("idf_sb", [128, 128], F32)
        env["ones_cb"] = L("ones_cb", [128, 1], BF16)
        env["ones_bb"] = L("ones_bb", [128, 128], BF16)
        env["ones_rf"] = L("ones_rf", [1, 128], F32)
        env["n128"] = L("n128_r", [1, 512], BF16)
        env["eps_c"] = L("eps_c", [128, 1], F32)
        env["rstd"] = L("rstd_sb", [128, NT], F32)
        env["nmr"] = L("nmr_sb", [128, NT], F32)
        env["scr"] = L("scr_c", [128, 32], F32)
        env["scr1"] = L("scr_1", [1, 8], F32)
        env["xnfm"] = L("xnfm", [128, ND * 1024], BF16)

        with ExitStack() as sCtx:
          env["sCtx"] = sCtx
          with ExitStack() as sA:
            env["xn2"] = sA.enter_context(nc.sbuf_tensor("xn2f", [128, ND * S], BF16, side="left"))
            env["cs"] = sA.enter_context(nc.sbuf_tensor("cs_sb", [128, S], F32, side="left"))
            ldc = []
            for i, (name, dst) in enumerate([
                    ("lnwc", env["lnw"]), ("lnbc", env["lnb"]),
                    ("gscp", env["gsc"]), ("uscp", env["usc"]),
                    ("pmat", env["pm"]), ("idf", env["idf"]),
                    ("cs", env["cs"])]):
                prog.dma("sync", lambda e, d=dst, s=dram[name]: e.dma_start(out=d[:, :], in_=s[:, :]),
                         sem=f"ldc{i}")
                ldc.append((f"ldc{i}", 16))
            env["ldc"] = ldc
            for dst, valc in [(env["ones_cb"], 1.0), (env["ones_bb"], 1.0), (env["ones_rf"], 1.0),
                              (env["n128"], -128.0), (env["eps_c"], EPS)]:
                prog.op("vector", lambda e, d=dst, vv=valc: e.memset(d[:, :], vv))
            env["dvc_done"] = prog.v("vector")

            _emit_ln(env, xb)
            if debug:
                prog.barrier()
                prog.dma("sync", lambda e: e.dma_start(out=dram["d_xn2"][:, :], in_=env["xn2"][:, :]), sem="st")
                prog.dma("sync", lambda e: e.dma_start(out=dram["d_xnfm"][:, :], in_=env["xnfm"][:, :]), sem="st")
                prog.dma("sync", lambda e: e.dma_start(out=dram["d_rstd"][:, :], in_=env["rstd"][:, :]), sem="st")
                prog.dma("sync", lambda e: e.dma_start(out=dram["d_nmr"][:, :], in_=env["nmr"][:, :]), sem="st")

            if stage >= 2:
                _emit_attention(env)
                if stage == 2:
                    prog.barrier()
          if stage >= 3:
            env["acc"] = top.enter_context(nc.sbuf_tensor("out_acc", [128, NTO * D], F32, side="left"))
            _emit_outproj(env)
            prog.barrier()
        if stage >= 4:
            _emit_ffn(env, xb if stage >= 5 else None)
            prog.barrier()

        prog.barrier()
        st_total = prog.dcnt.get("st", 0)
        if st_total:
            prog.op("sync", lambda e: e.nop(), waits=[("st", st_total)])
        prog.emit()
    return nc



def _emit_ln(env, xb):
    nc, prog = env["nc"], env["prog"]
    scr, eps_c = env["scr"], env["eps_c"]
    rstd, nmr = env["rstd"], env["nmr"]
    xn2, xnfm = env["xn2"], env["xnfm"]
    lnw, lnb, idf = env["lnw"], env["lnb"], env["idf"]
    V, Sc, T = "vector", "scalar", "tensor"

    with ExitStack() as ctx:
        Ls = lambda *a: ctx.enter_context(nc.sbuf_tensor(*a, side="left"))
        NC = 4
        xt = [Ls(f"xt{i}", [128, D], F32) for i in range(NC)]
        xnt = [Ls(f"xnt{i}", [128, D], F32) for i in range(NC)]
        xn2t = [Ls(f"xn2t{i}", [128, D], F32) for i in range(NC)]
        pst = [ctx.enter_context(nc.psum_tensor(f"pst{i}", [128, 512], F32, side="right"))
               for i in range(4)]

        xt_free = [0] * NC
        src_free = [0] * NC
        slot_ev = [None] * 4
        col1_free = [0] * NC
        prev_d_last = [0] * NC
        prev_a_last = [0] * NC
        sq_prev = [0] * NC
        gp_last = [0] * NC
        env["x2ev"] = [0] * NT
        res = {}

        def chain(tt):
            sl = tt % NC
            cb = 8 * sl
            cc = [scr[:, cb + i:cb + i + 1] for i in range(8)]
            ldx = prog.dma("sync",
                           lambda e, t=tt, s=sl: e.dma_start(
                               out=xt[s][:, :], in_=xb[t * 128:(t + 1) * 128, :]),
                           sem=f"ldx{sl}", waits=[(Sc, xt_free[sl]), (V, prev_d_last[sl]),
                                                  ("gpsimd", gp_last[sl])])
            d1 = prog.op(V, lambda e, s=sl, c=cc: e.reduce_sum(c[0], xt[s][:, :], axis=AX.X),
                         waits=[(f"ldx{sl}", ldx), (V, prev_d_last[sl]), (Sc, prev_a_last[sl]),
                                ("gpsimd", gp_last[sl])]
                               + (env["ldc"] if tt == 0 else []))
            a1 = prog.op(Sc, lambda e, s=sl, c=cc: e.activation(xnt[s][:, :], xt[s][:, :],
                                                                AF.Square, accum_out=c[1]),
                         waits=[(f"ldx{sl}", ldx), (V, col1_free[sl]), (Sc, max(sq_prev)),
                                (T, src_free[sl])])
            sq_prev[sl] = a1
            yield
            d2 = prog.op(V, lambda e, c=cc: e.tensor_scalar(c[2], c[0], INV_D, None, ALU.mult),
                         waits=[(V, d1)])
            d3 = prog.op(V, lambda e, c=cc: e.tensor_scalar(c[3], c[1], INV_D, None, ALU.mult),
                         waits=[(Sc, a1), (V, d2)])
            col1_free[sl] = d3
            d4 = prog.op(V, lambda e, c=cc: e.tensor_scalar(c[4], c[2], c[2], None, ALU.mult),
                         waits=[(V, d3)])
            d5 = prog.op(V, lambda e, c=cc: e.tensor_sub(c[3], c[3], c[4]), waits=[(V, d4)])
            yield
            a2 = prog.op(Sc, lambda e, c=cc: e.activation(c[5], c[3], AF.Ln, bias=eps_c[:, :]),
                         waits=[(V, d5)])
            a3 = prog.op(Sc, lambda e, t=tt, c=cc: e.activation(rstd[:, t:t + 1], c[5],
                                                                AF.Exp, scale=-0.5),
                         waits=[(Sc, a2)])
            yield
            d6 = prog.op(V, lambda e, t=tt, c=cc: e.tensor_scalar(nmr[:, t:t + 1], c[2],
                                                                  rstd[:, t:t + 1], -1.0,
                                                                  ALU.mult, ALU.mult),
                         waits=[(Sc, a3)])
            d6b = prog.op("gpsimd", lambda e, t=tt, s=sl: e.tensor_scalar(
                              xnt[s][:, :], xt[s][:, :], rstd[:, t:t + 1],
                              nmr[:, t:t + 1], ALU.mult, ALU.add),
                          waits=[(V, d6), (T, src_free[sl]), (Sc, a1)])
            yield
            a5 = prog.op(Sc, lambda e, s=sl, c=cc: e.activation(xt[s][:, :], xnt[s][:, :],
                                                                AF.Square, accum_out=c[1]),
                         waits=[(V, col1_free[sl]), ("gpsimd", d6b), (Sc, max(sq_prev))])
            sq_prev[sl] = a5
            yield
            d9 = prog.op(V, lambda e, c=cc: e.tensor_scalar(c[3], c[1], INV_D, None, ALU.mult),
                         waits=[(Sc, a5)])
            col1_free[sl] = d9
            yield
            a6 = prog.op(Sc, lambda e, c=cc: e.activation(c[5], c[3], AF.Ln, bias=eps_c[:, :]),
                         waits=[(V, d9)])
            a7 = prog.op(Sc, lambda e, c=cc: e.activation(c[6], c[5], AF.Exp, scale=-0.5),
                         waits=[(Sc, a6)])
            yield
            a8 = prog.op("gpsimd", lambda e, s=sl, c=cc: e.tensor_scalar(
                             xn2t[s][:, :], xnt[s][:, :], c[6], 0.0,
                             ALU.mult, ALU.add),
                         waits=[(Sc, a7), (T, src_free[sl])])
            prev_d_last[sl], prev_a_last[sl] = d9, a7
            xt_free[sl] = a5
            gp_last[sl] = a8
            res[tt] = (sl, a8, d6b)

        def transposes(tt):
            sl, a8, d6b = res[tt]
            x2ev_pair = [0, 0]
            work = [("x2", dc) for dc in range(ND)]
            if tt < NTO:
                work += [("xn", dc) for dc in range(ND)]
            last_t = 0
            for j, (kind, dc) in enumerate(work):
                pslot = j % 4
                p = pst[pslot]
                srcbuf = xn2t[sl] if kind == "x2" else xnt[sl]
                wts = [("gpsimd", a8)] if kind == "x2" else [("gpsimd", d6b)]
                if slot_ev[pslot] is not None:
                    wts.append(slot_ev[pslot])
                mm_t = prog.op(T, lambda e, sb=srcbuf, d=dc, pp=p: e.transpose(
                                   pp[:, 0:128], sb[:, d * 128:(d + 1) * 128], idf[:, :]),
                               waits=wts)
                last_t = mm_t
                if kind == "x2":
                    if dc % 2 == 0:
                        ev = prog.op(V, lambda e, t=tt, d=dc, pp=p: e.tensor_scalar(
                                         xn2[:, d * S + t * 128: d * S + (t + 1) * 128],
                                         pp[:, 0:128], lnw[:, d:d + 1], lnb[:, d:d + 1],
                                         ALU.mult, ALU.add),
                                     waits=[(T, mm_t)])
                        slot_ev[pslot] = (V, ev)
                    else:
                        ev = prog.op(Sc, lambda e, t=tt, d=dc, pp=p: e.activation(
                                         xn2[:, d * S + t * 128: d * S + (t + 1) * 128],
                                         pp[:, 0:128], AF.Identity,
                                         bias=lnb[:, d:d + 1], scale=lnw[:, d:d + 1]),
                                     waits=[(T, mm_t)])
                        slot_ev[pslot] = (Sc, ev)
                    x2ev_pair[dc % 2] = ev
                else:
                    ev = prog.op(V, lambda e, t=tt, d=dc, pp=p: e.tensor_copy(
                                     xnfm[:, d * 1024 + t * 128: d * 1024 + (t + 1) * 128],
                                     pp[:, 0:128]),
                                 waits=[(T, mm_t)])
                    slot_ev[pslot] = (V, ev)
            env["x2ev"][tt] = [(V, x2ev_pair[0]), (Sc, x2ev_pair[1])]
            src_free[sl] = last_t

        for k in range(NT // 4):
            alive = [chain(4 * k + i) for i in range(4)]
            while alive:
                for gch in list(alive):
                    try:
                        next(gch)
                    except StopIteration:
                        alive.remove(gch)
            for i in range(4):
                transposes(4 * k + i)
        env["p1T"] = prog.v("tensor")


def _emit_attention(env):
    nc, prog, dram = env["nc"], env["prog"], env["dram"]
    xn2f = env["xn2"]
    cs, pm = env["cs"], env["pm"]
    ones_cb, ones_rf, scr1 = env["ones_cb"], env["ones_rf"], env["scr1"]
    ones_bb = env["ones_bb"]
    debug = env["debug"]
    V, Sc, T = "vector", "scalar", "tensor"
    wip = dram["wip"]

    env["ctx"] = env["sCtx"].enter_context(
        nc.sbuf_tensor("ctxf", [128, NH * 1024], BF16, side="right"))
    ctxf = env["ctx"]
    with ExitStack() as ctx:
        R = lambda *a: ctx.enter_context(nc.sbuf_tensor(*a, side="right"))
        wf32 = [R(f"wf32_{i}", [128, 512], F32) for i in range(2)]
        wbf = [R(f"wbf{i}", [128, 4096], BF16) for i in range(3)]
        k2 = R("k2", [128, 2 * S], BF16)
        q2 = R("q2", [128, 2 * 1024], BF16)
        v2 = R("v2", [128, NT * 256], BF16)
        ebig = ctx.enter_context(nc.sbuf_tensor("ebig", [128, 8 * 512], BF16, side="left"))
        sqb = R("sqb", [128, 1024], BF16)
        rb = R("rb", [128, 512], F32)
        biasc = R("biasc", [128, 2], F32)
        pp = [ctx.enter_context(nc.psum_tensor(f"pp{i}", [128, 512], F32))
              for i in range(2)]
        ppR = ctx.enter_context(nc.psum_tensor("ppR", [128, 512], F32))
        ppS = ctx.enter_context(nc.psum_tensor("ppS", [128, 512], F32))
        ppD = [ctx.enter_context(nc.psum_tensor(f"ppD{i}", [128, 512], F32))
               for i in range(2)]
        ppC = [ctx.enter_context(nc.psum_tensor(f"ppC{i}", [128, 512], F32))
               for i in range(2)]

        wf32_free = [0, 0]
        wbf_mm_last = [0, 0, 0]
        slab_ready = {}
        pp_free = [[], [], [], []]
        ppR_free = []
        bankF = {"S": [], "D0": [], "D1": []}
        ppC_free = [[], []]
        e_free = 0
        cv_prev = 0
        tcos_free = 0
        sq_free = []
        k2_rd = []
        v2_rd = []
        tile_i = 0

        def load_slab(j):
            for qt in range(8):
                qp = (j * 8 + qt) % 2
                w_extra = [("tensor", env["p1T"])] if (j < 3) else []
                ld = prog.dma("sync",
                              lambda e, jj=j, h=qt, q=qp: e.dma_start(
                                  out=wf32[q][:, :], in_=wip[jj, :, h * 512:(h + 1) * 512]),
                              sem=f"ldw{qp}", waits=[("gpsimd", wf32_free[qp])] + w_extra)
                wts = [(f"ldw{qp}", ld)]
                if qt == 0:
                    wts.append((T, wbf_mm_last[j % 3]))
                cv = prog.op("gpsimd", lambda e, jj=j, h=qt, q=qp: e.tensor_copy(
                                 wbf[jj % 3][:, h * 512:(h + 1) * 512], wf32[q][:, :]),
                             waits=wts)
                wf32_free[qp] = cv
            slab_ready[j] = cv

        load_slab(0)
        load_slab(1)
        load_slab(2)

        pp4 = [pp[0], pp[1], ppC[0], ppC[1]]
        for g in range(8):

            rope_last = 0
            vproj_last = 0
            rope_done = {}
            qk_jobs = [("q", hh, tg) for hh in range(2) for tg in range(2)] + \
                      [("k", hh, tg) for hh in range(2) for tg in range(4)]
            pending_rope = None
            for kind, hh, tg in qk_jobs:
                slabj = 3 * g + (0 if kind == "q" else 1)
                dst = q2 if kind == "q" else k2
                dstoff = hh * (1024 if kind == "q" else 2048) + tg * 512
                sl = tile_i % 4
                w0 = [("gpsimd", slab_ready[slabj])] + pp_free[sl] + k2_rd
                if g == 0:
                    w0 += env["x2ev"][4 * tg + 3]
                if g == 0 and tile_i in (2, 3):
                    w0 += env["x2ev"][14] + env["x2ev"][15]
                k2_rd = []
                m_last = None
                for dc in range(ND):
                    m_last = prog.op(T, lambda e, d=dc, h=hh, t=tg, s=sl, sj=slabj: e.matmul(
                                         pp4[s][:, :],
                                         wbf[sj % 3][:, d * 256 + h * 128: d * 256 + (h + 1) * 128],
                                         xn2f[:, d * S + t * 512: d * S + t * 512 + 512],
                                         start=(d == 0), stop=(d == ND - 1)),
                                     waits=(w0 if dc == 0 else ()))
                wbf_mm_last[slabj % 3] = m_last
                aev = prog.op(Sc, lambda e, s=sl, do=dstoff, dd=dst: e.activation(
                                  dd[:, do:do + 512], pp4[s][:, :], AF.Copy),
                              waits=[(T, m_last)])
                if pending_rope is not None:
                    pending_rope()
                if P2_SUB == "mm":
                    pending_rope = None
                    pp_free[sl] = [(Sc, aev)]
                    tile_i += 1
                    continue

                def rope(m_last=m_last, aev=aev, sl=sl, tg=tg, dst=dst, dstoff=dstoff,
                         key=(kind, hh, tg)):
                    nonlocal tcos_free, ppR_free, rope_last
                    mB = prog.op(T, lambda e, do=dstoff, dd=dst: e.matmul(
                                     ppR[0:64, :], pm[:, :], dd[0:64, do:do + 512],
                                     start=True, stop=True),
                                 waits=[(Sc, aev)] + ppR_free)
                    dco = prog.op(V, lambda e, t=tg, do=dstoff, dd=dst: e.tensor_mul(
                                      dd[0:64, do:do + 512], dd[0:64, do:do + 512],
                                      cs[0:64, t * 512:t * 512 + 512]),
                                  waits=[(T, mB)])
                    dt2 = prog.op(V, lambda e, t=tg: e.tensor_mul(
                                      rb[0:64, :], ppR[0:64, :], cs[64:128, t * 512:t * 512 + 512]),
                                  waits=[(T, mB), (V, tcos_free)])
                    dadd = prog.op(V, lambda e, do=dstoff, dd=dst: e.tensor_add(
                                       dd[0:64, do:do + 512], dd[0:64, do:do + 512], rb[0:64, :]),
                                   waits=[(V, dt2)])
                    rope_last = dadd
                    rope_done[key] = dadd
                    tcos_free = dadd
                    ppR_free = [(V, dt2)]
                    pp_free[sl] = [(Sc, aev)]

                pp_free[sl] = [(Sc, aev)]
                pending_rope = rope
                tile_i += 1
            slabv = 3 * g + 2
            bias_ready = [0, 0]
            pb = [ppS, ppD[0], ppD[1]]
            pbk = ["S", "D0", "D1"]
            bsteps = []
            if P2_SUB != "qkv":
                for hh in range(2):
                    def mk_head(hh=hh):
                        state = {}

                        def s_sq(kind, half):
                            def f():
                                if kind == "q":
                                    state["gate"] = prog.op("gpsimd", lambda e, h=hh: e.tensor_mul(
                                        sqb[:, 0:1024], q2[:, h * 1024:(h + 1) * 1024],
                                        q2[:, h * 1024:(h + 1) * 1024]),
                                        waits=[(V, rope_done[("q", hh, 1)])] + sq_free)
                                else:
                                    state["gate"] = prog.op("gpsimd", lambda e, h=hh, q=half: e.tensor_mul(
                                        sqb[:, :],
                                        k2[:, h * 2048 + q * 1024: h * 2048 + q * 1024 + 1024],
                                        k2[:, h * 2048 + q * 1024: h * 2048 + q * 1024 + 1024]),
                                        waits=[(V, rope_done[("k", hh, 2 * half + 1)]),
                                               (T, state["mb"])])
                            return f

                        def s_mm(j):
                            def f():
                                bank = j % 3
                                mb = prog.op(T, lambda e, b=j % 2, k=bank: e.matmul(
                                                 pb[k][:, :], ones_bb[:, :],
                                                 sqb[:, b * 512:(b + 1) * 512],
                                                 start=True, stop=True),
                                             waits=[("gpsimd", state["gate"])] + bankF[pbk[bank]])
                                state["mb"] = mb
                                rd = prog.op(V, lambda e, j_=j, k=bank: e.reduce_max(
                                                 scr1[0:1, j_:j_ + 1], pb[k][0:1, :], axis=AX.X),
                                             waits=[(T, mb)])
                                bankF[pbk[bank]] = [(V, rd)]
                                state["rd"] = rd
                            return f

                        def s_fin():
                            nonlocal_state = state

                            def f():
                                rd_last = nonlocal_state["rd"]
                                c1 = prog.op(V, lambda e: e.tensor_max(scr1[0:1, 6:7], scr1[0:1, 0:1],
                                                                       scr1[0:1, 1:2]), waits=[(V, rd_last)])
                                c2 = prog.op(V, lambda e: e.tensor_max(scr1[0:1, 7:8], scr1[0:1, 2:3],
                                                                       scr1[0:1, 3:4]), waits=[(V, c1)])
                                c3 = prog.op(V, lambda e: e.tensor_max(scr1[0:1, 7:8], scr1[0:1, 7:8],
                                                                       scr1[0:1, 4:5]), waits=[(V, c2)])
                                c4 = prog.op(V, lambda e: e.tensor_max(scr1[0:1, 7:8], scr1[0:1, 7:8],
                                                                       scr1[0:1, 5:6]), waits=[(V, c3)])
                                c5 = prog.op(V, lambda e: e.tensor_add(scr1[0:1, 6:7], scr1[0:1, 6:7],
                                                                       scr1[0:1, 7:8]), waits=[(V, c4)])
                                c6 = prog.op(V, lambda e: e.tensor_scalar(scr1[0:1, 7:8], scr1[0:1, 6:7],
                                                                          -0.5 * SCALE, None, ALU.mult),
                                             waits=[(V, c5)])
                                mbb = prog.op(T, lambda e: e.matmul(ppS[:, 0:1], ones_rf[:, :],
                                                                    scr1[0:1, 7:8], start=True, stop=True),
                                              waits=[(V, c6)] + bankF["S"])
                                bias_ready[hh] = prog.op(V, lambda e, h=hh: e.tensor_copy(
                                                             biasc[:, h:h + 1], ppS[:, 0:1]),
                                                         waits=[(T, mbb)])
                                bankF["S"] = [(V, bias_ready[hh])]
                            return f

                        steps = [s_sq("q", 0), s_mm(0), s_mm(1), s_sq("k", 0), s_mm(2),
                                 s_mm(3), s_sq("k", 1), s_mm(4), s_mm(5), s_fin()]

                        def close_sqfree():
                            nonlocal sq_free
                            sq_free = [(T, state["mb"])]
                        steps.append(close_sqfree)
                        return steps
                    bsteps += mk_head()

            bi = 0
            for tb in range(NT):
                sl = tile_i % 4
                w0 = [("gpsimd", slab_ready[slabv])] + pp_free[sl] + v2_rd
                if g == 0:
                    w0 += env["x2ev"][tb]
                v2_rd = []
                m_last = None
                for dc in range(ND):
                    m_last = prog.op(T, lambda e, d=dc, t=tb, s=sl, sj=slabv: e.matmul(
                                         pp4[s][:, 0:256],
                                         xn2f[:, d * S + t * 128: d * S + t * 128 + 128],
                                         wbf[sj % 3][:, d * 256: d * 256 + 256],
                                         start=(d == 0), stop=(d == ND - 1)),
                                     waits=(w0 if dc == 0 else ()))
                wbf_mm_last[slabv % 3] = m_last
                aev = prog.op(Sc, lambda e, s=sl, t=tb: e.activation(
                                  v2[:, t * 256:(t + 1) * 256], pp4[s][:, 0:256], AF.Copy),
                              waits=[(T, m_last)])
                vproj_last = aev
                if pending_rope is not None:
                    pending_rope()
                    pending_rope = None
                pp_free[sl] = [(Sc, aev)]
                tile_i += 1
                for _ in range(2):
                    if bi < len(bsteps):
                        bsteps[bi]()
                        bi += 1
            while bi < len(bsteps):
                bsteps[bi]()
                bi += 1
            for j in range(3 * g + 3, min(3 * g + 6, 24)):
                load_slab(j)

            pps = [pp[0], pp[1], ppR]
            pp3_free = [pp_free[0], pp_free[1], ppR_free]
            sc3 = 0
            for hh in (range(2) if P2_SUB == "all" else ()):
                for qg in range(2):
                    un = (g * 4 + hh * 2 + qg) % 2
                    dkey = "D0" if un == 0 else "D1"
                    ms_seq = [0] * 16
                    ex_seq = [0] * 16
                    md_seq = [0] * 16
                    for kb in range(16):
                        sl = sc3 % 3
                        sc3 += 1
                        wts = list(pp3_free[sl])
                        if kb == 0:
                            wts += [(V, rope_last), (Sc, vproj_last)]
                        ms_seq[kb] = prog.op(T, lambda e, h=hh, q=qg, k=kb, s=sl: e.matmul(
                                                 pps[s][:, :],
                                                 k2[:, h * 2048 + k * 128: h * 2048 + (k + 1) * 128],
                                                 q2[:, h * 1024 + q * 512: h * 1024 + q * 512 + 512],
                                                 start=True, stop=True),
                                             waits=wts)
                        wts = [(T, ms_seq[kb]), (V, bias_ready[hh])]
                        if kb == 0:
                            wts.append((T, e_free))
                        if kb >= 8:
                            wts.append((T, md_seq[kb - 8]))
                        ex_seq[kb] = prog.op(Sc, lambda e, k=kb, s=sl, h=hh: e.activation(
                                                 ebig[:, (k % 8) * 512:(k % 8) * 512 + 512],
                                                 pps[s][:, :],
                                                 AF.Exp, bias=biasc[:, h:h + 1], scale=SCALE),
                                             waits=wts)
                        pp3_free[sl] = [(Sc, ex_seq[kb])]
                        if kb >= 2:
                            prog.op(T, lambda e, k=kb - 2, h=hh, u=un: e.matmul(
                                        ppC[u][:, :],
                                        v2[:, k * 256 + h * 128: k * 256 + (h + 1) * 128],
                                        ebig[:, (k % 8) * 512:(k % 8) * 512 + 512],
                                        start=(k == 0), stop=False),
                                    waits=[(Sc, ex_seq[kb - 2])] +
                                          (ppC_free[un] + pp_free[2 + un]
                                           if kb == 2 else []))
                            md_seq[kb - 2] = prog.op(T, lambda e, k=kb - 2, u=un: e.matmul(
                                        ppD[u][:, :], ones_bb[:, :],
                                        ebig[:, (k % 8) * 512:(k % 8) * 512 + 512],
                                        start=(k == 0), stop=False),
                                    waits=(bankF[dkey] if kb == 2 else ()))
                    prog.op(T, lambda e, h=hh, u=un: e.matmul(
                                ppC[u][:, :],
                                v2[:, 14 * 256 + h * 128: 14 * 256 + (h + 1) * 128],
                                ebig[:, 6 * 512:7 * 512],
                                start=False, stop=False),
                            waits=[(Sc, ex_seq[14])])
                    md_seq[14] = prog.op(T, lambda e, u=un: e.matmul(
                                ppD[u][:, :], ones_bb[:, :],
                                ebig[:, 6 * 512:7 * 512], start=False, stop=False))
                    mp_last = prog.op(T, lambda e, h=hh, u=un: e.matmul(
                                          ppC[u][:, :],
                                          v2[:, 15 * 256 + h * 128: 15 * 256 + (h + 1) * 128],
                                          ebig[:, 7 * 512:8 * 512],
                                          start=False, stop=True),
                                      waits=[(Sc, ex_seq[15])])
                    md_last = prog.op(T, lambda e, u=un: e.matmul(
                                ppD[u][:, :], ones_bb[:, :],
                                ebig[:, 7 * 512:8 * 512], start=False, stop=True))
                    md_seq[15] = md_last
                    e_free = md_last
                    rcb = prog.op(V, lambda e, u=un: e.reciprocal(rb[:, :], ppD[u][:, :]),
                                  waits=[(T, md_last), (V, cv_prev), (V, rope_last)])
                    bankF[dkey] = [(V, rcb)]
                    cv = prog.op(V, lambda e, h=hh, q=qg, gg=g, u=un: e.tensor_mul(
                                     ctxf[:, (2 * gg + h) * 1024 + q * 512:
                                          (2 * gg + h) * 1024 + q * 512 + 512],
                                     ppC[u][:, :], rb[:, :]),
                                 waits=[(V, rcb), (T, mp_last)])
                    cv_prev = cv
                    ppC_free[un] = [(V, cv)]
                    k2_rd = [(T, mp_last), (Sc, ex_seq[15])]
                    v2_rd = [(T, mp_last)]
            pp_free[0] = pp3_free[0]
            pp_free[1] = pp3_free[1]
            pp_free[2] = ppC_free[0]
            pp_free[3] = ppC_free[1]
            ppR_free = pp3_free[2]
            k2_rd = k2_rd[-2:]
            v2_rd = v2_rd[-1:]
            if P2_SUB == "all":
                tcos_free = cv_prev

            if debug and g == 0:
                stv = prog.dma("sync", lambda e: e.dma_start(out=dram["d_q"][:, :], in_=q2[:, :]),
                               sem="st", waits=[(V, rope_last)])
                stv = prog.dma("sync", lambda e: e.dma_start(out=dram["d_k"][:, :], in_=k2[:, :]),
                               sem="st", waits=[(V, rope_last)])
                stv = prog.dma("sync", lambda e: e.dma_start(out=dram["d_v"][:, :], in_=v2[:, :]),
                               sem="st", waits=[(Sc, vproj_last)])
                if P2_SUB != "qkv":
                    stv = prog.dma("sync", lambda e: e.dma_start(out=dram["d_bias"][:, :], in_=biasc[:, :]),
                                   sem="st", waits=[(V, bias_ready[1])])
                sn = prog.op("sync", lambda e: e.nop(), waits=[("st", stv)])
                k2_rd.append(("sync", sn))
                v2_rd.append(("sync", sn))
                bankF["S"].append(("sync", sn))

        if debug and P2_SUB == "all":
            stv = 0
            for blk in range(32):
                stv = prog.dma("sync", lambda e, bb=blk: e.dma_start(
                                   out=dram["d_ctx"][:, bb * 512:(bb + 1) * 512],
                                   in_=ctxf[:, bb * 512:(bb + 1) * 512]),
                               sem="st", waits=ppC_free[0] + ppC_free[1])
            prog.op("sync", lambda e: e.nop(), waits=[("st", stv)])
        env["p2T"] = prog.v("tensor")
        env["p2V"] = prog.v("vector")
        env["p2Sc"] = prog.v("scalar")
        env["p2G"] = prog.v("gpsimd")


def _emit_outproj(env):
    nc, prog, dram = env["nc"], env["prog"], env["dram"]
    ctxf, acc = env["ctx"], env["acc"]
    debug = env["debug"]
    V, Sc, T = "vector", "scalar", "tensor"
    wop = dram["wop"]

    with ExitStack() as ctx:
        Ls = lambda *a: ctx.enter_context(nc.sbuf_tensor(*a, side="left"))
        wof = [Ls(f"wof{i}", [128, 4096], F32) for i in range(2)]
        wob = [Ls(f"wob{i}", [128, 8192], BF16) for i in range(2)]
        po = [ctx.enter_context(nc.psum_tensor(f"po{i}", [128, 512], F32))
              for i in range(2)]
        wof_free = [0, 0]
        wob_mm_last = [0, 0]
        po_free = [[(Sc, env["p2Sc"])], [(Sc, env["p2Sc"])]]
        ev_last = 0
        slab_casts = {}

        def load_wo(fg):
            wp = fg % 2
            for half in range(2):
                wx = [(V, wof_free[wp])]
                if fg < 2 and half == 0:
                    wx += [(T, env["p2T"]), (V, env["p2V"]),
                           (Sc, env["p2Sc"]), ("gpsimd", env["p2G"])]
                ld = prog.dma("sync",
                              lambda e, f=fg, h=half, q=wp: e.dma_start(
                                  out=wof[q][:, :], in_=wop[f, :, h * 4096:(h + 1) * 4096]),
                              sem=f"ldo{wp}", waits=wx)
                wts = [(f"ldo{wp}", ld)]
                if half == 0:
                    wts.append((T, wob_mm_last[wp]))
                cv = prog.op(V, lambda e, h=half, q=wp: e.tensor_copy(
                                 wob[q][:, h * 4096:(h + 1) * 4096], wof[q][:, :]),
                             waits=wts)
                wof_free[wp] = cv
            slab_casts[fg] = wof_free[wp]

        load_wo(0)
        load_wo(1)
        for fg in range(4):
            wp = fg % 2
            casts = slab_casts[fg]
            for tb in range(NTO):
                sl = tb % 2
                w0 = [(V, casts)] + po_free[sl]
                m_last = None
                for h in range(NH):
                    m_last = prog.op(T, lambda e, hh=h, t=tb, s=sl, q=wp: e.matmul(
                                         po[s][:, :],
                                         ctxf[:, hh * 1024 + t * 128: hh * 1024 + (t + 1) * 128],
                                         wob[q][:, hh * 512:(hh + 1) * 512],
                                         start=(hh == 0), stop=(hh == NH - 1)),
                                     waits=(w0 if h == 0 else ()))
                ev = prog.op(Sc, lambda e, t=tb, f=fg, s=sl: e.activation(
                                 acc[:, t * D + f * 512: t * D + f * 512 + 512],
                                 po[s][:, :], AF.Copy),
                             waits=[(T, m_last)] +
                                   ([(V, env["p2V"]), ("gpsimd", env["p2G"])]
                                    if (fg == 0 and tb == 0) else []))
                po_free[sl] = [(Sc, ev)]
                ev_last = ev
            wob_mm_last[wp] = m_last
            if fg + 2 < 4:
                load_wo(fg + 2)
        if debug:
            stv = prog.dma("sync", lambda e: e.dma_start(out=dram["d_acc"][:, :], in_=acc[:, :]),
                           sem="st", waits=[(Sc, ev_last)])
            prog.op("sync", lambda e: e.nop(), waits=[("st", stv)])


def _emit_ffn(env, fxb=None):
    nc, prog, dram = env["nc"], env["prog"], env["dram"]
    xnfm, acc = env["xnfm"], env["acc"]
    do_final = fxb is not None
    rstd, nmr = env["rstd"], env["nmr"]
    gsc, usc = env["gsc"], env["usc"]
    ones_cb, ones_rf, n128 = env["ones_cb"], env["ones_rf"], env["n128"]
    ones_bb = env["ones_bb"]
    debug = env["debug"]
    V, Sc, T, G = "vector", "scalar", "tensor", "gpsimd"
    gbp, ubp, dwp, dscp = dram["gbp"], dram["ubp"], dram["dwp"], dram["dscp"]

    with ExitStack() as ctx:
        Ls = lambda *a: ctx.enter_context(nc.sbuf_tensor(*a, side="left"))
        sgp = Ls("sgp", [128, 512], F32)
        sup = Ls("sup", [128, 512], F32)
        dscb = Ls("dscb", [128, 2048], BF16)
        hsr = Ls("hsr", [1, 512], BF16)
        if do_final:
            xst = Ls("xst0", [128, 512], F32)
            sst = ctx.enter_context(nc.psum_tensor("pfs", [128, 512], F32))
            r1p = Ls("r1p", [128, 16], F32)
        hfm = Ls("hfm", [128, NI * 512], BF16)
        pg = [ctx.enter_context(nc.psum_tensor(f"pg{i}", [128, 512], F32))
              for i in range(2)]
        pu = [ctx.enter_context(nc.psum_tensor(f"pu{i}", [128, 512], F32))
              for i in range(2)]
        phs = ctx.enter_context(nc.psum_tensor("phs", [128, 512], F32))
        ppB = ctx.enter_context(nc.psum_tensor("ppB", [128, 512], F32))
        pd = [pg[0], pg[1], pu[0], pu[1]]

        mprev = 0
        cprev = 0
        with ExitStack() as c0:
            dsrow = c0.enter_context(nc.sbuf_tensor("dsrow", [1, 512], F32, side="left"))
            for piece in range(4):
                ld = prog.dma("sync", lambda e, p=piece: e.dma_start(
                                  out=dsrow[:, :], in_=dscp[p:p + 1, :]),
                              sem="ldd", waits=[(T, mprev)])
                mprev = prog.op(T, lambda e: e.matmul(ppB[:, :], ones_rf[:, :],
                                                      dsrow[0:1, :], start=True, stop=True),
                                waits=[("ldd", ld), (V, cprev)])
                cprev = prog.op(V, lambda e, p=piece: e.tensor_copy(
                                    dscb[:, p * 512:(p + 1) * 512], ppB[:, :]),
                                waits=[(T, mprev)])
        dscb_ready = cprev
        ax_prev = 0
        ad_prev = 0
        ga_tab = [0] * 4
        if do_final:
            r1p_seq = prog.op(V, lambda e: e.tensor_scalar(r1p[:, :], rstd[:, :],
                                                           1.0, None, ALU.add))
            outp = dram["out"]
        wstage = Ls("wstage", [128, 16384], BF16)
        gb2 = [wstage[:, p * 4096:(p + 1) * 4096] for p in range(2)]
        ub2 = [wstage[:, 8192 + p * 4096: 8192 + (p + 1) * 4096] for p in range(2)]
        dw2 = [wstage[:, q * 8192:(q + 1) * 8192] for q in range(2)]

        sg_free = 0
        h_free = 0
        acc_g = 0
        he_prev = 0
        hs_prev = 0
        gb_mm_last = [0, 0]
        dw_mm_last = [mprev, mprev]
        pg_free = [[], []]
        pu_free = [[], []]
        pd_free = [[], [], [], []]
        phs_free = []
        tmp_free = 0
        for tg in range(2):
            pending_hs = []
            gb_ready = [0, 0]
            ub_ready = [0, 0]
            gh = 0
            for isc in range(32):
                par = isc % 2
                gb_ready[par] = prog.dma("sync",
                                         lambda e, i=isc, p=par: e.dma_start(
                                             out=gb2[p], in_=gbp[i, :, :]),
                                         sem=f"ldg{par}",
                                         waits=[(T, gb_mm_last[par]),
                                                (T, dw_mm_last[0])])
                ub_ready[par] = prog.dma("sync",
                                         lambda e, i=isc, p=par: e.dma_start(
                                             out=ub2[p], in_=ubp[i, :, :]),
                                         sem=f"ldu{par}",
                                         waits=[(T, gb_mm_last[par]),
                                                (T, dw_mm_last[1])])
                for ic2 in range(2):
                    ic = isc * 2 + ic2
                    psl = ic % 2
                    m = None
                    for hc in range(ND):
                        wts = []
                        if hc == 0:
                            wts = [(f"ldg{par}", gb_ready[par])] + pg_free[psl]
                        m = prog.op(T, lambda e, h=hc, i2=ic2, p=par, t=tg, s=psl: e.matmul(
                                        pg[s][:, :],
                                        gb2[p][:, h * 256 + i2 * 128: h * 256 + i2 * 128 + 128],
                                        xnfm[:, h * 1024 + t * 512: h * 1024 + t * 512 + 512],
                                        start=(h == 0), stop=(h == ND - 1)),
                                    waits=wts)
                    mg = m
                    if len(pending_hs) > 1:
                        pending_hs.pop(0)()
                    a1 = prog.op(Sc, lambda e, i=ic, s=psl: e.activation(
                                     sgp[:, :], pg[s][:, :], AF.Silu,
                                     scale=gsc[:, i:i + 1]),
                                 waits=[(T, mg), (G, sg_free)])
                    m = None
                    for hc in range(ND):
                        wts = []
                        if hc == 0:
                            wts = [(f"ldu{par}", ub_ready[par])] + pu_free[psl]
                        m = prog.op(T, lambda e, h=hc, i2=ic2, p=par, t=tg, s=psl: e.matmul(
                                        pu[s][:, :],
                                        ub2[p][:, h * 256 + i2 * 128: h * 256 + i2 * 128 + 128],
                                        xnfm[:, h * 1024 + t * 512: h * 1024 + t * 512 + 512],
                                        start=(h == 0), stop=(h == ND - 1)),
                                    waits=wts)
                    mu_ = m
                    gb_mm_last[par] = mu_
                    a2 = prog.op(Sc, lambda e, i=ic, s=psl: e.activation(
                                     sup[:, :], pu[s][:, :], AF.Identity,
                                     scale=usc[:, i:i + 1]),
                                 waits=[(T, mu_), (G, sg_free)])
                    gh = prog.op(G, lambda e, i=ic: e.tensor_mul(
                                     hfm[:, i * 512:(i + 1) * 512], sgp[:, :], sup[:, :]),
                                 waits=[(Sc, a2), (Sc, a1), (T, h_free)])
                    sg_free = gh
                    pg_free[psl] = [(Sc, a1)]
                    pu_free[psl] = [(Sc, a2)]

                    def hsum(i=ic, gh=gh):
                        nonlocal hs_prev
                        hs_prev = prog.op(T, lambda e, i2=i: e.matmul(
                                              phs[:, :], ones_bb[:, :],
                                              hfm[:, i2 * 512:(i2 + 1) * 512],
                                              start=(i2 == 0), stop=(i2 == NI - 1)),
                                          waits=([(V, he_prev)] + phs_free
                                                 if i == 0 else []) + [(G, gh)])
                    pending_hs.append(hsum)
            for f in pending_hs:
                f()
            pending_hs.clear()
            phs_free = []
            he = prog.op(V, lambda e: e.tensor_copy(hsr[:, :], phs[0:1, :]),
                         waits=[(T, hs_prev)])
            he_prev = he
            vbar = prog.v(V)
            scbar = prog.v(Sc)

            for fg in range(4):
                stops = [None] * 4
                for pc in range(4):
                    par = pc % 2
                    lw = prog.dma("sync", lambda e, f=fg, p=pc, q=par: e.dma_start(
                                      out=dw2[q], in_=dwp[f * 4 + p, :, :]),
                                  sem=f"lddw{par}",
                                  waits=[(T, dw_mm_last[par]),
                                         (T, gb_mm_last[0]), (T, gb_mm_last[1])])
                    for tb in range(4):
                        for icp in range(16):
                            wts = []
                            if icp == 0 and tb == 0:
                                wts = [(f"lddw{par}", lw), (G, gh), (V, he)]
                            if icp == 0 and pc == 0:
                                wts = wts + pd_free[tb] + \
                                      ([(V, vbar), (Sc, scbar)] if fg == 0 else [])
                            m = prog.op(T, lambda e, f=fg, p=pc, q=par, t=tb, i=icp: e.matmul(
                                            pd[t][:, :],
                                            hfm[:, (p * 16 + i) * 512 + t * 128:
                                                (p * 16 + i) * 512 + t * 128 + 128],
                                            dw2[q][:, i * 512:(i + 1) * 512],
                                            start=(p == 0 and i == 0), stop=False),
                                        waits=wts)
                        if pc == 3:
                            stops[tb] = prog.op(T, lambda e, t=tb: e.matmul(
                                                    pd[t][:, :],
                                                    hsr[0:1, t * 128:(t + 1) * 128],
                                                    n128[0:1, :],
                                                    start=False, stop=True))
                    dw_mm_last[par] = prog.v(T)
                for tb in range(4):
                    tv = prog.op(V, lambda e, t=tb, f=fg: e.tensor_mul(
                                     sgp[:, :], pd[t][:, :], dscb[:, f * 512:(f + 1) * 512]),
                                 waits=[(T, stops[tb]), (G, tmp_free), (G, gh),
                                        (V, dscb_ready)])
                    ga = prog.op(G, lambda e, t=tb, f=fg, tt=tg: e.tensor_add(
                                     acc[:, (tt * 4 + t) * D + f * 512:
                                         (tt * 4 + t) * D + f * 512 + 512],
                                     acc[:, (tt * 4 + t) * D + f * 512:
                                         (tt * 4 + t) * D + f * 512 + 512],
                                     sgp[:, :]),
                                 waits=[(V, tv)])
                    tmp_free = ga
                    acc_g = ga
                    ga_tab[tb] = ga
                    pd_free[tb] = [(V, tv)]
                if do_final:
                    for t4 in range(4):
                        tbg = tg * 4 + t4
                        ld = prog.dma("scalar", lambda e, tt=tbg, f=fg: e.dma_start(
                                          out=xst[:, :],
                                          in_=fxb[tt * 128:(tt + 1) * 128,
                                                  f * 512:(f + 1) * 512]),
                                      sem="ldp0", waits=[(V, ax_prev)])
                        ax = prog.op(V, lambda e, tt=tbg: e.tensor_scalar(
                                         sst[:, :], xst[:, :], r1p[:, tt:tt + 1],
                                         nmr[:, tt:tt + 1], ALU.mult, ALU.add),
                                     waits=[("ldp0", ld), (V, ad_prev),
                                            (V, r1p_seq)])
                        ad = prog.op(V, lambda e, tt=tbg, f=fg: e.tensor_add(
                                         acc[:, tt * D + f * 512: tt * D + f * 512 + 512],
                                         acc[:, tt * D + f * 512: tt * D + f * 512 + 512],
                                         sst[:, :]),
                                     waits=[(V, ax), (G, ga_tab[t4])])
                        prog.dma("scalar", lambda e, tt=tbg, f=fg: e.dma_start(
                                     out=outp[tt * 128:(tt + 1) * 128,
                                              f * 512:(f + 1) * 512],
                                     in_=acc[:, tt * D + f * 512:
                                             tt * D + f * 512 + 512]),
                                 sem="stp", waits=[(V, ad)])
                        ax_prev = ax
                        ad_prev = ad
            sg_free = tmp_free
            h_free = prog.v(T)
            phs_free = [(V, he)]
            pg_free[0] = pd_free[0]
            pg_free[1] = pd_free[1]
            pu_free[0] = pd_free[2]
            pu_free[1] = pd_free[3]
        if do_final:
            prog.op("sync", lambda e: e.nop(), waits=[("stp", prog.dcnt["stp"])])
        if debug and not do_final:
            stv = prog.dma("sync", lambda e: e.dma_start(out=dram["d_acc2"][:, :], in_=acc[:, :]),
                           sem="st", waits=[(G, acc_g)])
            prog.op("sync", lambda e: e.nop(), waits=[("st", stv)])


def _emit_final(env, xb):
    nc, prog, dram = env["nc"], env["prog"], env["dram"]
    acc = env["acc"]
    rstd, nmr = env["rstd"], env["nmr"]
    V, Sc, T, G = "vector", "scalar", "tensor", "gpsimd"
    out = dram["out"]

    with ExitStack() as ctx:
        Ls = lambda *a: ctx.enter_context(nc.sbuf_tensor(*a, side="left"))
        xt = [Ls(f"fxt{i}", [128, D], F32) for i in range(2)]
        xnt = [Ls(f"fxn{i}", [128, D], F32) for i in range(2)]
        fin = [Ls(f"fin{i}", [128, D], F32) for i in range(2)]
        xt_free = [0, 0]
        xn_free = [0, 0]
        fin_free = [0, 0]
        for tb in range(NTO):
            sl = tb % 2
            ld = prog.dma("sync", lambda e, t=tb, s=sl: e.dma_start(
                              out=xt[s][:, :], in_=xb[t * 128:(t + 1) * 128, :]),
                          sem=f"ldf{sl}", waits=[(V, xt_free[sl])])
            ax = prog.op(Sc, lambda e, t=tb, s=sl: e.activation(
                             xnt[s][:, :], xt[s][:, :], AF.Identity,
                             bias=nmr[:, t:t + 1], scale=rstd[:, t:t + 1]),
                         waits=[(f"ldf{sl}", ld), (V, xn_free[sl])])
            d1 = prog.op(V, lambda e, s=sl: e.tensor_add(fin[s][:, :], xt[s][:, :],
                                                         xnt[s][:, :]),
                         waits=[(Sc, ax)] + ([(f"stf{sl}", fin_free[sl])]
                                            if fin_free[sl] else []))
            d2 = prog.op(V, lambda e, t=tb, s=sl: e.tensor_add(
                             fin[s][:, :], fin[s][:, :], acc[:, t * D:(t + 1) * D]),
                         waits=[(V, d1)])
            fin_free[sl] = prog.dma("sync", lambda e, t=tb, s=sl: e.dma_start(
                                        out=out[t * 128:(t + 1) * 128, :], in_=fin[s][:, :]),
                                    sem=f"stf{sl}", waits=[(V, d2)])
            xt_free[sl] = d1
            xn_free[sl] = d1
        prog.op("sync", lambda e: e.nop(),
                waits=[("stf0", prog.dcnt.get("stf0", 0)), ("stf1", prog.dcnt.get("stf1", 0))])



def host_prep(inputs):
    X = np.asarray(inputs["L0_attn_X"], dtype=np.float32)
    ln_w = np.asarray(inputs["L0_attn_ln_w"], dtype=np.float32)
    ln_b = np.asarray(inputs["L0_attn_ln_b"], dtype=np.float32)
    W_in = np.ascontiguousarray(np.asarray(inputs["L0_attn_W_in"], dtype=np.float32))
    W_out = np.asarray(inputs["L0_attn_W_out"], dtype=np.float32)
    gw = np.asarray(inputs["L0_ffn_gate_w"])
    uw = np.asarray(inputs["L0_ffn_up_w"])
    dw = np.asarray(inputs["L0_ffn_down_w"])
    gs = np.asarray(inputs["L0_ffn_gate_scale"], dtype=np.float32)
    us = np.asarray(inputs["L0_ffn_up_scale"], dtype=np.float32)
    ds = np.asarray(inputs["L0_ffn_down_scale"], dtype=np.float32)

    wip = np.empty((24, 128, 4096), np.float32)
    for g in range(8):
        for w, base in enumerate((0, 2048, 4096)):
            blk = W_in[:, base + 2 * g * 128: base + 2 * g * 128 + 256]
            wip[g * 3 + w] = blk.reshape(16, 128, 256).transpose(1, 0, 2).reshape(128, 4096)
    wop = np.empty((4, 128, 8192), np.float32)
    for fg in range(4):
        blk = W_out[:, fg * 512:(fg + 1) * 512]
        wop[fg] = blk.reshape(16, 128, 512).transpose(1, 0, 2).reshape(128, 8192)

    def pack_gu(wq):
        a = wq.astype(np.float32).astype(BF)
        a = a.reshape(32, 256, 16, 128)
        return np.ascontiguousarray(a.transpose(0, 3, 2, 1)).reshape(32, 128, 4096)

    gbp = pack_gu(gw)
    ubp = pack_gu(uw)
    a = dw.astype(np.float32).astype(BF)
    a = a.reshape(4, 512, 4, 16, 128)
    dwp = np.ascontiguousarray(a.transpose(0, 2, 4, 3, 1)).reshape(16, 128, 8192)

    shared = dict(
        wip=wip, wop=wop, gbp=gbp, ubp=ubp, dwp=dwp,
        lnwc=np.ascontiguousarray(ln_w.reshape(ND, 128).T),
        lnbc=np.ascontiguousarray(ln_b.reshape(ND, 128).T),
        gscp=np.ascontiguousarray(gs.reshape(NI, 128).T),
        uscp=np.ascontiguousarray(us.reshape(NI, 128).T),
        dscp=np.ascontiguousarray(ds.reshape(4, 512)),
        idf=np.eye(128, dtype=np.float32),
    )
    pm = np.zeros((64, 64), np.float32)
    for t in range(32):
        pm[2 * t + 1, 2 * t] = -1.0
        pm[2 * t, 2 * t + 1] = 1.0
    shared["pmat"] = pm.astype(BF)

    inv_freq = (1.0 / (10000.0 ** (np.arange(0, 64, 2, dtype=np.float32)
                                   / np.float32(64.0)))).astype(np.float32)
    in_maps = []
    for c in range(8):
        b, qh = c // 2, c % 2
        off = qh * 1024
        pos = (np.arange(S) + off) % S
        ang = pos[:, None].astype(np.float32) * inv_freq[None, :]
        ang2 = np.repeat(ang, 2, axis=1)
        m = dict(shared)
        m["cs"] = np.ascontiguousarray(np.concatenate(
            [np.cos(ang2).T, np.sin(ang2).T], axis=0).astype(np.float32))
        m["xb"] = np.ascontiguousarray(np.concatenate([X[b, off:], X[b, :off]], axis=0))
        in_maps.append(m)
    return in_maps


def kernel(**inputs):
    in_maps = host_prep(inputs)
    nc = build_program(stage=STAGE, debug=DEBUG)
    res = run_bass_kernel_spmd(nc, in_maps, core_ids=list(range(8)))
    Y = np.empty((B, S, D), np.float32)
    for c in range(8):
        b, qh = c // 2, c % 2
        off = qh * 1024
        Y[b, off:off + 1024, :] = res.results[c]["out"]
    return Y


# version 28
# speedup vs baseline: 1.2206x; 1.0150x over previous
import math
from contextlib import ExitStack

import numpy as np
import ml_dtypes
import concourse.bass as bass
import concourse.mybir as mybir
from concourse.bass_utils import run_bass_kernel_spmd

F32 = mybir.dt.float32
BF16 = mybir.dt.bfloat16
AF = mybir.ActivationFunctionType
ALU = mybir.AluOpType
AX = mybir.AxisListType
BF = ml_dtypes.bfloat16

B, S, D, NH, DH, II = 4, 2048, 2048, 16, 128, 8192
ND = D // 128
NT = S // 128
NTO = 8
NI = II // 128
SCALE = 1.0 / math.sqrt(DH)
EPS = 1e-5
INV_D = 1.0 / D

STAGE = 5
DEBUG = False
P2_SUB = "all"

ENGS = ("sync", "tensor", "vector", "scalar", "gpsimd")


class Prog:

    def __init__(self, nc, ctx):
        self.nc = nc
        self.ctx = ctx
        self.ops = {e: [] for e in ENGS}
        self.seq = {e: 0 for e in ENGS}
        self.dcnt = {}
        self.marks = {e: set() for e in ENGS}
        self.sems = {}

    def _add(self, eng, fn, waits, dsem):
        ws = []
        for name, val in waits:
            if not val or val <= 0:
                continue
            ws.append((name, val))
            if name in ENGS:
                self.marks[name].add(val)
        self.seq[eng] += 1
        self.ops[eng].append((fn, tuple(ws), self.seq[eng], dsem))

    def op(self, eng, fn, waits=()):
        self._add(eng, fn, waits, None)
        return self.seq[eng]

    def dma(self, eng, fn, sem, waits=()):
        self.dcnt[sem] = self.dcnt.get(sem, 0) + 16
        self._add(eng, fn, waits, sem)
        return self.dcnt[sem]

    def v(self, eng):
        return self.seq[eng]

    def barrier(self):
        self.dcnt["bar"] = self.dcnt.get("bar", 0) + len(ENGS)
        target = self.dcnt["bar"]
        for e in ENGS:
            self.seq[e] += 1
            self.ops[e].append(("BARRIER", target, self.seq[e], None))

    def emit(self):
        nc = self.nc
        rank = {}
        for e in ENGS:
            rank[e] = {s: i + 1 for i, s in enumerate(sorted(self.marks[e]))}
        for e in ENGS:
            self.sems["q" + e] = self.ctx.enter_context(nc.semaphore("q" + e))
        for name in self.dcnt:
            self.sems[name] = self.ctx.enter_context(nc.semaphore(name))

        def mk(e):
            ops = self.ops[e]
            sems = self.sems
            marks = self.marks[e]

            def body(eng):
                for fn, waits, eseq, dsem in ops:
                    if fn == "BARRIER":
                        if e == "sync":
                            eng.sem_inc(sems["bar"], 1)
                        else:
                            eng.drain().then_inc(sems["bar"], 1)
                        eng.wait_ge(sems["bar"], waits)
                        continue
                    for wname, wval in waits:
                        if wname in ENGS:
                            eng.wait_ge(sems["q" + wname], rank[wname][wval])
                        else:
                            eng.wait_ge(sems[wname], wval)
                    inst = fn(eng)
                    if dsem is not None:
                        inst.then_inc(sems[dsem], 16)
                    if eseq in marks:
                        inst.then_inc(sems["q" + e], 1)
            return body

        with nc.Block() as blk:
            for e in ENGS:
                getattr(blk, e)(mk(e))



def build_program(stage=STAGE, debug=DEBUG):
    nc = bass.Bass("TRN2", target_bir_lowering=False)

    dram = {}

    def din(name, shape, dtype):
        dram[name] = nc.dram_tensor(name, shape, dtype, kind="ExternalInput").ap()
        return dram[name]

    def dout(name, shape, dtype):
        dram[name] = nc.dram_tensor(name, shape, dtype, kind="ExternalOutput").ap()
        return dram[name]

    xb = din("xb", [S, D], F32)
    din("wip", [24, 128, 4096], F32)
    din("wop", [4, 128, 8192], F32)
    din("gbp", [32, 128, 4096], BF16)
    din("ubp", [32, 128, 4096], BF16)
    din("dwp", [16, 128, 8192], BF16)
    din("cs", [128, S], F32)
    din("lnwc", [128, ND], F32)
    din("lnbc", [128, ND], F32)
    din("gscp", [128, NI], F32)
    din("uscp", [128, NI], F32)
    din("dscp", [4, 512], F32)
    din("pmat", [64, 64], BF16)
    din("idf", [128, 128], F32)

    dout("out", [NTO * 128, D], F32)

    if debug:
        dout("d_xn2", [128, ND * S], BF16)
        dout("d_xnfm", [128, ND * 1024], BF16)
        dout("d_rstd", [128, NT], F32)
        dout("d_nmr", [128, NT], F32)
        if stage >= 2:
            dout("d_q", [128, 2 * 1024], BF16)
            dout("d_k", [128, 2 * S], BF16)
            dout("d_v", [128, NT * 256], BF16)
            dout("d_bias", [128, 2], F32)
            dout("d_ctx", [128, NH * 1024], BF16)
        if stage >= 3:
            dout("d_acc", [128, NTO * D], F32)
        if stage >= 4:
            dout("d_acc2", [128, NTO * D], F32)

    with ExitStack() as top:
        prog = Prog(nc, top)
        L = lambda *a: top.enter_context(nc.sbuf_tensor(*a, side="left"))

        env = {"nc": nc, "prog": prog, "dram": dram, "debug": debug, "top": top,
               "stage": stage}

        env["lnw"] = L("lnw_sb", [128, ND], F32)
        env["lnb"] = L("lnb_sb", [128, ND], F32)
        env["gsc"] = L("gsc_sb", [128, NI], F32)
        env["usc"] = L("usc_sb", [128, NI], F32)
        env["pm"] = L("pm_sb", [64, 64], BF16)
        env["idf"] = L("idf_sb", [128, 128], F32)
        env["ones_cb"] = L("ones_cb", [128, 1], BF16)
        env["ones_bb"] = L("ones_bb", [128, 128], BF16)
        env["ones_rf"] = L("ones_rf", [1, 128], F32)
        env["n128"] = L("n128_r", [1, 512], BF16)
        env["eps_c"] = L("eps_c", [128, 1], F32)
        env["rstd"] = L("rstd_sb", [128, NT], F32)
        env["nmr"] = L("nmr_sb", [128, NT], F32)
        env["scr"] = L("scr_c", [128, 32], F32)
        env["scr1"] = L("scr_1", [1, 8], F32)
        env["xnfm"] = L("xnfm", [128, ND * 1024], BF16)

        with ExitStack() as sCtx:
          env["sCtx"] = sCtx
          with ExitStack() as sA:
            env["xn2"] = sA.enter_context(nc.sbuf_tensor("xn2f", [128, ND * S], BF16, side="left"))
            env["cs"] = sA.enter_context(nc.sbuf_tensor("cs_sb", [128, S], F32, side="left"))
            ldc = []
            for i, (name, dst) in enumerate([
                    ("lnwc", env["lnw"]), ("lnbc", env["lnb"]),
                    ("gscp", env["gsc"]), ("uscp", env["usc"]),
                    ("pmat", env["pm"]), ("idf", env["idf"]),
                    ("cs", env["cs"])]):
                prog.dma("sync", lambda e, d=dst, s=dram[name]: e.dma_start(out=d[:, :], in_=s[:, :]),
                         sem=f"ldc{i}")
                ldc.append((f"ldc{i}", 16))
            env["ldc"] = ldc
            for dst, valc in [(env["ones_cb"], 1.0), (env["ones_bb"], 1.0), (env["ones_rf"], 1.0),
                              (env["n128"], -128.0), (env["eps_c"], EPS)]:
                prog.op("vector", lambda e, d=dst, vv=valc: e.memset(d[:, :], vv))
            env["dvc_done"] = prog.v("vector")

            _emit_ln(env, xb)
            if debug:
                prog.barrier()
                prog.dma("sync", lambda e: e.dma_start(out=dram["d_xn2"][:, :], in_=env["xn2"][:, :]), sem="st")
                prog.dma("sync", lambda e: e.dma_start(out=dram["d_xnfm"][:, :], in_=env["xnfm"][:, :]), sem="st")
                prog.dma("sync", lambda e: e.dma_start(out=dram["d_rstd"][:, :], in_=env["rstd"][:, :]), sem="st")
                prog.dma("sync", lambda e: e.dma_start(out=dram["d_nmr"][:, :], in_=env["nmr"][:, :]), sem="st")

            if stage >= 2:
                _emit_attention(env)
                if stage == 2:
                    prog.barrier()
          if stage >= 3:
            env["acc"] = top.enter_context(nc.sbuf_tensor("out_acc", [128, NTO * D], F32, side="left"))
            _emit_outproj(env)
            prog.barrier()
        if stage >= 4:
            _emit_ffn(env, xb if stage >= 5 else None)
            prog.barrier()

        prog.barrier()
        st_total = prog.dcnt.get("st", 0)
        if st_total:
            prog.op("sync", lambda e: e.nop(), waits=[("st", st_total)])
        prog.emit()
    return nc



def _emit_ln(env, xb):
    nc, prog = env["nc"], env["prog"]
    scr, eps_c = env["scr"], env["eps_c"]
    rstd, nmr = env["rstd"], env["nmr"]
    xn2, xnfm = env["xn2"], env["xnfm"]
    lnw, lnb, idf = env["lnw"], env["lnb"], env["idf"]
    V, Sc, T = "vector", "scalar", "tensor"

    with ExitStack() as ctx:
        Ls = lambda *a: ctx.enter_context(nc.sbuf_tensor(*a, side="left"))
        NC = 4
        xt = [Ls(f"xt{i}", [128, D], F32) for i in range(NC)]
        xnt = [Ls(f"xnt{i}", [128, D], F32) for i in range(NC)]
        xn2t = [Ls(f"xn2t{i}", [128, D], F32) for i in range(NC)]
        pst = [ctx.enter_context(nc.psum_tensor(f"pst{i}", [128, 512], F32, side="right"))
               for i in range(4)]

        xt_free = [0] * NC
        src_free = [0] * NC
        slot_ev = [None] * 4
        col1_free = [0] * NC
        prev_d_last = [0] * NC
        prev_a_last = [0] * NC
        sq_prev = [0] * NC
        gp_last = [0] * NC
        env["x2ev"] = [0] * NT
        res = {}

        def chain(tt):
            sl = tt % NC
            cb = 8 * sl
            cc = [scr[:, cb + i:cb + i + 1] for i in range(8)]
            ldx = prog.dma("sync",
                           lambda e, t=tt, s=sl: e.dma_start(
                               out=xt[s][:, :], in_=xb[t * 128:(t + 1) * 128, :]),
                           sem=f"ldx{sl}", waits=[(Sc, xt_free[sl]), (V, prev_d_last[sl]),
                                                  ("gpsimd", gp_last[sl])])
            d1 = prog.op(V, lambda e, s=sl, c=cc: e.reduce_sum(c[0], xt[s][:, :], axis=AX.X),
                         waits=[(f"ldx{sl}", ldx), (V, prev_d_last[sl]), (Sc, prev_a_last[sl]),
                                ("gpsimd", gp_last[sl])]
                               + (env["ldc"] if tt == 0 else []))
            a1 = prog.op(Sc, lambda e, s=sl, c=cc: e.activation(xnt[s][:, :], xt[s][:, :],
                                                                AF.Square, accum_out=c[1]),
                         waits=[(f"ldx{sl}", ldx), (V, col1_free[sl]), (Sc, max(sq_prev)),
                                (T, src_free[sl])])
            sq_prev[sl] = a1
            yield
            d2 = prog.op(V, lambda e, c=cc: e.tensor_scalar(c[2], c[0], INV_D, None, ALU.mult),
                         waits=[(V, d1)])
            d3 = prog.op(V, lambda e, c=cc: e.tensor_scalar(c[3], c[1], INV_D, None, ALU.mult),
                         waits=[(Sc, a1), (V, d2)])
            col1_free[sl] = d3
            d4 = prog.op(V, lambda e, c=cc: e.tensor_scalar(c[4], c[2], c[2], None, ALU.mult),
                         waits=[(V, d3)])
            d5 = prog.op(V, lambda e, c=cc: e.tensor_sub(c[3], c[3], c[4]), waits=[(V, d4)])
            yield
            a2 = prog.op(Sc, lambda e, c=cc: e.activation(c[5], c[3], AF.Ln, bias=eps_c[:, :]),
                         waits=[(V, d5)])
            a3 = prog.op(Sc, lambda e, t=tt, c=cc: e.activation(rstd[:, t:t + 1], c[5],
                                                                AF.Exp, scale=-0.5),
                         waits=[(Sc, a2)])
            yield
            d6 = prog.op(V, lambda e, t=tt, c=cc: e.tensor_scalar(nmr[:, t:t + 1], c[2],
                                                                  rstd[:, t:t + 1], -1.0,
                                                                  ALU.mult, ALU.mult),
                         waits=[(Sc, a3)])
            d6b = prog.op("gpsimd", lambda e, t=tt, s=sl: e.tensor_scalar(
                              xnt[s][:, :], xt[s][:, :], rstd[:, t:t + 1],
                              nmr[:, t:t + 1], ALU.mult, ALU.add),
                          waits=[(V, d6), (T, src_free[sl]), (Sc, a1)])
            yield
            a5 = prog.op(Sc, lambda e, s=sl, c=cc: e.activation(xt[s][:, :], xnt[s][:, :],
                                                                AF.Square, accum_out=c[1]),
                         waits=[(V, col1_free[sl]), ("gpsimd", d6b), (Sc, max(sq_prev))])
            sq_prev[sl] = a5
            yield
            d9 = prog.op(V, lambda e, c=cc: e.tensor_scalar(c[3], c[1], INV_D, None, ALU.mult),
                         waits=[(Sc, a5)])
            col1_free[sl] = d9
            yield
            a6 = prog.op(Sc, lambda e, c=cc: e.activation(c[5], c[3], AF.Ln, bias=eps_c[:, :]),
                         waits=[(V, d9)])
            a7 = prog.op(Sc, lambda e, c=cc: e.activation(c[6], c[5], AF.Exp, scale=-0.5),
                         waits=[(Sc, a6)])
            yield
            a8 = prog.op("gpsimd", lambda e, s=sl, c=cc: e.tensor_scalar(
                             xn2t[s][:, :], xnt[s][:, :], c[6], 0.0,
                             ALU.mult, ALU.add),
                         waits=[(Sc, a7), (T, src_free[sl])])
            prev_d_last[sl], prev_a_last[sl] = d9, a7
            xt_free[sl] = a5
            gp_last[sl] = a8
            res[tt] = (sl, a8, d6b)

        def transposes(tt):
            sl, a8, d6b = res[tt]
            x2ev_pair = [0, 0]
            work = [("x2", dc) for dc in range(ND)]
            if tt < NTO:
                work += [("xn", dc) for dc in range(ND)]
            last_t = 0
            for j, (kind, dc) in enumerate(work):
                pslot = j % 4
                p = pst[pslot]
                srcbuf = xn2t[sl] if kind == "x2" else xnt[sl]
                wts = [("gpsimd", a8)] if kind == "x2" else [("gpsimd", d6b)]
                if slot_ev[pslot] is not None:
                    wts.append(slot_ev[pslot])
                mm_t = prog.op(T, lambda e, sb=srcbuf, d=dc, pp=p: e.transpose(
                                   pp[:, 0:128], sb[:, d * 128:(d + 1) * 128], idf[:, :]),
                               waits=wts)
                last_t = mm_t
                if kind == "x2":
                    if dc % 2 == 0:
                        ev = prog.op(V, lambda e, t=tt, d=dc, pp=p: e.tensor_scalar(
                                         xn2[:, d * S + t * 128: d * S + (t + 1) * 128],
                                         pp[:, 0:128], lnw[:, d:d + 1], lnb[:, d:d + 1],
                                         ALU.mult, ALU.add),
                                     waits=[(T, mm_t)])
                        slot_ev[pslot] = (V, ev)
                    else:
                        ev = prog.op(Sc, lambda e, t=tt, d=dc, pp=p: e.activation(
                                         xn2[:, d * S + t * 128: d * S + (t + 1) * 128],
                                         pp[:, 0:128], AF.Identity,
                                         bias=lnb[:, d:d + 1], scale=lnw[:, d:d + 1]),
                                     waits=[(T, mm_t)])
                        slot_ev[pslot] = (Sc, ev)
                    x2ev_pair[dc % 2] = ev
                else:
                    ev = prog.op(V, lambda e, t=tt, d=dc, pp=p: e.tensor_copy(
                                     xnfm[:, d * 1024 + t * 128: d * 1024 + (t + 1) * 128],
                                     pp[:, 0:128]),
                                 waits=[(T, mm_t)])
                    slot_ev[pslot] = (V, ev)
            env["x2ev"][tt] = [(V, x2ev_pair[0]), (Sc, x2ev_pair[1])]
            src_free[sl] = last_t

        for k in range(NT // 4):
            alive = [chain(4 * k + i) for i in range(4)]
            while alive:
                for gch in list(alive):
                    try:
                        next(gch)
                    except StopIteration:
                        alive.remove(gch)
            for i in range(4):
                transposes(4 * k + i)
        env["p1T"] = prog.v("tensor")


def _emit_attention(env):
    nc, prog, dram = env["nc"], env["prog"], env["dram"]
    xn2f = env["xn2"]
    cs, pm = env["cs"], env["pm"]
    ones_cb, ones_rf, scr1 = env["ones_cb"], env["ones_rf"], env["scr1"]
    ones_bb = env["ones_bb"]
    debug = env["debug"]
    V, Sc, T = "vector", "scalar", "tensor"
    wip = dram["wip"]

    env["ctx"] = env["sCtx"].enter_context(
        nc.sbuf_tensor("ctxf", [128, NH * 1024], BF16, side="right"))
    ctxf = env["ctx"]
    with ExitStack() as ctx:
        R = lambda *a: ctx.enter_context(nc.sbuf_tensor(*a, side="right"))
        wf32 = [R(f"wf32_{i}", [128, 512], F32) for i in range(2)]
        wbf = [R(f"wbf{i}", [128, 4096], BF16) for i in range(3)]
        k2 = R("k2", [128, 2 * S], BF16)
        q2 = R("q2", [128, 2 * 1024], BF16)
        v2 = R("v2", [128, NT * 256], BF16)
        ebig = ctx.enter_context(nc.sbuf_tensor("ebig", [128, 8 * 512], BF16, side="left"))
        sqb = R("sqb", [128, 1024], BF16)
        rb = R("rb", [128, 512], F32)
        biasc = R("biasc", [128, 2], F32)
        pp = [ctx.enter_context(nc.psum_tensor(f"pp{i}", [128, 512], F32))
              for i in range(2)]
        ppR = ctx.enter_context(nc.psum_tensor("ppR", [128, 512], F32))
        ppS = ctx.enter_context(nc.psum_tensor("ppS", [128, 512], F32))
        ppD = [ctx.enter_context(nc.psum_tensor(f"ppD{i}", [128, 512], F32))
               for i in range(2)]
        ppC = [ctx.enter_context(nc.psum_tensor(f"ppC{i}", [128, 512], F32))
               for i in range(2)]

        wf32_free = [0, 0]
        wbf_mm_last = [0, 0, 0]
        slab_ready = {}
        pp_free = [[], [], [], []]
        ppR_free = []
        bankF = {"S": [], "D0": [], "D1": []}
        ppC_free = [[], []]
        e_free = 0
        cv_prev = 0
        tcos_free = 0
        sq_free = []
        k2_rd = []
        v2_rd = []
        tile_i = 0

        def load_slab(j):
            for qt in range(8):
                qp = (j * 8 + qt) % 2
                w_extra = [("tensor", env["p1T"])] if (j < 3) else []
                ld = prog.dma("sync",
                              lambda e, jj=j, h=qt, q=qp: e.dma_start(
                                  out=wf32[q][:, :], in_=wip[jj, :, h * 512:(h + 1) * 512]),
                              sem=f"ldw{qp}", waits=[("gpsimd", wf32_free[qp])] + w_extra)
                wts = [(f"ldw{qp}", ld)]
                if qt == 0:
                    wts.append((T, wbf_mm_last[j % 3]))
                cv = prog.op("gpsimd", lambda e, jj=j, h=qt, q=qp: e.tensor_copy(
                                 wbf[jj % 3][:, h * 512:(h + 1) * 512], wf32[q][:, :]),
                             waits=wts)
                wf32_free[qp] = cv
            slab_ready[j] = cv

        load_slab(0)
        load_slab(1)
        load_slab(2)

        pp4 = [pp[0], pp[1], ppC[0], ppC[1]]
        for g in range(8):

            rope_last = 0
            vproj_last = 0
            rope_done = {}
            qk_jobs = [("q", hh, tg) for hh in range(2) for tg in range(2)] + \
                      [("k", hh, tg) for hh in range(2) for tg in range(4)]
            pending_rope = None
            for kind, hh, tg in qk_jobs:
                slabj = 3 * g + (0 if kind == "q" else 1)
                dst = q2 if kind == "q" else k2
                dstoff = hh * (1024 if kind == "q" else 2048) + tg * 512
                sl = tile_i % 4
                w0 = [("gpsimd", slab_ready[slabj])] + pp_free[sl] + k2_rd
                if g == 0:
                    w0 += env["x2ev"][4 * tg + 3]
                if g == 0 and tile_i in (2, 3):
                    w0 += env["x2ev"][14] + env["x2ev"][15]
                k2_rd = []
                m_last = None
                for dc in range(ND):
                    m_last = prog.op(T, lambda e, d=dc, h=hh, t=tg, s=sl, sj=slabj: e.matmul(
                                         pp4[s][:, :],
                                         wbf[sj % 3][:, d * 256 + h * 128: d * 256 + (h + 1) * 128],
                                         xn2f[:, d * S + t * 512: d * S + t * 512 + 512],
                                         start=(d == 0), stop=(d == ND - 1)),
                                     waits=(w0 if dc == 0 else ()))
                wbf_mm_last[slabj % 3] = m_last
                aev = prog.op(Sc, lambda e, s=sl, do=dstoff, dd=dst: e.activation(
                                  dd[:, do:do + 512], pp4[s][:, :], AF.Copy),
                              waits=[(T, m_last)])
                if pending_rope is not None:
                    pending_rope()
                if P2_SUB == "mm":
                    pending_rope = None
                    pp_free[sl] = [(Sc, aev)]
                    tile_i += 1
                    continue

                def rope(m_last=m_last, aev=aev, sl=sl, tg=tg, dst=dst, dstoff=dstoff,
                         key=(kind, hh, tg)):
                    nonlocal tcos_free, ppR_free, rope_last
                    mB = prog.op(T, lambda e, do=dstoff, dd=dst: e.matmul(
                                     ppR[0:64, :], pm[:, :], dd[0:64, do:do + 512],
                                     start=True, stop=True),
                                 waits=[(Sc, aev)] + ppR_free)
                    dco = prog.op(V, lambda e, t=tg, do=dstoff, dd=dst: e.tensor_mul(
                                      dd[0:64, do:do + 512], dd[0:64, do:do + 512],
                                      cs[0:64, t * 512:t * 512 + 512]),
                                  waits=[(T, mB)])
                    dt2 = prog.op(V, lambda e, t=tg: e.tensor_mul(
                                      rb[0:64, :], ppR[0:64, :], cs[64:128, t * 512:t * 512 + 512]),
                                  waits=[(T, mB), (V, tcos_free)])
                    dadd = prog.op(V, lambda e, do=dstoff, dd=dst: e.tensor_add(
                                       dd[0:64, do:do + 512], dd[0:64, do:do + 512], rb[0:64, :]),
                                   waits=[(V, dt2)])
                    rope_last = dadd
                    rope_done[key] = dadd
                    tcos_free = dadd
                    ppR_free = [(V, dt2)]
                    pp_free[sl] = [(Sc, aev)]

                pp_free[sl] = [(Sc, aev)]
                pending_rope = rope
                tile_i += 1
            slabv = 3 * g + 2
            bias_ready = [0, 0]
            pb = [ppS, ppD[0], ppD[1]]
            pbk = ["S", "D0", "D1"]
            bsteps = []
            if P2_SUB != "qkv":
                for hh in range(2):
                    def mk_head(hh=hh):
                        state = {}

                        def s_sq(kind, half):
                            def f():
                                if kind == "q":
                                    state["gate"] = prog.op("gpsimd", lambda e, h=hh: e.tensor_mul(
                                        sqb[:, 0:1024], q2[:, h * 1024:(h + 1) * 1024],
                                        q2[:, h * 1024:(h + 1) * 1024]),
                                        waits=[(V, rope_done[("q", hh, 1)])] + sq_free)
                                else:
                                    state["gate"] = prog.op("gpsimd", lambda e, h=hh, q=half: e.tensor_mul(
                                        sqb[:, :],
                                        k2[:, h * 2048 + q * 1024: h * 2048 + q * 1024 + 1024],
                                        k2[:, h * 2048 + q * 1024: h * 2048 + q * 1024 + 1024]),
                                        waits=[(V, rope_done[("k", hh, 2 * half + 1)]),
                                               (T, state["mb"])])
                            return f

                        def s_mm(j):
                            def f():
                                bank = j % 3
                                mb = prog.op(T, lambda e, b=j % 2, k=bank: e.matmul(
                                                 pb[k][:, :], ones_bb[:, :],
                                                 sqb[:, b * 512:(b + 1) * 512],
                                                 start=True, stop=True),
                                             waits=[("gpsimd", state["gate"])] + bankF[pbk[bank]])
                                state["mb"] = mb
                                rd = prog.op(V, lambda e, j_=j, k=bank: e.reduce_max(
                                                 scr1[0:1, j_:j_ + 1], pb[k][0:1, :], axis=AX.X),
                                             waits=[(T, mb)])
                                bankF[pbk[bank]] = [(V, rd)]
                                state["rd"] = rd
                            return f

                        def s_fin():
                            nonlocal_state = state

                            def f():
                                rd_last = nonlocal_state["rd"]
                                c1 = prog.op(V, lambda e: e.reduce_max(scr1[0:1, 6:7], scr1[0:1, 0:2],
                                                                       axis=AX.X), waits=[(V, rd_last)])
                                c4 = prog.op(V, lambda e: e.reduce_max(scr1[0:1, 7:8], scr1[0:1, 2:6],
                                                                       axis=AX.X), waits=[(V, c1)])
                                c5 = prog.op(V, lambda e: e.tensor_add(scr1[0:1, 6:7], scr1[0:1, 6:7],
                                                                       scr1[0:1, 7:8]), waits=[(V, c4)])
                                c6 = prog.op(V, lambda e: e.tensor_scalar(scr1[0:1, 7:8], scr1[0:1, 6:7],
                                                                          -0.5 * SCALE, None, ALU.mult),
                                             waits=[(V, c5)])
                                mbb = prog.op(T, lambda e: e.matmul(ppS[:, 0:1], ones_rf[:, :],
                                                                    scr1[0:1, 7:8], start=True, stop=True),
                                              waits=[(V, c6)] + bankF["S"])
                                bias_ready[hh] = prog.op(V, lambda e, h=hh: e.tensor_copy(
                                                             biasc[:, h:h + 1], ppS[:, 0:1]),
                                                         waits=[(T, mbb)])
                                bankF["S"] = [(V, bias_ready[hh])]
                            return f

                        steps = [s_sq("q", 0), s_mm(0), s_mm(1), s_sq("k", 0), s_mm(2),
                                 s_mm(3), s_sq("k", 1), s_mm(4), s_mm(5), s_fin()]

                        def close_sqfree():
                            nonlocal sq_free
                            sq_free = [(T, state["mb"])]
                        steps.append(close_sqfree)
                        return steps
                    bsteps += mk_head()

            bi = 0
            for tb in range(NT):
                sl = tile_i % 4
                w0 = [("gpsimd", slab_ready[slabv])] + pp_free[sl] + v2_rd
                if g == 0:
                    w0 += env["x2ev"][tb]
                v2_rd = []
                m_last = None
                for dc in range(ND):
                    m_last = prog.op(T, lambda e, d=dc, t=tb, s=sl, sj=slabv: e.matmul(
                                         pp4[s][:, 0:256],
                                         xn2f[:, d * S + t * 128: d * S + t * 128 + 128],
                                         wbf[sj % 3][:, d * 256: d * 256 + 256],
                                         start=(d == 0), stop=(d == ND - 1)),
                                     waits=(w0 if dc == 0 else ()))
                wbf_mm_last[slabv % 3] = m_last
                aev = prog.op(Sc, lambda e, s=sl, t=tb: e.activation(
                                  v2[:, t * 256:(t + 1) * 256], pp4[s][:, 0:256], AF.Copy),
                              waits=[(T, m_last)])
                vproj_last = aev
                if pending_rope is not None:
                    pending_rope()
                    pending_rope = None
                pp_free[sl] = [(Sc, aev)]
                tile_i += 1
                if tb >= 2:
                    for _ in range(2):
                        if bi < len(bsteps):
                            bsteps[bi]()
                            bi += 1
            while bi < len(bsteps):
                bsteps[bi]()
                bi += 1
            for j in range(3 * g + 3, min(3 * g + 6, 24)):
                load_slab(j)

            pps = [pp[0], pp[1], ppR]
            pp3_free = [pp_free[0], pp_free[1], ppR_free]
            sc3 = 0
            for hh in (range(2) if P2_SUB == "all" else ()):
                for qg in range(2):
                    un = (g * 4 + hh * 2 + qg) % 2
                    dkey = "D0" if un == 0 else "D1"
                    ms_seq = [0] * 16
                    ex_seq = [0] * 16
                    md_seq = [0] * 16
                    for kb in range(16):
                        sl = sc3 % 3
                        sc3 += 1
                        wts = list(pp3_free[sl])
                        if kb == 0:
                            wts += [(V, rope_last), (Sc, vproj_last)]
                        ms_seq[kb] = prog.op(T, lambda e, h=hh, q=qg, k=kb, s=sl: e.matmul(
                                                 pps[s][:, :],
                                                 k2[:, h * 2048 + k * 128: h * 2048 + (k + 1) * 128],
                                                 q2[:, h * 1024 + q * 512: h * 1024 + q * 512 + 512],
                                                 start=True, stop=True),
                                             waits=wts)
                        wts = [(T, ms_seq[kb]), (V, bias_ready[hh])]
                        if kb == 0:
                            wts.append((T, e_free))
                        if kb >= 8:
                            wts.append((T, md_seq[kb - 8]))
                        ex_seq[kb] = prog.op(Sc, lambda e, k=kb, s=sl, h=hh: e.activation(
                                                 ebig[:, (k % 8) * 512:(k % 8) * 512 + 512],
                                                 pps[s][:, :],
                                                 AF.Exp, bias=biasc[:, h:h + 1], scale=SCALE),
                                             waits=wts)
                        pp3_free[sl] = [(Sc, ex_seq[kb])]
                        if kb >= 2:
                            prog.op(T, lambda e, k=kb - 2, h=hh, u=un: e.matmul(
                                        ppC[u][:, :],
                                        v2[:, k * 256 + h * 128: k * 256 + (h + 1) * 128],
                                        ebig[:, (k % 8) * 512:(k % 8) * 512 + 512],
                                        start=(k == 0), stop=False),
                                    waits=[(Sc, ex_seq[kb - 2])] +
                                          (ppC_free[un] + pp_free[2 + un]
                                           if kb == 2 else []))
                            md_seq[kb - 2] = prog.op(T, lambda e, k=kb - 2, u=un: e.matmul(
                                        ppD[u][:, :], ones_bb[:, :],
                                        ebig[:, (k % 8) * 512:(k % 8) * 512 + 512],
                                        start=(k == 0), stop=False),
                                    waits=(bankF[dkey] if kb == 2 else ()))
                    prog.op(T, lambda e, h=hh, u=un: e.matmul(
                                ppC[u][:, :],
                                v2[:, 14 * 256 + h * 128: 14 * 256 + (h + 1) * 128],
                                ebig[:, 6 * 512:7 * 512],
                                start=False, stop=False),
                            waits=[(Sc, ex_seq[14])])
                    md_seq[14] = prog.op(T, lambda e, u=un: e.matmul(
                                ppD[u][:, :], ones_bb[:, :],
                                ebig[:, 6 * 512:7 * 512], start=False, stop=False))
                    mp_last = prog.op(T, lambda e, h=hh, u=un: e.matmul(
                                          ppC[u][:, :],
                                          v2[:, 15 * 256 + h * 128: 15 * 256 + (h + 1) * 128],
                                          ebig[:, 7 * 512:8 * 512],
                                          start=False, stop=True),
                                      waits=[(Sc, ex_seq[15])])
                    md_last = prog.op(T, lambda e, u=un: e.matmul(
                                ppD[u][:, :], ones_bb[:, :],
                                ebig[:, 7 * 512:8 * 512], start=False, stop=True))
                    md_seq[15] = md_last
                    e_free = md_last
                    rcb = prog.op(V, lambda e, u=un: e.reciprocal(rb[:, :], ppD[u][:, :]),
                                  waits=[(T, md_last), (V, cv_prev), (V, rope_last)])
                    bankF[dkey] = [(V, rcb)]
                    cv = prog.op(V, lambda e, h=hh, q=qg, gg=g, u=un: e.tensor_mul(
                                     ctxf[:, (2 * gg + h) * 1024 + q * 512:
                                          (2 * gg + h) * 1024 + q * 512 + 512],
                                     ppC[u][:, :], rb[:, :]),
                                 waits=[(V, rcb), (T, mp_last)])
                    cv_prev = cv
                    ppC_free[un] = [(V, cv)]
                    k2_rd = [(T, mp_last), (Sc, ex_seq[15])]
                    v2_rd = [(T, mp_last)]
            pp_free[0] = pp3_free[0]
            pp_free[1] = pp3_free[1]
            pp_free[2] = ppC_free[0]
            pp_free[3] = ppC_free[1]
            ppR_free = pp3_free[2]
            k2_rd = k2_rd[-2:]
            v2_rd = v2_rd[-1:]
            if P2_SUB == "all":
                tcos_free = cv_prev

            if debug and g == 0:
                stv = prog.dma("sync", lambda e: e.dma_start(out=dram["d_q"][:, :], in_=q2[:, :]),
                               sem="st", waits=[(V, rope_last)])
                stv = prog.dma("sync", lambda e: e.dma_start(out=dram["d_k"][:, :], in_=k2[:, :]),
                               sem="st", waits=[(V, rope_last)])
                stv = prog.dma("sync", lambda e: e.dma_start(out=dram["d_v"][:, :], in_=v2[:, :]),
                               sem="st", waits=[(Sc, vproj_last)])
                if P2_SUB != "qkv":
                    stv = prog.dma("sync", lambda e: e.dma_start(out=dram["d_bias"][:, :], in_=biasc[:, :]),
                                   sem="st", waits=[(V, bias_ready[1])])
                sn = prog.op("sync", lambda e: e.nop(), waits=[("st", stv)])
                k2_rd.append(("sync", sn))
                v2_rd.append(("sync", sn))
                bankF["S"].append(("sync", sn))

        if debug and P2_SUB == "all":
            stv = 0
            for blk in range(32):
                stv = prog.dma("sync", lambda e, bb=blk: e.dma_start(
                                   out=dram["d_ctx"][:, bb * 512:(bb + 1) * 512],
                                   in_=ctxf[:, bb * 512:(bb + 1) * 512]),
                               sem="st", waits=ppC_free[0] + ppC_free[1])
            prog.op("sync", lambda e: e.nop(), waits=[("st", stv)])
        env["p2T"] = prog.v("tensor")
        env["p2V"] = prog.v("vector")
        env["p2Sc"] = prog.v("scalar")
        env["p2G"] = prog.v("gpsimd")


def _emit_outproj(env):
    nc, prog, dram = env["nc"], env["prog"], env["dram"]
    ctxf, acc = env["ctx"], env["acc"]
    debug = env["debug"]
    V, Sc, T = "vector", "scalar", "tensor"
    wop = dram["wop"]

    with ExitStack() as ctx:
        Ls = lambda *a: ctx.enter_context(nc.sbuf_tensor(*a, side="left"))
        wof = [Ls(f"wof{i}", [128, 4096], F32) for i in range(2)]
        wob = [Ls(f"wob{i}", [128, 8192], BF16) for i in range(2)]
        po = [ctx.enter_context(nc.psum_tensor(f"po{i}", [128, 512], F32))
              for i in range(2)]
        wof_free = [0, 0]
        wob_mm_last = [0, 0]
        po_free = [[(Sc, env["p2Sc"])], [(Sc, env["p2Sc"])]]
        ev_last = 0
        slab_casts = {}

        def load_wo(fg):
            wp = fg % 2
            for half in range(2):
                wx = [(V, wof_free[wp])]
                if fg < 2 and half == 0:
                    wx += [(T, env["p2T"]), (V, env["p2V"]),
                           (Sc, env["p2Sc"]), ("gpsimd", env["p2G"])]
                ld = prog.dma("sync",
                              lambda e, f=fg, h=half, q=wp: e.dma_start(
                                  out=wof[q][:, :], in_=wop[f, :, h * 4096:(h + 1) * 4096]),
                              sem=f"ldo{wp}", waits=wx)
                wts = [(f"ldo{wp}", ld)]
                if half == 0:
                    wts.append((T, wob_mm_last[wp]))
                cv = prog.op(V, lambda e, h=half, q=wp: e.tensor_copy(
                                 wob[q][:, h * 4096:(h + 1) * 4096], wof[q][:, :]),
                             waits=wts)
                wof_free[wp] = cv
            slab_casts[fg] = wof_free[wp]

        load_wo(0)
        load_wo(1)
        for fg in range(4):
            wp = fg % 2
            casts = slab_casts[fg]
            for tb in range(NTO):
                sl = tb % 2
                w0 = [(V, casts)] + po_free[sl]
                m_last = None
                for h in range(NH):
                    m_last = prog.op(T, lambda e, hh=h, t=tb, s=sl, q=wp: e.matmul(
                                         po[s][:, :],
                                         ctxf[:, hh * 1024 + t * 128: hh * 1024 + (t + 1) * 128],
                                         wob[q][:, hh * 512:(hh + 1) * 512],
                                         start=(hh == 0), stop=(hh == NH - 1)),
                                     waits=(w0 if h == 0 else ()))
                ev = prog.op(Sc, lambda e, t=tb, f=fg, s=sl: e.activation(
                                 acc[:, t * D + f * 512: t * D + f * 512 + 512],
                                 po[s][:, :], AF.Copy),
                             waits=[(T, m_last)] +
                                   ([(V, env["p2V"]), ("gpsimd", env["p2G"])]
                                    if (fg == 0 and tb == 0) else []))
                po_free[sl] = [(Sc, ev)]
                ev_last = ev
            wob_mm_last[wp] = m_last
            if fg + 2 < 4:
                load_wo(fg + 2)
        if debug:
            stv = prog.dma("sync", lambda e: e.dma_start(out=dram["d_acc"][:, :], in_=acc[:, :]),
                           sem="st", waits=[(Sc, ev_last)])
            prog.op("sync", lambda e: e.nop(), waits=[("st", stv)])


def _emit_ffn(env, fxb=None):
    nc, prog, dram = env["nc"], env["prog"], env["dram"]
    xnfm, acc = env["xnfm"], env["acc"]
    do_final = fxb is not None
    rstd, nmr = env["rstd"], env["nmr"]
    gsc, usc = env["gsc"], env["usc"]
    ones_cb, ones_rf, n128 = env["ones_cb"], env["ones_rf"], env["n128"]
    ones_bb = env["ones_bb"]
    debug = env["debug"]
    V, Sc, T, G = "vector", "scalar", "tensor", "gpsimd"
    gbp, ubp, dwp, dscp = dram["gbp"], dram["ubp"], dram["dwp"], dram["dscp"]

    with ExitStack() as ctx:
        Ls = lambda *a: ctx.enter_context(nc.sbuf_tensor(*a, side="left"))
        sgp = Ls("sgp", [128, 512], F32)
        sup = Ls("sup", [128, 512], F32)
        dscb = Ls("dscb", [128, 2048], BF16)
        hsr = Ls("hsr", [1, 512], BF16)
        if do_final:
            xst = Ls("xst0", [128, 512], F32)
            sst = ctx.enter_context(nc.psum_tensor("pfs", [128, 512], F32))
            r1p = Ls("r1p", [128, 16], F32)
        hfm = Ls("hfm", [128, NI * 512], BF16)
        pg = [ctx.enter_context(nc.psum_tensor(f"pg{i}", [128, 512], F32))
              for i in range(2)]
        pu = [ctx.enter_context(nc.psum_tensor(f"pu{i}", [128, 512], F32))
              for i in range(2)]
        phs = ctx.enter_context(nc.psum_tensor("phs", [128, 512], F32))
        ppB = ctx.enter_context(nc.psum_tensor("ppB", [128, 512], F32))
        pd = [pg[0], pg[1], pu[0], pu[1]]

        mprev = 0
        cprev = 0
        with ExitStack() as c0:
            dsrow = c0.enter_context(nc.sbuf_tensor("dsrow", [1, 512], F32, side="left"))
            for piece in range(4):
                ld = prog.dma("sync", lambda e, p=piece: e.dma_start(
                                  out=dsrow[:, :], in_=dscp[p:p + 1, :]),
                              sem="ldd", waits=[(T, mprev)])
                mprev = prog.op(T, lambda e: e.matmul(ppB[:, :], ones_rf[:, :],
                                                      dsrow[0:1, :], start=True, stop=True),
                                waits=[("ldd", ld), (V, cprev)])
                cprev = prog.op(V, lambda e, p=piece: e.tensor_copy(
                                    dscb[:, p * 512:(p + 1) * 512], ppB[:, :]),
                                waits=[(T, mprev)])
        dscb_ready = cprev
        ax_prev = 0
        ad_prev = 0
        ga_tab = [0] * 4
        if do_final:
            r1p_seq = prog.op(V, lambda e: e.tensor_scalar(r1p[:, :], rstd[:, :],
                                                           1.0, None, ALU.add))
            outp = dram["out"]
        wstage = Ls("wstage", [128, 16384], BF16)
        gb2 = [wstage[:, p * 4096:(p + 1) * 4096] for p in range(2)]
        ub2 = [wstage[:, 8192 + p * 4096: 8192 + (p + 1) * 4096] for p in range(2)]
        dw2 = [wstage[:, q * 8192:(q + 1) * 8192] for q in range(2)]

        sg_free = 0
        h_free = 0
        acc_g = 0
        he_prev = 0
        hs_prev = 0
        gb_mm_last = [0, 0]
        dw_mm_last = [mprev, mprev]
        pg_free = [[], []]
        pu_free = [[], []]
        pd_free = [[], [], [], []]
        phs_free = []
        tmp_free = 0
        for tg in range(2):
            pending_hs = []
            gb_ready = [0, 0]
            ub_ready = [0, 0]
            gh = 0
            for isc in range(32):
                par = isc % 2
                gb_ready[par] = prog.dma("sync",
                                         lambda e, i=isc, p=par: e.dma_start(
                                             out=gb2[p], in_=gbp[i, :, :]),
                                         sem=f"ldg{par}",
                                         waits=[(T, gb_mm_last[par]),
                                                (T, dw_mm_last[0])])
                ub_ready[par] = prog.dma("sync",
                                         lambda e, i=isc, p=par: e.dma_start(
                                             out=ub2[p], in_=ubp[i, :, :]),
                                         sem=f"ldu{par}",
                                         waits=[(T, gb_mm_last[par]),
                                                (T, dw_mm_last[1])])
                for ic2 in range(2):
                    ic = isc * 2 + ic2
                    psl = ic % 2
                    m = None
                    for hc in range(ND):
                        wts = []
                        if hc == 0:
                            wts = [(f"ldg{par}", gb_ready[par])] + pg_free[psl]
                        m = prog.op(T, lambda e, h=hc, i2=ic2, p=par, t=tg, s=psl: e.matmul(
                                        pg[s][:, :],
                                        gb2[p][:, h * 256 + i2 * 128: h * 256 + i2 * 128 + 128],
                                        xnfm[:, h * 1024 + t * 512: h * 1024 + t * 512 + 512],
                                        start=(h == 0), stop=(h == ND - 1)),
                                    waits=wts)
                    mg = m
                    if len(pending_hs) > 1:
                        pending_hs.pop(0)()
                    a1 = prog.op(Sc, lambda e, i=ic, s=psl: e.activation(
                                     sgp[:, :], pg[s][:, :], AF.Silu,
                                     scale=gsc[:, i:i + 1]),
                                 waits=[(T, mg), (G, sg_free)])
                    m = None
                    for hc in range(ND):
                        wts = []
                        if hc == 0:
                            wts = [(f"ldu{par}", ub_ready[par])] + pu_free[psl]
                        m = prog.op(T, lambda e, h=hc, i2=ic2, p=par, t=tg, s=psl: e.matmul(
                                        pu[s][:, :],
                                        ub2[p][:, h * 256 + i2 * 128: h * 256 + i2 * 128 + 128],
                                        xnfm[:, h * 1024 + t * 512: h * 1024 + t * 512 + 512],
                                        start=(h == 0), stop=(h == ND - 1)),
                                    waits=wts)
                    mu_ = m
                    gb_mm_last[par] = mu_
                    a2 = prog.op(Sc, lambda e, i=ic, s=psl: e.activation(
                                     sup[:, :], pu[s][:, :], AF.Identity,
                                     scale=usc[:, i:i + 1]),
                                 waits=[(T, mu_), (G, sg_free)])
                    gh = prog.op(G, lambda e, i=ic: e.tensor_mul(
                                     hfm[:, i * 512:(i + 1) * 512], sgp[:, :], sup[:, :]),
                                 waits=[(Sc, a2), (Sc, a1), (T, h_free)])
                    sg_free = gh
                    pg_free[psl] = [(Sc, a1)]
                    pu_free[psl] = [(Sc, a2)]

                    def hsum(i=ic, gh=gh):
                        nonlocal hs_prev
                        hs_prev = prog.op(T, lambda e, i2=i: e.matmul(
                                              phs[:, :], ones_bb[:, :],
                                              hfm[:, i2 * 512:(i2 + 1) * 512],
                                              start=(i2 == 0), stop=(i2 == NI - 1)),
                                          waits=([(V, he_prev)] + phs_free
                                                 if i == 0 else []) + [(G, gh)])
                    pending_hs.append(hsum)
            for f in pending_hs:
                f()
            pending_hs.clear()
            phs_free = []
            he = prog.op(V, lambda e: e.tensor_copy(hsr[:, :], phs[0:1, :]),
                         waits=[(T, hs_prev)])
            he_prev = he
            vbar = prog.v(V)
            scbar = prog.v(Sc)

            for fg in range(4):
                stops = [None] * 4
                for pc in range(4):
                    par = pc % 2
                    lw = prog.dma("sync", lambda e, f=fg, p=pc, q=par: e.dma_start(
                                      out=dw2[q], in_=dwp[f * 4 + p, :, :]),
                                  sem=f"lddw{par}",
                                  waits=[(T, dw_mm_last[par]),
                                         (T, gb_mm_last[0]), (T, gb_mm_last[1])])
                    for tb in range(4):
                        for icp in range(16):
                            wts = []
                            if icp == 0 and tb == 0:
                                wts = [(f"lddw{par}", lw), (G, gh), (V, he)]
                            if icp == 0 and pc == 0:
                                wts = wts + pd_free[tb] + \
                                      ([(V, vbar), (Sc, scbar)] if fg == 0 else [])
                            m = prog.op(T, lambda e, f=fg, p=pc, q=par, t=tb, i=icp: e.matmul(
                                            pd[t][:, :],
                                            hfm[:, (p * 16 + i) * 512 + t * 128:
                                                (p * 16 + i) * 512 + t * 128 + 128],
                                            dw2[q][:, i * 512:(i + 1) * 512],
                                            start=(p == 0 and i == 0), stop=False),
                                        waits=wts)
                        if pc == 3:
                            stops[tb] = prog.op(T, lambda e, t=tb: e.matmul(
                                                    pd[t][:, :],
                                                    hsr[0:1, t * 128:(t + 1) * 128],
                                                    n128[0:1, :],
                                                    start=False, stop=True))
                    dw_mm_last[par] = prog.v(T)
                for tb in range(4):
                    tv = prog.op(V, lambda e, t=tb, f=fg: e.tensor_mul(
                                     sgp[:, :], pd[t][:, :], dscb[:, f * 512:(f + 1) * 512]),
                                 waits=[(T, stops[tb]), (G, tmp_free), (G, gh),
                                        (V, dscb_ready)])
                    ga = prog.op(G, lambda e, t=tb, f=fg, tt=tg: e.tensor_add(
                                     acc[:, (tt * 4 + t) * D + f * 512:
                                         (tt * 4 + t) * D + f * 512 + 512],
                                     acc[:, (tt * 4 + t) * D + f * 512:
                                         (tt * 4 + t) * D + f * 512 + 512],
                                     sgp[:, :]),
                                 waits=[(V, tv)])
                    tmp_free = ga
                    acc_g = ga
                    ga_tab[tb] = ga
                    pd_free[tb] = [(V, tv)]
                if do_final:
                    for t4 in range(4):
                        tbg = tg * 4 + t4
                        ld = prog.dma("scalar", lambda e, tt=tbg, f=fg: e.dma_start(
                                          out=xst[:, :],
                                          in_=fxb[tt * 128:(tt + 1) * 128,
                                                  f * 512:(f + 1) * 512]),
                                      sem="ldp0", waits=[(V, ax_prev)])
                        ax = prog.op(V, lambda e, tt=tbg: e.tensor_scalar(
                                         sst[:, :], xst[:, :], r1p[:, tt:tt + 1],
                                         nmr[:, tt:tt + 1], ALU.mult, ALU.add),
                                     waits=[("ldp0", ld), (V, ad_prev),
                                            (V, r1p_seq)])
                        ad = prog.op(V, lambda e, tt=tbg, f=fg: e.tensor_add(
                                         acc[:, tt * D + f * 512: tt * D + f * 512 + 512],
                                         acc[:, tt * D + f * 512: tt * D + f * 512 + 512],
                                         sst[:, :]),
                                     waits=[(V, ax), (G, ga_tab[t4])])
                        prog.dma("scalar", lambda e, tt=tbg, f=fg: e.dma_start(
                                     out=outp[tt * 128:(tt + 1) * 128,
                                              f * 512:(f + 1) * 512],
                                     in_=acc[:, tt * D + f * 512:
                                             tt * D + f * 512 + 512]),
                                 sem="stp", waits=[(V, ad)])
                        ax_prev = ax
                        ad_prev = ad
            sg_free = tmp_free
            h_free = prog.v(T)
            phs_free = [(V, he)]
            pg_free[0] = pd_free[0]
            pg_free[1] = pd_free[1]
            pu_free[0] = pd_free[2]
            pu_free[1] = pd_free[3]
        if do_final:
            prog.op("sync", lambda e: e.nop(), waits=[("stp", prog.dcnt["stp"])])
        if debug and not do_final:
            stv = prog.dma("sync", lambda e: e.dma_start(out=dram["d_acc2"][:, :], in_=acc[:, :]),
                           sem="st", waits=[(G, acc_g)])
            prog.op("sync", lambda e: e.nop(), waits=[("st", stv)])


def _emit_final(env, xb):
    nc, prog, dram = env["nc"], env["prog"], env["dram"]
    acc = env["acc"]
    rstd, nmr = env["rstd"], env["nmr"]
    V, Sc, T, G = "vector", "scalar", "tensor", "gpsimd"
    out = dram["out"]

    with ExitStack() as ctx:
        Ls = lambda *a: ctx.enter_context(nc.sbuf_tensor(*a, side="left"))
        xt = [Ls(f"fxt{i}", [128, D], F32) for i in range(2)]
        xnt = [Ls(f"fxn{i}", [128, D], F32) for i in range(2)]
        fin = [Ls(f"fin{i}", [128, D], F32) for i in range(2)]
        xt_free = [0, 0]
        xn_free = [0, 0]
        fin_free = [0, 0]
        for tb in range(NTO):
            sl = tb % 2
            ld = prog.dma("sync", lambda e, t=tb, s=sl: e.dma_start(
                              out=xt[s][:, :], in_=xb[t * 128:(t + 1) * 128, :]),
                          sem=f"ldf{sl}", waits=[(V, xt_free[sl])])
            ax = prog.op(Sc, lambda e, t=tb, s=sl: e.activation(
                             xnt[s][:, :], xt[s][:, :], AF.Identity,
                             bias=nmr[:, t:t + 1], scale=rstd[:, t:t + 1]),
                         waits=[(f"ldf{sl}", ld), (V, xn_free[sl])])
            d1 = prog.op(V, lambda e, s=sl: e.tensor_add(fin[s][:, :], xt[s][:, :],
                                                         xnt[s][:, :]),
                         waits=[(Sc, ax)] + ([(f"stf{sl}", fin_free[sl])]
                                            if fin_free[sl] else []))
            d2 = prog.op(V, lambda e, t=tb, s=sl: e.tensor_add(
                             fin[s][:, :], fin[s][:, :], acc[:, t * D:(t + 1) * D]),
                         waits=[(V, d1)])
            fin_free[sl] = prog.dma("sync", lambda e, t=tb, s=sl: e.dma_start(
                                        out=out[t * 128:(t + 1) * 128, :], in_=fin[s][:, :]),
                                    sem=f"stf{sl}", waits=[(V, d2)])
            xt_free[sl] = d1
            xn_free[sl] = d1
        prog.op("sync", lambda e: e.nop(),
                waits=[("stf0", prog.dcnt.get("stf0", 0)), ("stf1", prog.dcnt.get("stf1", 0))])



def host_prep(inputs):
    X = np.asarray(inputs["L0_attn_X"], dtype=np.float32)
    ln_w = np.asarray(inputs["L0_attn_ln_w"], dtype=np.float32)
    ln_b = np.asarray(inputs["L0_attn_ln_b"], dtype=np.float32)
    W_in = np.ascontiguousarray(np.asarray(inputs["L0_attn_W_in"], dtype=np.float32))
    W_out = np.asarray(inputs["L0_attn_W_out"], dtype=np.float32)
    gw = np.asarray(inputs["L0_ffn_gate_w"])
    uw = np.asarray(inputs["L0_ffn_up_w"])
    dw = np.asarray(inputs["L0_ffn_down_w"])
    gs = np.asarray(inputs["L0_ffn_gate_scale"], dtype=np.float32)
    us = np.asarray(inputs["L0_ffn_up_scale"], dtype=np.float32)
    ds = np.asarray(inputs["L0_ffn_down_scale"], dtype=np.float32)

    wip = np.empty((24, 128, 4096), np.float32)
    for g in range(8):
        for w, base in enumerate((0, 2048, 4096)):
            blk = W_in[:, base + 2 * g * 128: base + 2 * g * 128 + 256]
            wip[g * 3 + w] = blk.reshape(16, 128, 256).transpose(1, 0, 2).reshape(128, 4096)
    wop = np.empty((4, 128, 8192), np.float32)
    for fg in range(4):
        blk = W_out[:, fg * 512:(fg + 1) * 512]
        wop[fg] = blk.reshape(16, 128, 512).transpose(1, 0, 2).reshape(128, 8192)

    def pack_gu(wq):
        a = wq.astype(np.float32).astype(BF)
        a = a.reshape(32, 256, 16, 128)
        return np.ascontiguousarray(a.transpose(0, 3, 2, 1)).reshape(32, 128, 4096)

    gbp = pack_gu(gw)
    ubp = pack_gu(uw)
    a = dw.astype(np.float32).astype(BF)
    a = a.reshape(4, 512, 4, 16, 128)
    dwp = np.ascontiguousarray(a.transpose(0, 2, 4, 3, 1)).reshape(16, 128, 8192)

    shared = dict(
        wip=wip, wop=wop, gbp=gbp, ubp=ubp, dwp=dwp,
        lnwc=np.ascontiguousarray(ln_w.reshape(ND, 128).T),
        lnbc=np.ascontiguousarray(ln_b.reshape(ND, 128).T),
        gscp=np.ascontiguousarray(gs.reshape(NI, 128).T),
        uscp=np.ascontiguousarray(us.reshape(NI, 128).T),
        dscp=np.ascontiguousarray(ds.reshape(4, 512)),
        idf=np.eye(128, dtype=np.float32),
    )
    pm = np.zeros((64, 64), np.float32)
    for t in range(32):
        pm[2 * t + 1, 2 * t] = -1.0
        pm[2 * t, 2 * t + 1] = 1.0
    shared["pmat"] = pm.astype(BF)

    inv_freq = (1.0 / (10000.0 ** (np.arange(0, 64, 2, dtype=np.float32)
                                   / np.float32(64.0)))).astype(np.float32)
    in_maps = []
    for c in range(8):
        b, qh = c // 2, c % 2
        off = qh * 1024
        pos = (np.arange(S) + off) % S
        ang = pos[:, None].astype(np.float32) * inv_freq[None, :]
        ang2 = np.repeat(ang, 2, axis=1)
        m = dict(shared)
        m["cs"] = np.ascontiguousarray(np.concatenate(
            [np.cos(ang2).T, np.sin(ang2).T], axis=0).astype(np.float32))
        m["xb"] = np.ascontiguousarray(np.concatenate([X[b, off:], X[b, :off]], axis=0))
        in_maps.append(m)
    return in_maps


def kernel(**inputs):
    in_maps = host_prep(inputs)
    nc = build_program(stage=STAGE, debug=DEBUG)
    res = run_bass_kernel_spmd(nc, in_maps, core_ids=list(range(8)))
    Y = np.empty((B, S, D), np.float32)
    for c in range(8):
        b, qh = c // 2, c % 2
        off = qh * 1024
        Y[b, off:off + 1024, :] = res.results[c]["out"]
    return Y


# version 29 (final)
# speedup vs baseline: 1.2222x; 1.0014x over previous
import math
from contextlib import ExitStack

import numpy as np
import ml_dtypes
import concourse.bass as bass
import concourse.mybir as mybir
from concourse.bass_utils import run_bass_kernel_spmd

F32 = mybir.dt.float32
BF16 = mybir.dt.bfloat16
AF = mybir.ActivationFunctionType
ALU = mybir.AluOpType
AX = mybir.AxisListType
BF = ml_dtypes.bfloat16

B, S, D, NH, DH, II = 4, 2048, 2048, 16, 128, 8192
ND = D // 128
NT = S // 128
NTO = 8
NI = II // 128
SCALE = 1.0 / math.sqrt(DH)
EPS = 1e-5
INV_D = 1.0 / D

STAGE = 5
DEBUG = False
P2_SUB = "all"

ENGS = ("sync", "tensor", "vector", "scalar", "gpsimd")


class Prog:

    def __init__(self, nc, ctx):
        self.nc = nc
        self.ctx = ctx
        self.ops = {e: [] for e in ENGS}
        self.seq = {e: 0 for e in ENGS}
        self.dcnt = {}
        self.marks = {e: set() for e in ENGS}
        self.sems = {}

    def _add(self, eng, fn, waits, dsem):
        ws = []
        for name, val in waits:
            if not val or val <= 0:
                continue
            ws.append((name, val))
            if name in ENGS:
                self.marks[name].add(val)
        self.seq[eng] += 1
        self.ops[eng].append((fn, tuple(ws), self.seq[eng], dsem))

    def op(self, eng, fn, waits=()):
        self._add(eng, fn, waits, None)
        return self.seq[eng]

    def dma(self, eng, fn, sem, waits=()):
        self.dcnt[sem] = self.dcnt.get(sem, 0) + 16
        self._add(eng, fn, waits, sem)
        return self.dcnt[sem]

    def v(self, eng):
        return self.seq[eng]

    def barrier(self):
        self.dcnt["bar"] = self.dcnt.get("bar", 0) + len(ENGS)
        target = self.dcnt["bar"]
        for e in ENGS:
            self.seq[e] += 1
            self.ops[e].append(("BARRIER", target, self.seq[e], None))

    def emit(self):
        nc = self.nc
        rank = {}
        for e in ENGS:
            rank[e] = {s: i + 1 for i, s in enumerate(sorted(self.marks[e]))}
        for e in ENGS:
            self.sems["q" + e] = self.ctx.enter_context(nc.semaphore("q" + e))
        for name in self.dcnt:
            self.sems[name] = self.ctx.enter_context(nc.semaphore(name))

        def mk(e):
            ops = self.ops[e]
            sems = self.sems
            marks = self.marks[e]

            def body(eng):
                for fn, waits, eseq, dsem in ops:
                    if fn == "BARRIER":
                        if e == "sync":
                            eng.sem_inc(sems["bar"], 1)
                        else:
                            eng.drain().then_inc(sems["bar"], 1)
                        eng.wait_ge(sems["bar"], waits)
                        continue
                    for wname, wval in waits:
                        if wname in ENGS:
                            eng.wait_ge(sems["q" + wname], rank[wname][wval])
                        else:
                            eng.wait_ge(sems[wname], wval)
                    inst = fn(eng)
                    if dsem is not None:
                        inst.then_inc(sems[dsem], 16)
                    if eseq in marks:
                        inst.then_inc(sems["q" + e], 1)
            return body

        with nc.Block() as blk:
            for e in ENGS:
                getattr(blk, e)(mk(e))



def build_program(stage=STAGE, debug=DEBUG):
    nc = bass.Bass("TRN2", target_bir_lowering=False)

    dram = {}

    def din(name, shape, dtype):
        dram[name] = nc.dram_tensor(name, shape, dtype, kind="ExternalInput").ap()
        return dram[name]

    def dout(name, shape, dtype):
        dram[name] = nc.dram_tensor(name, shape, dtype, kind="ExternalOutput").ap()
        return dram[name]

    xb = din("xb", [S, D], F32)
    din("wip", [24, 128, 4096], F32)
    din("wop", [4, 128, 8192], F32)
    din("gbp", [32, 128, 4096], BF16)
    din("ubp", [32, 128, 4096], BF16)
    din("dwp", [16, 128, 8192], BF16)
    din("cs", [128, S], F32)
    din("lnwc", [128, ND], F32)
    din("lnbc", [128, ND], F32)
    din("gscp", [128, NI], F32)
    din("uscp", [128, NI], F32)
    din("dscp", [4, 512], F32)
    din("pmat", [64, 64], BF16)
    din("idf", [128, 128], F32)

    dout("out", [NTO * 128, D], F32)

    if debug:
        dout("d_xn2", [128, ND * S], BF16)
        dout("d_xnfm", [128, ND * 1024], BF16)
        dout("d_rstd", [128, NT], F32)
        dout("d_nmr", [128, NT], F32)
        if stage >= 2:
            dout("d_q", [128, 2 * 1024], BF16)
            dout("d_k", [128, 2 * S], BF16)
            dout("d_v", [128, NT * 256], BF16)
            dout("d_bias", [128, 2], F32)
            dout("d_ctx", [128, NH * 1024], BF16)
        if stage >= 3:
            dout("d_acc", [128, NTO * D], F32)
        if stage >= 4:
            dout("d_acc2", [128, NTO * D], F32)

    with ExitStack() as top:
        prog = Prog(nc, top)
        L = lambda *a: top.enter_context(nc.sbuf_tensor(*a, side="left"))

        env = {"nc": nc, "prog": prog, "dram": dram, "debug": debug, "top": top,
               "stage": stage}

        env["lnw"] = L("lnw_sb", [128, ND], F32)
        env["lnb"] = L("lnb_sb", [128, ND], F32)
        env["gsc"] = L("gsc_sb", [128, NI], F32)
        env["usc"] = L("usc_sb", [128, NI], F32)
        env["pm"] = L("pm_sb", [64, 64], BF16)
        env["idf"] = L("idf_sb", [128, 128], F32)
        env["ones_cb"] = L("ones_cb", [128, 1], BF16)
        env["ones_bb"] = L("ones_bb", [128, 128], BF16)
        env["ones_rf"] = L("ones_rf", [1, 128], F32)
        env["n128"] = L("n128_r", [1, 512], BF16)
        env["eps_c"] = L("eps_c", [128, 1], F32)
        env["rstd"] = L("rstd_sb", [128, NT], F32)
        env["nmr"] = L("nmr_sb", [128, NT], F32)
        env["scr"] = L("scr_c", [128, 32], F32)
        env["scr1"] = L("scr_1", [1, 8], F32)
        env["xnfm"] = L("xnfm", [128, ND * 1024], BF16)

        with ExitStack() as sCtx:
          env["sCtx"] = sCtx
          with ExitStack() as sA:
            env["xn2"] = sA.enter_context(nc.sbuf_tensor("xn2f", [128, ND * S], BF16, side="left"))
            env["cs"] = sA.enter_context(nc.sbuf_tensor("cs_sb", [128, S], F32, side="left"))
            ldc = []
            for i, (name, dst) in enumerate([
                    ("lnwc", env["lnw"]), ("lnbc", env["lnb"]),
                    ("gscp", env["gsc"]), ("uscp", env["usc"]),
                    ("pmat", env["pm"]), ("idf", env["idf"]),
                    ("cs", env["cs"])]):
                prog.dma("sync", lambda e, d=dst, s=dram[name]: e.dma_start(out=d[:, :], in_=s[:, :]),
                         sem=f"ldc{i}")
                ldc.append((f"ldc{i}", 16))
            env["ldc"] = ldc
            for dst, valc in [(env["ones_cb"], 1.0), (env["ones_bb"], 1.0), (env["ones_rf"], 1.0),
                              (env["n128"], -128.0), (env["eps_c"], EPS)]:
                prog.op("vector", lambda e, d=dst, vv=valc: e.memset(d[:, :], vv))
            env["dvc_done"] = prog.v("vector")

            _emit_ln(env, xb)
            if debug:
                prog.barrier()
                prog.dma("sync", lambda e: e.dma_start(out=dram["d_xn2"][:, :], in_=env["xn2"][:, :]), sem="st")
                prog.dma("sync", lambda e: e.dma_start(out=dram["d_xnfm"][:, :], in_=env["xnfm"][:, :]), sem="st")
                prog.dma("sync", lambda e: e.dma_start(out=dram["d_rstd"][:, :], in_=env["rstd"][:, :]), sem="st")
                prog.dma("sync", lambda e: e.dma_start(out=dram["d_nmr"][:, :], in_=env["nmr"][:, :]), sem="st")

            if stage >= 2:
                _emit_attention(env)
                if stage == 2:
                    prog.barrier()
          if stage >= 3:
            env["acc"] = top.enter_context(nc.sbuf_tensor("out_acc", [128, NTO * D], F32, side="left"))
            _emit_outproj(env)
            prog.barrier()
        if stage >= 4:
            _emit_ffn(env, xb if stage >= 5 else None)
            prog.barrier()

        prog.barrier()
        st_total = prog.dcnt.get("st", 0)
        if st_total:
            prog.op("sync", lambda e: e.nop(), waits=[("st", st_total)])
        prog.emit()
    return nc



def _emit_ln(env, xb):
    nc, prog = env["nc"], env["prog"]
    scr, eps_c = env["scr"], env["eps_c"]
    rstd, nmr = env["rstd"], env["nmr"]
    xn2, xnfm = env["xn2"], env["xnfm"]
    lnw, lnb, idf = env["lnw"], env["lnb"], env["idf"]
    V, Sc, T = "vector", "scalar", "tensor"

    with ExitStack() as ctx:
        Ls = lambda *a: ctx.enter_context(nc.sbuf_tensor(*a, side="left"))
        NC = 4
        xt = [Ls(f"xt{i}", [128, D], F32) for i in range(NC)]
        xnt = [Ls(f"xnt{i}", [128, D], F32) for i in range(NC)]
        xn2t = [Ls(f"xn2t{i}", [128, D], F32) for i in range(NC)]
        pst = [ctx.enter_context(nc.psum_tensor(f"pst{i}", [128, 512], F32, side="right"))
               for i in range(4)]

        xt_free = [0] * NC
        src_free = [0] * NC
        slot_ev = [None] * 4
        col1_free = [0] * NC
        prev_d_last = [0] * NC
        prev_a_last = [0] * NC
        sq_prev = [0] * NC
        gp_last = [0] * NC
        env["x2ev"] = [0] * NT
        res = {}

        def chain(tt):
            sl = tt % NC
            cb = 8 * sl
            cc = [scr[:, cb + i:cb + i + 1] for i in range(8)]
            ldx = prog.dma("sync",
                           lambda e, t=tt, s=sl: e.dma_start(
                               out=xt[s][:, :], in_=xb[t * 128:(t + 1) * 128, :]),
                           sem=f"ldx{sl}", waits=[(Sc, xt_free[sl]), (V, prev_d_last[sl]),
                                                  ("gpsimd", gp_last[sl])])
            d1 = prog.op(V, lambda e, s=sl, c=cc: e.reduce_sum(c[0], xt[s][:, :], axis=AX.X),
                         waits=[(f"ldx{sl}", ldx), (V, prev_d_last[sl]), (Sc, prev_a_last[sl]),
                                ("gpsimd", gp_last[sl])]
                               + (env["ldc"] if tt == 0 else []))
            a1 = prog.op(Sc, lambda e, s=sl, c=cc: e.activation(xnt[s][:, :], xt[s][:, :],
                                                                AF.Square, accum_out=c[1]),
                         waits=[(f"ldx{sl}", ldx), (V, col1_free[sl]), (Sc, max(sq_prev)),
                                (T, src_free[sl])])
            sq_prev[sl] = a1
            yield
            d2 = prog.op(V, lambda e, c=cc: e.tensor_scalar(c[2], c[0], INV_D, None, ALU.mult),
                         waits=[(V, d1)])
            d3 = prog.op(V, lambda e, c=cc: e.tensor_scalar(c[3], c[1], INV_D, None, ALU.mult),
                         waits=[(Sc, a1), (V, d2)])
            col1_free[sl] = d3
            d4 = prog.op(V, lambda e, c=cc: e.tensor_scalar(c[4], c[2], c[2], None, ALU.mult),
                         waits=[(V, d3)])
            d5 = prog.op(V, lambda e, c=cc: e.tensor_sub(c[3], c[3], c[4]), waits=[(V, d4)])
            yield
            a2 = prog.op(Sc, lambda e, c=cc: e.activation(c[5], c[3], AF.Ln, bias=eps_c[:, :]),
                         waits=[(V, d5)])
            a3 = prog.op(Sc, lambda e, t=tt, c=cc: e.activation(rstd[:, t:t + 1], c[5],
                                                                AF.Exp, scale=-0.5),
                         waits=[(Sc, a2)])
            yield
            d6 = prog.op(V, lambda e, t=tt, c=cc: e.tensor_scalar(nmr[:, t:t + 1], c[2],
                                                                  rstd[:, t:t + 1], -1.0,
                                                                  ALU.mult, ALU.mult),
                         waits=[(Sc, a3)])
            d6b = prog.op("gpsimd", lambda e, t=tt, s=sl: e.tensor_scalar(
                              xnt[s][:, :], xt[s][:, :], rstd[:, t:t + 1],
                              nmr[:, t:t + 1], ALU.mult, ALU.add),
                          waits=[(V, d6), (T, src_free[sl]), (Sc, a1)])
            yield
            a5 = prog.op(Sc, lambda e, s=sl, c=cc: e.activation(xt[s][:, :], xnt[s][:, :],
                                                                AF.Square, accum_out=c[1]),
                         waits=[(V, col1_free[sl]), ("gpsimd", d6b), (Sc, max(sq_prev))])
            sq_prev[sl] = a5
            yield
            d9 = prog.op(V, lambda e, c=cc: e.tensor_scalar(c[3], c[1], INV_D, None, ALU.mult),
                         waits=[(Sc, a5)])
            col1_free[sl] = d9
            yield
            a6 = prog.op(Sc, lambda e, c=cc: e.activation(c[5], c[3], AF.Ln, bias=eps_c[:, :]),
                         waits=[(V, d9)])
            a7 = prog.op(Sc, lambda e, c=cc: e.activation(c[6], c[5], AF.Exp, scale=-0.5),
                         waits=[(Sc, a6)])
            yield
            a8 = prog.op("gpsimd", lambda e, s=sl, c=cc: e.tensor_scalar(
                             xn2t[s][:, :], xnt[s][:, :], c[6], 0.0,
                             ALU.mult, ALU.add),
                         waits=[(Sc, a7), (T, src_free[sl])])
            prev_d_last[sl], prev_a_last[sl] = d9, a7
            xt_free[sl] = a5
            gp_last[sl] = a8
            res[tt] = (sl, a8, d6b)

        def transposes(tt):
            sl, a8, d6b = res[tt]
            x2ev_pair = [0, 0]
            work = [("x2", dc) for dc in range(ND)]
            if tt < NTO:
                work += [("xn", dc) for dc in range(ND)]
            last_t = 0
            for j, (kind, dc) in enumerate(work):
                pslot = j % 4
                p = pst[pslot]
                srcbuf = xn2t[sl] if kind == "x2" else xnt[sl]
                wts = [("gpsimd", a8)] if kind == "x2" else [("gpsimd", d6b)]
                if slot_ev[pslot] is not None:
                    wts.append(slot_ev[pslot])
                mm_t = prog.op(T, lambda e, sb=srcbuf, d=dc, pp=p: e.transpose(
                                   pp[:, 0:128], sb[:, d * 128:(d + 1) * 128], idf[:, :]),
                               waits=wts)
                last_t = mm_t
                if kind == "x2":
                    if dc % 2 == 0:
                        ev = prog.op(V, lambda e, t=tt, d=dc, pp=p: e.tensor_scalar(
                                         xn2[:, d * S + t * 128: d * S + (t + 1) * 128],
                                         pp[:, 0:128], lnw[:, d:d + 1], lnb[:, d:d + 1],
                                         ALU.mult, ALU.add),
                                     waits=[(T, mm_t)])
                        slot_ev[pslot] = (V, ev)
                    else:
                        ev = prog.op(Sc, lambda e, t=tt, d=dc, pp=p: e.activation(
                                         xn2[:, d * S + t * 128: d * S + (t + 1) * 128],
                                         pp[:, 0:128], AF.Identity,
                                         bias=lnb[:, d:d + 1], scale=lnw[:, d:d + 1]),
                                     waits=[(T, mm_t)])
                        slot_ev[pslot] = (Sc, ev)
                    x2ev_pair[dc % 2] = ev
                else:
                    ev = prog.op(V, lambda e, t=tt, d=dc, pp=p: e.tensor_copy(
                                     xnfm[:, d * 1024 + t * 128: d * 1024 + (t + 1) * 128],
                                     pp[:, 0:128]),
                                 waits=[(T, mm_t)])
                    slot_ev[pslot] = (V, ev)
            env["x2ev"][tt] = [(V, x2ev_pair[0]), (Sc, x2ev_pair[1])]
            src_free[sl] = last_t

        for k in range(NT // 4):
            alive = [chain(4 * k + i) for i in range(4)]
            while alive:
                for gch in list(alive):
                    try:
                        next(gch)
                    except StopIteration:
                        alive.remove(gch)
            for i in range(4):
                transposes(4 * k + i)
        env["p1T"] = prog.v("tensor")


def _emit_attention(env):
    nc, prog, dram = env["nc"], env["prog"], env["dram"]
    xn2f = env["xn2"]
    cs, pm = env["cs"], env["pm"]
    ones_cb, ones_rf, scr1 = env["ones_cb"], env["ones_rf"], env["scr1"]
    ones_bb = env["ones_bb"]
    debug = env["debug"]
    V, Sc, T = "vector", "scalar", "tensor"
    wip = dram["wip"]

    env["ctx"] = env["sCtx"].enter_context(
        nc.sbuf_tensor("ctxf", [128, NH * 1024], BF16, side="right"))
    ctxf = env["ctx"]
    with ExitStack() as ctx:
        R = lambda *a: ctx.enter_context(nc.sbuf_tensor(*a, side="right"))
        wf32 = [R(f"wf32_{i}", [128, 512], F32) for i in range(2)]
        wbf = [R(f"wbf{i}", [128, 4096], BF16) for i in range(3)]
        k2 = R("k2", [128, 2 * S], BF16)
        q2 = R("q2", [128, 2 * 1024], BF16)
        v2 = R("v2", [128, NT * 256], BF16)
        ebig = ctx.enter_context(nc.sbuf_tensor("ebig", [128, 8 * 512], BF16, side="left"))
        sqb = R("sqb", [128, 1024], BF16)
        rb = R("rb", [128, 512], F32)
        biasc = R("biasc", [128, 2], F32)
        pp = [ctx.enter_context(nc.psum_tensor(f"pp{i}", [128, 512], F32))
              for i in range(2)]
        ppR = ctx.enter_context(nc.psum_tensor("ppR", [128, 512], F32))
        ppS = ctx.enter_context(nc.psum_tensor("ppS", [128, 512], F32))
        ppD = [ctx.enter_context(nc.psum_tensor(f"ppD{i}", [128, 512], F32))
               for i in range(2)]
        ppC = [ctx.enter_context(nc.psum_tensor(f"ppC{i}", [128, 512], F32))
               for i in range(2)]

        wf32_free = [0, 0]
        wbf_mm_last = [0, 0, 0]
        slab_ready = {}
        pp_free = [[], [], [], []]
        ppR_free = []
        bankF = {"S": [], "D0": [], "D1": []}
        ppC_free = [[], []]
        e_free = 0
        cv_prev = 0
        tcos_free = 0
        sq_free = []
        k2_rd = []
        v2_rd = []
        tile_i = 0

        def load_slab(j):
            for qt in range(8):
                qp = (j * 8 + qt) % 2
                w_extra = [("tensor", env["p1T"])] if (j < 3) else []
                ld = prog.dma("sync",
                              lambda e, jj=j, h=qt, q=qp: e.dma_start(
                                  out=wf32[q][:, :], in_=wip[jj, :, h * 512:(h + 1) * 512]),
                              sem=f"ldw{qp}", waits=[("gpsimd", wf32_free[qp])] + w_extra)
                wts = [(f"ldw{qp}", ld)]
                if qt == 0:
                    wts.append((T, wbf_mm_last[j % 3]))
                cv = prog.op("gpsimd", lambda e, jj=j, h=qt, q=qp: e.tensor_copy(
                                 wbf[jj % 3][:, h * 512:(h + 1) * 512], wf32[q][:, :]),
                             waits=wts)
                wf32_free[qp] = cv
            slab_ready[j] = cv

        load_slab(0)
        load_slab(1)
        load_slab(2)

        pp4 = [pp[0], pp[1], ppC[0], ppC[1]]
        for g in range(8):

            rope_last = 0
            vproj_last = 0
            rope_done = {}
            qk_jobs = [("q", hh, tg) for hh in range(2) for tg in range(2)] + \
                      [("k", hh, tg) for hh in range(2) for tg in range(4)]
            pending_rope = None
            for kind, hh, tg in qk_jobs:
                slabj = 3 * g + (0 if kind == "q" else 1)
                dst = q2 if kind == "q" else k2
                dstoff = hh * (1024 if kind == "q" else 2048) + tg * 512
                sl = tile_i % 4
                w0 = [("gpsimd", slab_ready[slabj])] + pp_free[sl] + k2_rd
                if g == 0:
                    w0 += env["x2ev"][4 * tg + 3]
                if g == 0 and tile_i in (2, 3):
                    w0 += env["x2ev"][14] + env["x2ev"][15]
                k2_rd = []
                m_last = None
                for dc in range(ND):
                    m_last = prog.op(T, lambda e, d=dc, h=hh, t=tg, s=sl, sj=slabj: e.matmul(
                                         pp4[s][:, :],
                                         wbf[sj % 3][:, d * 256 + h * 128: d * 256 + (h + 1) * 128],
                                         xn2f[:, d * S + t * 512: d * S + t * 512 + 512],
                                         start=(d == 0), stop=(d == ND - 1)),
                                     waits=(w0 if dc == 0 else ()))
                wbf_mm_last[slabj % 3] = m_last
                aev = prog.op(Sc, lambda e, s=sl, do=dstoff, dd=dst: e.activation(
                                  dd[:, do:do + 512], pp4[s][:, :], AF.Copy),
                              waits=[(T, m_last)])
                if pending_rope is not None:
                    pending_rope()
                if P2_SUB == "mm":
                    pending_rope = None
                    pp_free[sl] = [(Sc, aev)]
                    tile_i += 1
                    continue

                def rope(m_last=m_last, aev=aev, sl=sl, tg=tg, dst=dst, dstoff=dstoff,
                         key=(kind, hh, tg)):
                    nonlocal tcos_free, ppR_free, rope_last
                    mB = prog.op(T, lambda e, do=dstoff, dd=dst: e.matmul(
                                     ppR[0:64, :], pm[:, :], dd[0:64, do:do + 512],
                                     start=True, stop=True),
                                 waits=[(Sc, aev)] + ppR_free)
                    dco = prog.op(V, lambda e, t=tg, do=dstoff, dd=dst: e.tensor_mul(
                                      dd[0:64, do:do + 512], dd[0:64, do:do + 512],
                                      cs[0:64, t * 512:t * 512 + 512]),
                                  waits=[(T, mB)])
                    dt2 = prog.op(V, lambda e, t=tg: e.tensor_mul(
                                      rb[0:64, :], ppR[0:64, :], cs[64:128, t * 512:t * 512 + 512]),
                                  waits=[(T, mB), (V, tcos_free)])
                    dadd = prog.op(V, lambda e, do=dstoff, dd=dst: e.tensor_add(
                                       dd[0:64, do:do + 512], dd[0:64, do:do + 512], rb[0:64, :]),
                                   waits=[(V, dt2)])
                    rope_last = dadd
                    rope_done[key] = dadd
                    tcos_free = dadd
                    ppR_free = [(V, dt2)]
                    pp_free[sl] = [(Sc, aev)]

                pp_free[sl] = [(Sc, aev)]
                pending_rope = rope
                tile_i += 1
            slabv = 3 * g + 2
            bias_ready = [0, 0]
            pb = [ppS, ppD[0], ppD[1]]
            pbk = ["S", "D0", "D1"]
            bsteps = []
            if P2_SUB != "qkv":
                for hh in range(2):
                    def mk_head(hh=hh):
                        state = {}

                        def s_sq(kind, half):
                            def f():
                                if kind == "q":
                                    state["gate"] = prog.op("gpsimd", lambda e, h=hh: e.tensor_mul(
                                        sqb[:, 0:1024], q2[:, h * 1024:(h + 1) * 1024],
                                        q2[:, h * 1024:(h + 1) * 1024]),
                                        waits=[(V, rope_done[("q", hh, 1)])] + sq_free)
                                else:
                                    state["gate"] = prog.op("gpsimd", lambda e, h=hh, q=half: e.tensor_mul(
                                        sqb[:, :],
                                        k2[:, h * 2048 + q * 1024: h * 2048 + q * 1024 + 1024],
                                        k2[:, h * 2048 + q * 1024: h * 2048 + q * 1024 + 1024]),
                                        waits=[(V, rope_done[("k", hh, 2 * half + 1)]),
                                               (T, state["mb"])])
                            return f

                        def s_mm(j):
                            def f():
                                bank = j % 3
                                mb = prog.op(T, lambda e, b=j % 2, k=bank: e.matmul(
                                                 pb[k][:, :], ones_bb[:, :],
                                                 sqb[:, b * 512:(b + 1) * 512],
                                                 start=True, stop=True),
                                             waits=[("gpsimd", state["gate"])] + bankF[pbk[bank]])
                                state["mb"] = mb
                                rd = prog.op(V, lambda e, j_=j, k=bank: e.reduce_max(
                                                 scr1[0:1, j_:j_ + 1], pb[k][0:1, :], axis=AX.X),
                                             waits=[(T, mb)])
                                bankF[pbk[bank]] = [(V, rd)]
                                state["rd"] = rd
                            return f

                        def s_fin():
                            nonlocal_state = state

                            def f():
                                rd_last = nonlocal_state["rd"]
                                c1 = prog.op(V, lambda e: e.reduce_max(scr1[0:1, 6:7], scr1[0:1, 0:2],
                                                                       axis=AX.X), waits=[(V, rd_last)])
                                c4 = prog.op(V, lambda e: e.reduce_max(scr1[0:1, 7:8], scr1[0:1, 2:6],
                                                                       axis=AX.X), waits=[(V, c1)])
                                c5 = prog.op(V, lambda e: e.tensor_add(scr1[0:1, 6:7], scr1[0:1, 6:7],
                                                                       scr1[0:1, 7:8]), waits=[(V, c4)])
                                c6 = prog.op(V, lambda e: e.tensor_scalar(scr1[0:1, 7:8], scr1[0:1, 6:7],
                                                                          -0.5 * SCALE, None, ALU.mult),
                                             waits=[(V, c5)])
                                mbb = prog.op(T, lambda e: e.matmul(ppS[:, 0:1], ones_rf[:, :],
                                                                    scr1[0:1, 7:8], start=True, stop=True),
                                              waits=[(V, c6)] + bankF["S"])
                                bias_ready[hh] = prog.op(V, lambda e, h=hh: e.tensor_copy(
                                                             biasc[:, h:h + 1], ppS[:, 0:1]),
                                                         waits=[(T, mbb)])
                                bankF["S"] = [(V, bias_ready[hh])]
                            return f

                        steps = [s_sq("q", 0), s_mm(0), s_mm(1), s_sq("k", 0), s_mm(2),
                                 s_mm(3), s_sq("k", 1), s_mm(4), s_mm(5), s_fin()]

                        def close_sqfree():
                            nonlocal sq_free
                            sq_free = [(T, state["mb"])]
                        steps.append(close_sqfree)
                        return steps
                    bsteps += mk_head()

            bi = 0
            for tb in range(NT):
                sl = tile_i % 4
                w0 = [("gpsimd", slab_ready[slabv])] + pp_free[sl] + v2_rd
                if g == 0:
                    w0 += env["x2ev"][tb]
                v2_rd = []
                m_last = None
                for dc in range(ND):
                    m_last = prog.op(T, lambda e, d=dc, t=tb, s=sl, sj=slabv: e.matmul(
                                         pp4[s][:, 0:256],
                                         xn2f[:, d * S + t * 128: d * S + t * 128 + 128],
                                         wbf[sj % 3][:, d * 256: d * 256 + 256],
                                         start=(d == 0), stop=(d == ND - 1)),
                                     waits=(w0 if dc == 0 else ()))
                wbf_mm_last[slabv % 3] = m_last
                aev = prog.op(Sc, lambda e, s=sl, t=tb: e.activation(
                                  v2[:, t * 256:(t + 1) * 256], pp4[s][:, 0:256], AF.Copy),
                              waits=[(T, m_last)])
                vproj_last = aev
                if pending_rope is not None:
                    pending_rope()
                    pending_rope = None
                pp_free[sl] = [(Sc, aev)]
                tile_i += 1
                for _ in range(2):
                    if bi < len(bsteps):
                        bsteps[bi]()
                        bi += 1
            while bi < len(bsteps):
                bsteps[bi]()
                bi += 1
            for j in range(3 * g + 3, min(3 * g + 6, 24)):
                load_slab(j)

            pps = [pp[0], pp[1], ppR]
            pp3_free = [pp_free[0], pp_free[1], ppR_free]
            sc3 = 0
            for hh in (range(2) if P2_SUB == "all" else ()):
                for qg in range(2):
                    un = (g * 4 + hh * 2 + qg) % 2
                    dkey = "D0" if un == 0 else "D1"
                    ms_seq = [0] * 16
                    ex_seq = [0] * 16
                    md_seq = [0] * 16
                    for kb in range(16):
                        sl = sc3 % 3
                        sc3 += 1
                        wts = list(pp3_free[sl])
                        if kb == 0:
                            wts += [(V, rope_last), (Sc, vproj_last)]
                        ms_seq[kb] = prog.op(T, lambda e, h=hh, q=qg, k=kb, s=sl: e.matmul(
                                                 pps[s][:, :],
                                                 k2[:, h * 2048 + k * 128: h * 2048 + (k + 1) * 128],
                                                 q2[:, h * 1024 + q * 512: h * 1024 + q * 512 + 512],
                                                 start=True, stop=True),
                                             waits=wts)
                        wts = [(T, ms_seq[kb]), (V, bias_ready[hh])]
                        if kb == 0:
                            wts.append((T, e_free))
                        if kb >= 8:
                            wts.append((T, md_seq[kb - 8]))
                        ex_seq[kb] = prog.op(Sc, lambda e, k=kb, s=sl, h=hh: e.activation(
                                                 ebig[:, (k % 8) * 512:(k % 8) * 512 + 512],
                                                 pps[s][:, :],
                                                 AF.Exp, bias=biasc[:, h:h + 1], scale=SCALE),
                                             waits=wts)
                        pp3_free[sl] = [(Sc, ex_seq[kb])]
                        if kb >= 2:
                            prog.op(T, lambda e, k=kb - 2, h=hh, u=un: e.matmul(
                                        ppC[u][:, :],
                                        v2[:, k * 256 + h * 128: k * 256 + (h + 1) * 128],
                                        ebig[:, (k % 8) * 512:(k % 8) * 512 + 512],
                                        start=(k == 0), stop=False),
                                    waits=[(Sc, ex_seq[kb - 2])] +
                                          (ppC_free[un] + pp_free[2 + un]
                                           if kb == 2 else []))
                            md_seq[kb - 2] = prog.op(T, lambda e, k=kb - 2, u=un: e.matmul(
                                        ppD[u][:, :], ones_bb[:, :],
                                        ebig[:, (k % 8) * 512:(k % 8) * 512 + 512],
                                        start=(k == 0), stop=False),
                                    waits=(bankF[dkey] if kb == 2 else ()))
                    prog.op(T, lambda e, h=hh, u=un: e.matmul(
                                ppC[u][:, :],
                                v2[:, 14 * 256 + h * 128: 14 * 256 + (h + 1) * 128],
                                ebig[:, 6 * 512:7 * 512],
                                start=False, stop=False),
                            waits=[(Sc, ex_seq[14])])
                    md_seq[14] = prog.op(T, lambda e, u=un: e.matmul(
                                ppD[u][:, :], ones_bb[:, :],
                                ebig[:, 6 * 512:7 * 512], start=False, stop=False))
                    mp_last = prog.op(T, lambda e, h=hh, u=un: e.matmul(
                                          ppC[u][:, :],
                                          v2[:, 15 * 256 + h * 128: 15 * 256 + (h + 1) * 128],
                                          ebig[:, 7 * 512:8 * 512],
                                          start=False, stop=True),
                                      waits=[(Sc, ex_seq[15])])
                    md_last = prog.op(T, lambda e, u=un: e.matmul(
                                ppD[u][:, :], ones_bb[:, :],
                                ebig[:, 7 * 512:8 * 512], start=False, stop=True))
                    md_seq[15] = md_last
                    e_free = md_last
                    rcb = prog.op(V, lambda e, u=un: e.reciprocal(rb[:, :], ppD[u][:, :]),
                                  waits=[(T, md_last), (V, cv_prev), (V, rope_last)])
                    bankF[dkey] = [(V, rcb)]
                    cv = prog.op(V, lambda e, h=hh, q=qg, gg=g, u=un: e.tensor_mul(
                                     ctxf[:, (2 * gg + h) * 1024 + q * 512:
                                          (2 * gg + h) * 1024 + q * 512 + 512],
                                     ppC[u][:, :], rb[:, :]),
                                 waits=[(V, rcb), (T, mp_last)])
                    cv_prev = cv
                    ppC_free[un] = [(V, cv)]
                    k2_rd = [(T, mp_last), (Sc, ex_seq[15])]
                    v2_rd = [(T, mp_last)]
            pp_free[0] = pp3_free[0]
            pp_free[1] = pp3_free[1]
            pp_free[2] = ppC_free[0]
            pp_free[3] = ppC_free[1]
            ppR_free = pp3_free[2]
            k2_rd = k2_rd[-2:]
            v2_rd = v2_rd[-1:]
            if P2_SUB == "all":
                tcos_free = cv_prev

            if debug and g == 0:
                stv = prog.dma("sync", lambda e: e.dma_start(out=dram["d_q"][:, :], in_=q2[:, :]),
                               sem="st", waits=[(V, rope_last)])
                stv = prog.dma("sync", lambda e: e.dma_start(out=dram["d_k"][:, :], in_=k2[:, :]),
                               sem="st", waits=[(V, rope_last)])
                stv = prog.dma("sync", lambda e: e.dma_start(out=dram["d_v"][:, :], in_=v2[:, :]),
                               sem="st", waits=[(Sc, vproj_last)])
                if P2_SUB != "qkv":
                    stv = prog.dma("sync", lambda e: e.dma_start(out=dram["d_bias"][:, :], in_=biasc[:, :]),
                                   sem="st", waits=[(V, bias_ready[1])])
                sn = prog.op("sync", lambda e: e.nop(), waits=[("st", stv)])
                k2_rd.append(("sync", sn))
                v2_rd.append(("sync", sn))
                bankF["S"].append(("sync", sn))

        if debug and P2_SUB == "all":
            stv = 0
            for blk in range(32):
                stv = prog.dma("sync", lambda e, bb=blk: e.dma_start(
                                   out=dram["d_ctx"][:, bb * 512:(bb + 1) * 512],
                                   in_=ctxf[:, bb * 512:(bb + 1) * 512]),
                               sem="st", waits=ppC_free[0] + ppC_free[1])
            prog.op("sync", lambda e: e.nop(), waits=[("st", stv)])
        env["p2T"] = prog.v("tensor")
        env["p2V"] = prog.v("vector")
        env["p2Sc"] = prog.v("scalar")
        env["p2G"] = prog.v("gpsimd")


def _emit_outproj(env):
    nc, prog, dram = env["nc"], env["prog"], env["dram"]
    ctxf, acc = env["ctx"], env["acc"]
    debug = env["debug"]
    V, Sc, T = "vector", "scalar", "tensor"
    wop = dram["wop"]

    with ExitStack() as ctx:
        Ls = lambda *a: ctx.enter_context(nc.sbuf_tensor(*a, side="left"))
        wof = [Ls(f"wof{i}", [128, 4096], F32) for i in range(2)]
        wob = [Ls(f"wob{i}", [128, 8192], BF16) for i in range(2)]
        po = [ctx.enter_context(nc.psum_tensor(f"po{i}", [128, 512], F32))
              for i in range(2)]
        wof_free = [0, 0]
        wob_mm_last = [0, 0]
        po_free = [[(Sc, env["p2Sc"])], [(Sc, env["p2Sc"])]]
        ev_last = 0
        slab_casts = {}

        def load_wo(fg):
            wp = fg % 2
            for half in range(2):
                wx = [(V, wof_free[wp])]
                if fg < 2 and half == 0:
                    wx += [(T, env["p2T"]), (V, env["p2V"]),
                           (Sc, env["p2Sc"]), ("gpsimd", env["p2G"])]
                ld = prog.dma("sync",
                              lambda e, f=fg, h=half, q=wp: e.dma_start(
                                  out=wof[q][:, :], in_=wop[f, :, h * 4096:(h + 1) * 4096]),
                              sem=f"ldo{wp}", waits=wx)
                wts = [(f"ldo{wp}", ld)]
                if half == 0:
                    wts.append((T, wob_mm_last[wp]))
                cv = prog.op(V, lambda e, h=half, q=wp: e.tensor_copy(
                                 wob[q][:, h * 4096:(h + 1) * 4096], wof[q][:, :]),
                             waits=wts)
                wof_free[wp] = cv
            slab_casts[fg] = wof_free[wp]

        load_wo(0)
        load_wo(1)
        for fg in range(4):
            wp = fg % 2
            casts = slab_casts[fg]
            for tb in range(NTO):
                sl = tb % 2
                w0 = [(V, casts)] + po_free[sl]
                m_last = None
                for h in range(NH):
                    m_last = prog.op(T, lambda e, hh=h, t=tb, s=sl, q=wp: e.matmul(
                                         po[s][:, :],
                                         ctxf[:, hh * 1024 + t * 128: hh * 1024 + (t + 1) * 128],
                                         wob[q][:, hh * 512:(hh + 1) * 512],
                                         start=(hh == 0), stop=(hh == NH - 1)),
                                     waits=(w0 if h == 0 else ()))
                ev = prog.op(Sc, lambda e, t=tb, f=fg, s=sl: e.activation(
                                 acc[:, t * D + f * 512: t * D + f * 512 + 512],
                                 po[s][:, :], AF.Copy),
                             waits=[(T, m_last)] +
                                   ([(V, env["p2V"]), ("gpsimd", env["p2G"])]
                                    if (fg == 0 and tb == 0) else []))
                po_free[sl] = [(Sc, ev)]
                ev_last = ev
            wob_mm_last[wp] = m_last
            if fg + 2 < 4:
                load_wo(fg + 2)
        if debug:
            stv = prog.dma("sync", lambda e: e.dma_start(out=dram["d_acc"][:, :], in_=acc[:, :]),
                           sem="st", waits=[(Sc, ev_last)])
            prog.op("sync", lambda e: e.nop(), waits=[("st", stv)])


def _emit_ffn(env, fxb=None):
    nc, prog, dram = env["nc"], env["prog"], env["dram"]
    xnfm, acc = env["xnfm"], env["acc"]
    do_final = fxb is not None
    rstd, nmr = env["rstd"], env["nmr"]
    gsc, usc = env["gsc"], env["usc"]
    ones_cb, ones_rf, n128 = env["ones_cb"], env["ones_rf"], env["n128"]
    ones_bb = env["ones_bb"]
    debug = env["debug"]
    V, Sc, T, G = "vector", "scalar", "tensor", "gpsimd"
    gbp, ubp, dwp, dscp = dram["gbp"], dram["ubp"], dram["dwp"], dram["dscp"]

    with ExitStack() as ctx:
        Ls = lambda *a: ctx.enter_context(nc.sbuf_tensor(*a, side="left"))
        sgp = Ls("sgp", [128, 512], F32)
        sup = Ls("sup", [128, 512], F32)
        dscb = Ls("dscb", [128, 2048], BF16)
        hsr = Ls("hsr", [1, 512], BF16)
        if do_final:
            xst = Ls("xst0", [128, 512], F32)
            sst = ctx.enter_context(nc.psum_tensor("pfs", [128, 512], F32))
            r1p = Ls("r1p", [128, 16], F32)
        hfm = Ls("hfm", [128, NI * 512], BF16)
        pg = [ctx.enter_context(nc.psum_tensor(f"pg{i}", [128, 512], F32))
              for i in range(2)]
        pu = [ctx.enter_context(nc.psum_tensor(f"pu{i}", [128, 512], F32))
              for i in range(2)]
        phs = ctx.enter_context(nc.psum_tensor("phs", [128, 512], F32))
        ppB = ctx.enter_context(nc.psum_tensor("ppB", [128, 512], F32))
        pd = [pg[0], pg[1], pu[0], pu[1]]

        mprev = 0
        cprev = 0
        with ExitStack() as c0:
            dsrow = c0.enter_context(nc.sbuf_tensor("dsrow", [1, 512], F32, side="left"))
            for piece in range(4):
                ld = prog.dma("sync", lambda e, p=piece: e.dma_start(
                                  out=dsrow[:, :], in_=dscp[p:p + 1, :]),
                              sem="ldd", waits=[(T, mprev)])
                mprev = prog.op(T, lambda e: e.matmul(ppB[:, :], ones_rf[:, :],
                                                      dsrow[0:1, :], start=True, stop=True),
                                waits=[("ldd", ld), (V, cprev)])
                cprev = prog.op(V, lambda e, p=piece: e.tensor_copy(
                                    dscb[:, p * 512:(p + 1) * 512], ppB[:, :]),
                                waits=[(T, mprev)])
        dscb_ready = cprev
        ax_prev = 0
        ad_prev = 0
        ga_tab = [0] * 4
        if do_final:
            r1p_seq = prog.op(V, lambda e: e.tensor_scalar(r1p[:, :], rstd[:, :],
                                                           1.0, None, ALU.add))
            outp = dram["out"]
        wstage = Ls("wstage", [128, 16384], BF16)
        gb2 = [wstage[:, p * 4096:(p + 1) * 4096] for p in range(2)]
        ub2 = [wstage[:, 8192 + p * 4096: 8192 + (p + 1) * 4096] for p in range(2)]
        dw2 = [wstage[:, q * 8192:(q + 1) * 8192] for q in range(2)]

        sg_free = 0
        h_free = 0
        acc_g = 0
        he_prev = 0
        hs_prev = 0
        gb_mm_last = [0, 0]
        dw_mm_last = [mprev, mprev]
        pg_free = [[], []]
        pu_free = [[], []]
        pd_free = [[], [], [], []]
        phs_free = []
        tmp_free = 0
        for tg in range(2):
            pending_hs = []
            gb_ready = [0, 0]
            ub_ready = [0, 0]
            gh = 0
            for isc in range(32):
                par = isc % 2
                gb_ready[par] = prog.dma("sync",
                                         lambda e, i=isc, p=par: e.dma_start(
                                             out=gb2[p], in_=gbp[i, :, :]),
                                         sem=f"ldg{par}",
                                         waits=[(T, gb_mm_last[par]),
                                                (T, dw_mm_last[0])])
                ub_ready[par] = prog.dma("sync",
                                         lambda e, i=isc, p=par: e.dma_start(
                                             out=ub2[p], in_=ubp[i, :, :]),
                                         sem=f"ldu{par}",
                                         waits=[(T, gb_mm_last[par]),
                                                (T, dw_mm_last[1])])
                for ic2 in range(2):
                    ic = isc * 2 + ic2
                    psl = ic % 2
                    m = None
                    for hc in range(ND):
                        wts = []
                        if hc == 0:
                            wts = [(f"ldg{par}", gb_ready[par])] + pg_free[psl]
                        m = prog.op(T, lambda e, h=hc, i2=ic2, p=par, t=tg, s=psl: e.matmul(
                                        pg[s][:, :],
                                        gb2[p][:, h * 256 + i2 * 128: h * 256 + i2 * 128 + 128],
                                        xnfm[:, h * 1024 + t * 512: h * 1024 + t * 512 + 512],
                                        start=(h == 0), stop=(h == ND - 1)),
                                    waits=wts)
                    mg = m
                    if len(pending_hs) > 1:
                        pending_hs.pop(0)()
                    a1 = prog.op(Sc, lambda e, i=ic, s=psl: e.activation(
                                     sgp[:, :], pg[s][:, :], AF.Silu,
                                     scale=gsc[:, i:i + 1]),
                                 waits=[(T, mg), (G, sg_free)])
                    m = None
                    for hc in range(ND):
                        wts = []
                        if hc == 0:
                            wts = [(f"ldu{par}", ub_ready[par])] + pu_free[psl]
                        m = prog.op(T, lambda e, h=hc, i2=ic2, p=par, t=tg, s=psl: e.matmul(
                                        pu[s][:, :],
                                        ub2[p][:, h * 256 + i2 * 128: h * 256 + i2 * 128 + 128],
                                        xnfm[:, h * 1024 + t * 512: h * 1024 + t * 512 + 512],
                                        start=(h == 0), stop=(h == ND - 1)),
                                    waits=wts)
                    mu_ = m
                    gb_mm_last[par] = mu_
                    a2 = prog.op(Sc, lambda e, i=ic, s=psl: e.activation(
                                     sup[:, :], pu[s][:, :], AF.Identity,
                                     scale=usc[:, i:i + 1]),
                                 waits=[(T, mu_), (G, sg_free)])
                    gh = prog.op(G, lambda e, i=ic: e.tensor_mul(
                                     hfm[:, i * 512:(i + 1) * 512], sgp[:, :], sup[:, :]),
                                 waits=[(Sc, a2), (Sc, a1), (T, h_free)])
                    sg_free = gh
                    pg_free[psl] = [(Sc, a1)]
                    pu_free[psl] = [(Sc, a2)]

                    def hsum(i=ic, gh=gh):
                        nonlocal hs_prev
                        hs_prev = prog.op(T, lambda e, i2=i: e.matmul(
                                              phs[:, :], ones_bb[:, :],
                                              hfm[:, i2 * 512:(i2 + 1) * 512],
                                              start=(i2 == 0), stop=(i2 == NI - 1)),
                                          waits=([(V, he_prev)] + phs_free
                                                 if i == 0 else []) + [(G, gh)])
                    pending_hs.append(hsum)
            for f in pending_hs:
                f()
            pending_hs.clear()
            phs_free = []
            he = prog.op(V, lambda e: e.tensor_copy(hsr[:, :], phs[0:1, :]),
                         waits=[(T, hs_prev)])
            he_prev = he
            vbar = prog.v(V)
            scbar = prog.v(Sc)

            for fg in range(4):
                stops = [None] * 4
                for pc in range(4):
                    par = pc % 2
                    lw = prog.dma("sync", lambda e, f=fg, p=pc, q=par: e.dma_start(
                                      out=dw2[q], in_=dwp[f * 4 + p, :, :]),
                                  sem=f"lddw{par}",
                                  waits=[(T, dw_mm_last[par]),
                                         (T, gb_mm_last[0]), (T, gb_mm_last[1])])
                    for tb in range(4):
                        for icp in range(16):
                            wts = []
                            if icp == 0 and tb == 0:
                                wts = [(f"lddw{par}", lw), (G, gh), (V, he)]
                            if icp == 0 and pc == 0:
                                wts = wts + pd_free[tb] + \
                                      ([(V, vbar), (Sc, scbar)] if fg == 0 else [])
                            m = prog.op(T, lambda e, f=fg, p=pc, q=par, t=tb, i=icp: e.matmul(
                                            pd[t][:, :],
                                            hfm[:, (p * 16 + i) * 512 + t * 128:
                                                (p * 16 + i) * 512 + t * 128 + 128],
                                            dw2[q][:, i * 512:(i + 1) * 512],
                                            start=(p == 0 and i == 0), stop=False),
                                        waits=wts)
                        if pc == 3:
                            stops[tb] = prog.op(T, lambda e, t=tb: e.matmul(
                                                    pd[t][:, :],
                                                    hsr[0:1, t * 128:(t + 1) * 128],
                                                    n128[0:1, :],
                                                    start=False, stop=True))
                    dw_mm_last[par] = prog.v(T)
                for tb in range(4):
                    tv = prog.op(V, lambda e, t=tb, f=fg: e.tensor_mul(
                                     sgp[:, :], pd[t][:, :], dscb[:, f * 512:(f + 1) * 512]),
                                 waits=[(T, stops[tb]), (G, tmp_free), (G, gh),
                                        (V, dscb_ready)])
                    ga = prog.op(G, lambda e, t=tb, f=fg, tt=tg: e.tensor_add(
                                     acc[:, (tt * 4 + t) * D + f * 512:
                                         (tt * 4 + t) * D + f * 512 + 512],
                                     acc[:, (tt * 4 + t) * D + f * 512:
                                         (tt * 4 + t) * D + f * 512 + 512],
                                     sgp[:, :]),
                                 waits=[(V, tv)])
                    tmp_free = ga
                    acc_g = ga
                    ga_tab[tb] = ga
                    pd_free[tb] = [(V, tv)]
                if do_final:
                    for t4 in range(4):
                        tbg = tg * 4 + t4
                        ld = prog.dma("scalar", lambda e, tt=tbg, f=fg: e.dma_start(
                                          out=xst[:, :],
                                          in_=fxb[tt * 128:(tt + 1) * 128,
                                                  f * 512:(f + 1) * 512]),
                                      sem="ldp0", waits=[(V, ax_prev)])
                        ax = prog.op(V, lambda e, tt=tbg: e.tensor_scalar(
                                         sst[:, :], xst[:, :], r1p[:, tt:tt + 1],
                                         nmr[:, tt:tt + 1], ALU.mult, ALU.add),
                                     waits=[("ldp0", ld), (V, ad_prev),
                                            (V, r1p_seq)])
                        ad = prog.op(V, lambda e, tt=tbg, f=fg: e.tensor_add(
                                         acc[:, tt * D + f * 512: tt * D + f * 512 + 512],
                                         acc[:, tt * D + f * 512: tt * D + f * 512 + 512],
                                         sst[:, :]),
                                     waits=[(V, ax), (G, ga_tab[t4])])
                        prog.dma("scalar", lambda e, tt=tbg, f=fg: e.dma_start(
                                     out=outp[tt * 128:(tt + 1) * 128,
                                              f * 512:(f + 1) * 512],
                                     in_=acc[:, tt * D + f * 512:
                                             tt * D + f * 512 + 512]),
                                 sem="stp", waits=[(V, ad)])
                        ax_prev = ax
                        ad_prev = ad
            sg_free = tmp_free
            h_free = prog.v(T)
            phs_free = [(V, he)]
            pg_free[0] = pd_free[0]
            pg_free[1] = pd_free[1]
            pu_free[0] = pd_free[2]
            pu_free[1] = pd_free[3]
        if do_final:
            prog.op("sync", lambda e: e.nop(), waits=[("stp", prog.dcnt["stp"])])
        if debug and not do_final:
            stv = prog.dma("sync", lambda e: e.dma_start(out=dram["d_acc2"][:, :], in_=acc[:, :]),
                           sem="st", waits=[(G, acc_g)])
            prog.op("sync", lambda e: e.nop(), waits=[("st", stv)])


def _emit_final(env, xb):
    nc, prog, dram = env["nc"], env["prog"], env["dram"]
    acc = env["acc"]
    rstd, nmr = env["rstd"], env["nmr"]
    V, Sc, T, G = "vector", "scalar", "tensor", "gpsimd"
    out = dram["out"]

    with ExitStack() as ctx:
        Ls = lambda *a: ctx.enter_context(nc.sbuf_tensor(*a, side="left"))
        xt = [Ls(f"fxt{i}", [128, D], F32) for i in range(2)]
        xnt = [Ls(f"fxn{i}", [128, D], F32) for i in range(2)]
        fin = [Ls(f"fin{i}", [128, D], F32) for i in range(2)]
        xt_free = [0, 0]
        xn_free = [0, 0]
        fin_free = [0, 0]
        for tb in range(NTO):
            sl = tb % 2
            ld = prog.dma("sync", lambda e, t=tb, s=sl: e.dma_start(
                              out=xt[s][:, :], in_=xb[t * 128:(t + 1) * 128, :]),
                          sem=f"ldf{sl}", waits=[(V, xt_free[sl])])
            ax = prog.op(Sc, lambda e, t=tb, s=sl: e.activation(
                             xnt[s][:, :], xt[s][:, :], AF.Identity,
                             bias=nmr[:, t:t + 1], scale=rstd[:, t:t + 1]),
                         waits=[(f"ldf{sl}", ld), (V, xn_free[sl])])
            d1 = prog.op(V, lambda e, s=sl: e.tensor_add(fin[s][:, :], xt[s][:, :],
                                                         xnt[s][:, :]),
                         waits=[(Sc, ax)] + ([(f"stf{sl}", fin_free[sl])]
                                            if fin_free[sl] else []))
            d2 = prog.op(V, lambda e, t=tb, s=sl: e.tensor_add(
                             fin[s][:, :], fin[s][:, :], acc[:, t * D:(t + 1) * D]),
                         waits=[(V, d1)])
            fin_free[sl] = prog.dma("sync", lambda e, t=tb, s=sl: e.dma_start(
                                        out=out[t * 128:(t + 1) * 128, :], in_=fin[s][:, :]),
                                    sem=f"stf{sl}", waits=[(V, d2)])
            xt_free[sl] = d1
            xn_free[sl] = d1
        prog.op("sync", lambda e: e.nop(),
                waits=[("stf0", prog.dcnt.get("stf0", 0)), ("stf1", prog.dcnt.get("stf1", 0))])



def host_prep(inputs):
    X = np.asarray(inputs["L0_attn_X"], dtype=np.float32)
    ln_w = np.asarray(inputs["L0_attn_ln_w"], dtype=np.float32)
    ln_b = np.asarray(inputs["L0_attn_ln_b"], dtype=np.float32)
    W_in = np.ascontiguousarray(np.asarray(inputs["L0_attn_W_in"], dtype=np.float32))
    W_out = np.asarray(inputs["L0_attn_W_out"], dtype=np.float32)
    gw = np.asarray(inputs["L0_ffn_gate_w"])
    uw = np.asarray(inputs["L0_ffn_up_w"])
    dw = np.asarray(inputs["L0_ffn_down_w"])
    gs = np.asarray(inputs["L0_ffn_gate_scale"], dtype=np.float32)
    us = np.asarray(inputs["L0_ffn_up_scale"], dtype=np.float32)
    ds = np.asarray(inputs["L0_ffn_down_scale"], dtype=np.float32)

    wip = np.empty((24, 128, 4096), np.float32)
    for g in range(8):
        for w, base in enumerate((0, 2048, 4096)):
            blk = W_in[:, base + 2 * g * 128: base + 2 * g * 128 + 256]
            wip[g * 3 + w] = blk.reshape(16, 128, 256).transpose(1, 0, 2).reshape(128, 4096)
    wop = np.empty((4, 128, 8192), np.float32)
    for fg in range(4):
        blk = W_out[:, fg * 512:(fg + 1) * 512]
        wop[fg] = blk.reshape(16, 128, 512).transpose(1, 0, 2).reshape(128, 8192)

    def pack_gu(wq):
        a = wq.astype(np.float32).astype(BF)
        a = a.reshape(32, 256, 16, 128)
        return np.ascontiguousarray(a.transpose(0, 3, 2, 1)).reshape(32, 128, 4096)

    gbp = pack_gu(gw)
    ubp = pack_gu(uw)
    a = dw.astype(np.float32).astype(BF)
    a = a.reshape(4, 512, 4, 16, 128)
    dwp = np.ascontiguousarray(a.transpose(0, 2, 4, 3, 1)).reshape(16, 128, 8192)

    shared = dict(
        wip=wip, wop=wop, gbp=gbp, ubp=ubp, dwp=dwp,
        lnwc=np.ascontiguousarray(ln_w.reshape(ND, 128).T),
        lnbc=np.ascontiguousarray(ln_b.reshape(ND, 128).T),
        gscp=np.ascontiguousarray(gs.reshape(NI, 128).T),
        uscp=np.ascontiguousarray(us.reshape(NI, 128).T),
        dscp=np.ascontiguousarray(ds.reshape(4, 512)),
        idf=np.eye(128, dtype=np.float32),
    )
    pm = np.zeros((64, 64), np.float32)
    for t in range(32):
        pm[2 * t + 1, 2 * t] = -1.0
        pm[2 * t, 2 * t + 1] = 1.0
    shared["pmat"] = pm.astype(BF)

    inv_freq = (1.0 / (10000.0 ** (np.arange(0, 64, 2, dtype=np.float32)
                                   / np.float32(64.0)))).astype(np.float32)
    in_maps = []
    for c in range(8):
        b, qh = c // 2, c % 2
        off = qh * 1024
        pos = (np.arange(S) + off) % S
        ang = pos[:, None].astype(np.float32) * inv_freq[None, :]
        ang2 = np.repeat(ang, 2, axis=1)
        m = dict(shared)
        m["cs"] = np.ascontiguousarray(np.concatenate(
            [np.cos(ang2).T, np.sin(ang2).T], axis=0).astype(np.float32))
        m["xb"] = np.ascontiguousarray(np.concatenate([X[b, off:], X[b, :off]], axis=0))
        in_maps.append(m)
    return in_maps


def kernel(**inputs):
    in_maps = host_prep(inputs)
    nc = build_program(stage=STAGE, debug=DEBUG)
    res = run_bass_kernel_spmd(nc, in_maps, core_ids=list(range(8)))
    Y = np.empty((B, S, D), np.float32)
    for c in range(8):
        b, qh = c // 2, c % 2
        off = qh * 1024
        Y[b, off:off + 1024, :] = res.results[c]["out"]
    return Y
